# version 12
# baseline (speedup 1.0000x reference)
import numpy as np
import concourse.bass as bass
import concourse.mybir as mybir
from concourse.bass_utils import run_bass_kernel_spmd
from contextlib import ExitStack

F32 = mybir.dt.float32
BF16 = mybir.dt.bfloat16
I32 = mybir.dt.int32
AF = mybir.ActivationFunctionType
ALU = mybir.AluOpType

D = 2048
T = 2048
CH = 16
NH = 16
HD = 128
TAU = 512
NTAU = T // TAU
LAT = 512
LATC = LAT // 128
L0H = 5632
L0HC = L0H // 128
L1H = 8192
L1HC = L1H // 128
SM_SCALE = float(1.0 / np.sqrt(128.0))
SM_SHIFT = -12.0
LN_EPS = 1e-5
Q_EPS = 1e-8
INV_D = 1.0 / D

ENGS = ("sync", "tensor", "scalar", "vector", "gpsimd")
ESEM = {"tensor": "pe", "scalar": "act", "vector": "dve", "gpsimd": "pool"}
QROT = {"sync": ["qs%d" % i for i in range(4)],
        "gpsimd": ["qp%d" % i for i in range(6)],
        "scalar": ["qa%d" % i for i in range(2)]}
ALL_SEMS = ("pe", "act", "dve", "pool") + tuple(s for v in QROT.values() for s in v)


class G:

    def __init__(self, nc):
        self.nc = nc
        self.ops = []
        self.cnt = {s: 0 for s in ALL_SEMS}
        self.qrr = {}
        self.snapshots = {}
        self.barv = 0
        self.es = ExitStack()
        self.uid = 0

    def _nm(self, name):
        self.uid += 1
        return "%s_u%d" % (name, self.uid)

    def sb(self, name, shape, dtype, es=None):
        return (es or self.es).enter_context(self.nc.sbuf_tensor(self._nm(name), shape, dtype))

    def ps(self, name, shape, es=None, dtype=F32):
        return (es or self.es).enter_context(self.nc.psum_tensor(self._nm(name), shape, dtype))

    def op(self, eng, fn, waits=(), inc=False):
        w = [x for x in waits if x is not None]
        ev = None
        upd = None
        if inc:
            sem = ESEM[eng]
            self.cnt[sem] += 1
            ev = (sem, self.cnt[sem])
            upd = (sem, 1)
        self.ops.append((eng, fn, w, upd, self.barv))
        return ev

    def dma(self, eng, out, in_, waits=()):
        rot = QROT[eng]
        q = rot[self.qrr.setdefault(eng, 0) % len(rot)]
        self.qrr[eng] += 1
        self.cnt[q] += 16
        ev = (q, self.cnt[q])
        def fn(e, o=out, i=in_):
            return e.dma_start(out=o, in_=i)
        self.ops.append((eng, fn, [x for x in waits if x is not None], (q, 16), self.barv))
        return ev

    def load(self, out, in_, waits=()):
        return self.dma("sync", out, in_, waits)

    def loadc(self, out, in_, waits=()):
        return self.dma("gpsimd", out, in_, waits)

    def store(self, out, in_, waits=()):
        return self.dma("scalar", out, in_, waits)

    def barrier(self):
        self.barv += 1
        self.snapshots[self.barv] = dict(self.cnt)

    def build(self, sems):
        ops = self.ops
        snapshots = self.snapshots

        def mk(ename):
            def f(eng):
                waited = {}
                seen_bar = [0]
                own = ESEM.get(ename)

                def do_wait(s, v):
                    if v <= 0 or waited.get(s, 0) >= v:
                        return
                    eng.wait_ge(sems[s], v)
                    waited[s] = v

                for (e, fn, waits, upd, barv) in ops:
                    if e != ename:
                        continue
                    if barv > seen_bar[0]:
                        for s, v in snapshots[barv].items():
                            if s != own:
                                do_wait(s, v)
                        seen_bar[0] = barv
                    for (s, v) in waits:
                        do_wait(s, v)
                    ins = fn(eng)
                    if upd is not None:
                        ins.then_inc(sems[upd[0]], upd[1])
            return f

        with self.nc.Block() as block:
            for ename in ENGS:
                getattr(block, ename)(mk(ename))


def mm(g, out_ap, lhsT_ap, rhs_ap, start, stop, waits=(), inc=False):
    return g.op(
        "tensor",
        lambda e, o=out_ap, l=lhsT_ap, r=rhs_ap, s1=start, s2=stop:
            e.matmul(o, l, r, start=s1, stop=s2),
        waits, inc)


class Ctx:
    pass


def setup_common(g, masks_dram=None):
    c = Ctx()
    c.x = g.sb("x_sb", [128, CH * T], BF16)
    c.ao = g.sb("ao_sb", [128, CH * T], BF16)
    c.allones = g.sb("allones", [128, 128], BF16)
    c.ones1 = g.sb("ones1", [1, 128], BF16)
    c.mrow = g.sb("mrow", [1, 128], BF16)
    c.cpool = g.sb("cpool", [128, 8], F32)
    g.op("vector", lambda e: e.memset(c.allones[:], 1.0), inc=True)
    g.op("vector", lambda e: e.memset(c.ones1[:], 1.0), inc=True)
    g.op("vector", lambda e: e.memset(c.mrow[:], -128.0), inc=True)
    g.op("vector", lambda e: e.memset(c.cpool[:, 0:1], SM_SHIFT), inc=True)
    g.op("vector", lambda e: e.memset(c.cpool[:, 1:2], SM_SCALE), inc=True)
    g.op("vector", lambda e: e.memset(c.cpool[:, 2:3], LN_EPS), inc=True)
    c.ev_init = g.op("vector", lambda e: e.memset(c.cpool[:, 3:4], Q_EPS), inc=True)
    c.ev_masks = None
    if masks_dram is not None:
        c.masks = g.sb("masks", [128, 3 * 128], BF16)
        for i in range(3):
            c.ev_masks = g.loadc(c.masks[:, i * 128:(i + 1) * 128], masks_dram[i])
    return c


def xs(c, ch, t0, n):
    return c.x[:, ch * T + t0: ch * T + t0 + n]


def aos(c, ch, t0, n):
    return c.ao[:, ch * T + t0: ch * T + t0 + n]


def emit_load_stream(g, c, xin, fp32_src):
    ev = None
    if fp32_src:
        for ch in range(CH):
            ev = g.loadc(xs(c, ch, 0, T), xin[ch * 128:(ch + 1) * 128, :])
    else:
        for i in range(4):
            dst = c.x[:, i * 4 * T:(i + 1) * 4 * T].rearrange("p (c t) -> p c t", t=T)
            s = xin.rearrange("(c p) t -> c p t", p=128)[i * 4:(i + 1) * 4]
            ev = g.load(dst, s.rearrange("c p t -> p c t"))
    c.ev_stream = ev
    return ev


def emit_store_stream(g, c, xout, waits):
    ev = None
    for i in range(4):
        s = c.x[:, i * 4 * T:(i + 1) * 4 * T].rearrange("p (c t) -> p c t", t=T)
        d = xout.rearrange("(c p) t -> c p t", p=128)[i * 4:(i + 1) * 4]
        ev = g.load(d.rearrange("c p t -> p c t"), s, waits=waits)
    return ev


def emit_ln(g, c, ev_r_ready, final_out=None):
    es = ExitStack()
    psum_s = g.ps("ln_sum", [128, TAU], es)
    psum_q = g.ps("ln_sq", [128, TAU], es)
    r2 = [g.sb("ln_r2", [128, TAU], BF16, es) for _ in range(2)]
    m_bc = g.sb("ln_m", [128, TAU], F32, es)
    tmp = g.sb("ln_t", [128, TAU], F32, es)
    rstd = g.sb("ln_rs", [128, TAU], F32, es)
    t1 = g.sb("ln_t1", [128, TAU], F32, es)
    if final_out is not None:
        outf = [g.sb("ln_of", [128, TAU], F32, es) for _ in range(2)]
    ev_all = []
    prev_stats_consumed = []
    prev_dma = {}
    for tau in range(NTAU):
        t0 = tau * TAU
        ev_sq = [None] * CH
        ev_mmq = [None] * CH
        last_sum = None
        for chn in range(CH):
            ev_sq[chn] = g.op(
                "scalar",
                lambda e, ch=chn, t=t0: e.activation(r2[ch % 2][:], xs(c, ch, t, TAU), AF.Square),
                waits=list(ev_r_ready) + ([ev_mmq[chn - 2]] if chn >= 2 else []),
                inc=True)
            last_sum = mm(g, psum_s[:], c.allones[:], xs(c, chn, t0, TAU),
                          chn == 0, chn == CH - 1,
                          waits=(list(ev_r_ready) + prev_stats_consumed) if chn == 0 else (),
                          inc=(chn == CH - 1))
            ev_mmq[chn] = mm(g, psum_q[:], c.allones[:], r2[chn % 2][:],
                             chn == 0, chn == CH - 1,
                             waits=[ev_sq[chn]] + (prev_stats_consumed if chn == 0 else []),
                             inc=True)
        last_sq = ev_mmq[CH - 1]
        e1 = g.op("vector",
                  lambda e: e.tensor_scalar(m_bc[:], psum_s[:], INV_D, None, ALU.mult),
                  waits=[last_sum], inc=True)
        e2 = g.op("vector", lambda e: e.tensor_mul(tmp[:], m_bc[:], m_bc[:]),
                  waits=[e1], inc=True)
        e3 = g.op("vector",
                  lambda e: e.scalar_tensor_tensor(tmp[:], psum_q[:], INV_D, tmp[:],
                                                   ALU.mult, ALU.subtract),
                  waits=[last_sq, e2], inc=True)
        e4 = g.op("scalar", lambda e: e.activation(tmp[:], tmp[:], AF.Sqrt,
                                                   bias=c.cpool[:, 2:3]),
                  waits=[e3], inc=True)
        e5 = g.op("vector", lambda e: e.reciprocal(rstd[:], tmp[:]), waits=[e4], inc=True)
        prev_stats_consumed = [e1, e3]
        for chn in range(CH):
            ea = g.op("vector",
                      lambda e, ch=chn, t=t0: e.tensor_sub(t1[:], xs(c, ch, t, TAU), m_bc[:]),
                      waits=[e5, last_sq, last_sum, ev_sq[CH - 1]], inc=True)
            if final_out is None:
                eb = g.op("vector",
                          lambda e, ch=chn, t=t0: e.tensor_mul(xs(c, ch, t, TAU), t1[:], rstd[:]),
                          waits=[ea], inc=True)
                ev_all.append(eb)
            else:
                eb = g.op("vector",
                          lambda e, ch=chn: e.tensor_mul(outf[ch % 2][:], t1[:], rstd[:]),
                          waits=[ea, prev_dma.get(chn - 2)], inc=True)
                prev_dma[chn] = g.store(final_out[chn * 128:(chn + 1) * 128, t0:t0 + TAU],
                                        outf[chn % 2][:], waits=[eb])
                ev_all.append(prev_dma[chn])
        prev_dma = {chn - CH: v for chn, v in prev_dma.items()} if final_out is not None else {}
    es.close()
    return ev_all


def emit_outproj(g, c, w_dram, bias_col, final_out=None, prep=None):
    es = ExitStack()
    wom = [g.sb("wo_m", [128, CH * 128], BF16, es) for _ in range(2)]
    psY = [g.ps("ps_y", [128, T], es) for _ in range(2)]
    ev_w = {}
    ev_mm = {}
    ev_evac = {}
    for m in range(2):
        ev_w[m] = g.loadc(wom[m][:], w_dram[m])
    for m in range(CH):
        e_last = None
        for chn in range(CH):
            for tau in range(NTAU):
                e_last = mm(g, psY[m % 2][:, tau * TAU:(tau + 1) * TAU],
                            wom[m % 2][:, chn * 128:(chn + 1) * 128],
                            aos(c, chn, tau * TAU, TAU),
                            chn == 0, chn == CH - 1,
                            waits=([ev_w[m], ev_evac.get(m - 2)]
                                   if (chn == 0 and tau == 0) else ()),
                            inc=(chn == CH - 1 and tau == NTAU - 1))
        ev_mm[m] = e_last
        if m + 2 < CH:
            ev_w[m + 2] = g.loadc(wom[m % 2][:], w_dram[m + 2], waits=[ev_mm[m]])
        eev = None
        for tau in range(NTAU):
            if bias_col is None:
                eev = g.op("vector",
                           lambda e, mm_=m, tt=tau: e.tensor_add(
                               xs(c, mm_, tt * TAU, TAU),
                               psY[mm_ % 2][:, tt * TAU:(tt + 1) * TAU],
                               xs(c, mm_, tt * TAU, TAU)),
                           waits=[ev_mm[m]], inc=True)
            else:
                eev = g.op("vector",
                           lambda e, mm_=m, tt=tau: e.scalar_tensor_tensor(
                               xs(c, mm_, tt * TAU, TAU),
                               psY[mm_ % 2][:, tt * TAU:(tt + 1) * TAU],
                               bias_col[:, mm_:mm_ + 1], xs(c, mm_, tt * TAU, TAU),
                               ALU.add, ALU.add),
                           waits=[ev_mm[m]], inc=True)
        ev_evac[m] = eev
        if prep is not None:
            prep.drive(6)
    es.close()
    return emit_ln(g, c, [ev_evac[m] for m in range(CH)], final_out)


def emit_l0_attn(g, c, W, dbg_kv=None, prep=None):
    es = ExitStack()
    kv = g.sb("kv_sb", [128, LATC * T], BF16, es)
    kv_es = ExitStack()
    wkvb = g.sb("wkvb", [128, LATC * CH * 128], BF16, kv_es)
    pskv = [g.ps("ps_kv", [128, TAU], kv_es) for _ in range(2)]
    ev_wkv = g.loadc(wkvb[:], W["wkv"], waits=[c.ev_init])
    ev_kv_ev = {}
    ev_kv_mm_last = None
    step = 0
    for lc in range(LATC):
        for tau in range(NTAU):
            b = step % 2
            last = None
            for chn in range(CH):
                last = mm(g, pskv[b][:],
                          wkvb[:, (lc * CH + chn) * 128:(lc * CH + chn + 1) * 128],
                          xs(c, chn, tau * TAU, TAU), chn == 0, chn == CH - 1,
                          waits=([ev_wkv, c.ev_stream, ev_kv_ev.get(step - 2)]
                                 if chn == 0 else ()),
                          inc=(chn == CH - 1))
            ev_kv_mm_last = last
            ev_kv_ev[step] = g.op(
                "scalar",
                lambda e, l=lc, tt=tau, bb=b: e.activation(
                    kv[:, l * T + tt * TAU: l * T + (tt + 1) * TAU], pskv[bb][:], AF.Copy),
                waits=[last], inc=True)
            step += 1
    ev_kv_done = ev_kv_ev[step - 1]
    if dbg_kv is not None:
        for i in range(LATC):
            g.store(dbg_kv[i * 128:(i + 1) * 128, :], kv[:, i * T:(i + 1) * T],
                    waits=[ev_kv_done])
    kv_es.close()

    wqh = [g.sb("wq_h", [128, CH * 128], BF16, es) for _ in range(2)]
    wkh = [g.sb("wk_h", [128, LATC * 128], BF16, es) for _ in range(2)]
    wvh = [g.sb("wv_h", [128, LATC * 128], BF16, es) for _ in range(2)]
    q_h = g.sb("q_h", [128, T], BF16, es)
    k_h = g.sb("k_h", [128, T], BF16, es)
    v_h = g.sb("v_h", [128, T], BF16, es)
    pT = [g.sb("pT", [128, 384], BF16, es) for _ in range(2)]
    rcp = g.sb("rcp", [128, 128], F32, es)
    psp = [g.ps("ps_p", [128, TAU], es) for _ in range(2)]
    psS = [g.ps("ps_s", [128, 384], es) for _ in range(2)]
    psR_t = g.ps("ps_r", [128, 128], es)
    psA_t = g.ps("ps_a", [128, 128], es)
    psR = [psR_t[:], psR_t[:]]
    psA = [psA_t[:], psA_t[:]]

    def load_head_w(h, waits):
        b = h % 2
        e1 = g.loadc(wqh[b][:], W["wq"][h], waits=waits)
        e2 = g.loadc(wkh[b][:], W["wk"][h], waits=waits)
        e3 = g.loadc(wvh[b][:], W["wv"][h], waits=waits)
        return [e1, e2, e3]

    ev_wh = {0: load_head_w(0, [ev_kv_mm_last]), 1: load_head_w(1, [ev_kv_mm_last])}
    ev_pev = {}
    ev_projdone = {}
    ev_exp = {}
    ev_R = {}
    ev_AV = {}
    ev_rcp = {}
    ev_ao = {}
    gstep = 0
    for h in range(NH):
        b = h % 2
        first_evs = []
        for tau in range(NTAU):
            pb = gstep % 2
            last = None
            for chn in range(CH):
                w0 = []
                if chn == 0:
                    w0 = ev_wh[h] + [ev_pev.get(gstep - 2)]
                    if h == 0 and tau == 0:
                        w0.append(ev_kv_done)
                last = mm(g, psp[pb][:], wqh[b][:, chn * 128:(chn + 1) * 128],
                          xs(c, chn, tau * TAU, TAU), chn == 0, chn == CH - 1,
                          waits=w0, inc=(chn == CH - 1))
            ev_pev[gstep] = g.op(
                "scalar",
                lambda e, tt=tau, bb=pb: e.activation(q_h[:, tt * TAU:(tt + 1) * TAU],
                                                      psp[bb][:], AF.Copy),
                waits=[last], inc=True)
            first_evs.append(ev_pev[gstep])
            gstep += 1
        for tau in range(NTAU):
            pb = gstep % 2
            last = None
            for lc in range(LATC):
                last = mm(g, psp[pb][:], wkh[b][:, lc * 128:(lc + 1) * 128],
                          kv[:, lc * T + tau * TAU: lc * T + (tau + 1) * TAU],
                          lc == 0, lc == LATC - 1,
                          waits=([ev_pev.get(gstep - 2)] if lc == 0 else ()),
                          inc=(lc == LATC - 1))
            ev_pev[gstep] = g.op(
                "scalar",
                lambda e, tt=tau, bb=pb: e.activation(k_h[:, tt * TAU:(tt + 1) * TAU],
                                                      psp[bb][:], AF.Copy),
                waits=[last], inc=True)
            first_evs.append(ev_pev[gstep])
            gstep += 1
        for grp in range(4):
            pb = gstep % 2
            last = None
            for j in range(4):
                kc = grp * 4 + j
                for lc in range(LATC):
                    last = mm(g, psp[pb][:, j * 128:(j + 1) * 128],
                              kv[:, lc * T + kc * 128: lc * T + (kc + 1) * 128],
                              wvh[b][:, lc * 128:(lc + 1) * 128],
                              lc == 0, lc == LATC - 1,
                              waits=([ev_pev.get(gstep - 2)] if (j == 0 and lc == 0) else ()),
                              inc=(j == 3 and lc == LATC - 1))
            ev_pev[gstep] = g.op(
                "vector",
                lambda e, gg=grp, bb=pb: e.tensor_copy(v_h[:, gg * TAU:(gg + 1) * TAU],
                                                       psp[bb][:]),
                waits=[last], inc=True)
            first_evs.append(ev_pev[gstep])
            gstep += 1
        ev_projdone[h] = ev_pev[gstep - 1]
        if h + 2 < NH:
            ev_wh[h + 2] = load_head_w(h + 2, [ev_projdone[h]])
        if prep is not None:
            prep.drive(18)

        for q in range(CH):
            k_ = h * CH + q
            p2 = k_ % 2
            qsl = q_h[:, q * 128:(q + 1) * 128]
            tiles = [(psS[p2][:, 0:128], k_h[:, q * 128:(q + 1) * 128],
                      (128, 0, 128), c.allones[:], v_h[:, q * 128:(q + 1) * 128])]
            if q >= 1:
                tiles.append((psS[p2][:, 128:256], k_h[:, (q - 1) * 128:q * 128],
                              (128, 128, 256), c.allones[:], v_h[:, (q - 1) * 128:q * 128]))
            if q >= 2:
                tiles.append((psS[p2][0:16, 256:384], k_h[:, 0:16],
                              (16, 256, 384), c.allones[0:16, :], v_h[0:16, 0:128]))
            n_t = len(tiles)
            ev_S = None
            for i, (pslice, lh, _, _, _) in enumerate(tiles):
                w0 = []
                if i == 0:
                    w0 = ([ev_exp.get(k_ - 2)] + (first_evs if q == 0 else []))
                ev_S = mm(g, pslice, lh, qsl, True, True, waits=w0, inc=(i == n_t - 1))
            war_pT = [ev_AV.get(k_ - 2), ev_R.get(k_ - 2)]
            nse = 128 if q == 0 else 256
            e_exp = g.op(
                "scalar",
                lambda e, p=p2, n=nse: e.activation(pT[p][:, 0:n], psS[p][:, 0:n], AF.Exp,
                                                    bias=c.cpool[:, 0:1],
                                                    scale=c.cpool[:, 1:2]),
                waits=[ev_S] + war_pT, inc=True)
            if q >= 2:
                e_exp = g.op(
                    "scalar",
                    lambda e, p=p2: e.activation(pT[p][0:16, 256:384], psS[p][0:16, 256:384],
                                                 AF.Exp, bias=c.cpool[0:16, 0:1],
                                                 scale=c.cpool[0:16, 1:2]),
                    inc=True)
            ev_exp[k_] = e_exp
            e_m = g.op("vector",
                       lambda e, p=p2: e.tensor_mul(pT[p][:, 0:128], pT[p][:, 0:128],
                                                    c.masks[:, 0:128]),
                       waits=[ev_exp[k_], c.ev_masks], inc=True)
            if q >= 1:
                mi = 2 if q == 1 else 1
                e_m = g.op("vector",
                           lambda e, p=p2, m=mi: e.tensor_mul(
                               pT[p][:, 128:256], pT[p][:, 128:256],
                               c.masks[:, m * 128:(m + 1) * 128]),
                           inc=True)
            ev_mask = e_m
            ev_r = None
            for i, (_, _, (np_, c0, c1), ao_lh, _) in enumerate(tiles):
                pts = pT[p2][0:np_, c0:c1]
                ev_r = mm(g, psR[p2], ao_lh, pts, i == 0, i == n_t - 1,
                          waits=([ev_mask, ev_rcp.get(k_ - 1)] if i == 0 else ()),
                          inc=(i == n_t - 1))
            ev_R[k_] = ev_r
            ev_a = None
            for i, (_, _, (np_, c0, c1), _, v_lh) in enumerate(tiles):
                pts = pT[p2][0:np_, c0:c1]
                ev_a = mm(g, psA[p2], v_lh, pts, i == 0, i == n_t - 1,
                          waits=([ev_mask, ev_ao.get(k_ - 1)] if i == 0 else ()),
                          inc=(i == n_t - 1))
            ev_AV[k_] = ev_a
            ev_rcp[k_] = g.op("vector", lambda e, pr_=psR[p2]: e.reciprocal(rcp[:], pr_),
                              waits=[ev_R[k_]], inc=True)
            ev_ao[k_] = g.op(
                "vector",
                lambda e, pa_=psA[p2], hh=h, qq=q: e.tensor_mul(
                    c.ao[:, hh * T + qq * 128: hh * T + (qq + 1) * 128], pa_, rcp[:]),
                waits=[ev_AV[k_], ev_rcp[k_]], inc=True)
    es.close()




def hs_(c, i, n=TAU):
    return c.ao[:, i * TAU:(i) * TAU + n]


def gen_quantize_prep(g, c, specs, out_evs):
    SW = 1024
    stg = [g.sb("qp_stg", [128, SW], F32) for _ in range(2)]
    pb = g.sb("qp_p", [128, SW], BF16)
    nb = g.sb("qp_n", [128, SW], BF16)
    qb = [g.sb("qp_q", [128, SW], BF16) for _ in range(2)]
    onescol = g.sb("qp_onc", [128, 1], F32)
    onesrow = g.sb("qp_onr", [1, 128], F32)
    tot = g.sb("qp_tot", [1, 1], F32)
    state = {}
    for key, w_dram, scr_dram, n_rc, n_cols in specs:
        nslab = [(j, min(SW, n_cols - j)) for j in range(0, n_cols, SW)]
        slabs = [(rc, j0, w) for rc in range(n_rc) for (j0, w) in nslab]
        partials = g.sb("qp_part", [128, len(slabs)], F32)
        ginv = g.sb("qp_gi", [128, 1], F32)
        gninv = g.sb("qp_gni", [128, 1], F32)
        gg = g.sb("qp_gg", [128, 1], F32)
        ptot = g.sb("qp_pt", [128, 1], F32)
        state[key] = [slabs, partials, ginv, gninv, gg, ptot, nslab]
    e_on = g.op("vector", lambda e: e.memset(onescol[:], 1.0), inc=True)
    e_on = g.op("vector", lambda e: e.memset(onesrow[:], 1.0), inc=True)
    yield
    ev_ld = {}
    ev_red = {}
    u = 0
    for key, w_dram, scr_dram, n_rc, n_cols in specs:
        slabs, partials, ginv, gninv, gg, ptot, nslab = state[key]
        for si, (rc, j0, wd) in enumerate(slabs):
            b = u % 2
            ev_ld[u] = g.load(stg[b][:, 0:wd], w_dram[rc * 128:(rc + 1) * 128, j0:j0 + wd],
                              waits=[ev_red.get(u - 2)])
            ev_red[u] = g.op(
                "vector",
                lambda e, s=si, b_=b, w_=wd, pt_=partials: e.tensor_reduce(
                    pt_[:, s:s + 1], stg[b_][:, 0:w_], mybir.AxisListType.X, ALU.add,
                    apply_absolute_value=True),
                waits=[ev_ld[u]], inc=True)
            u += 1
            yield
    ev_red_last = ev_red[u - 1]
    for key, w_dram, scr_dram, n_rc, n_cols in specs:
        slabs, partials, ginv, gninv, gg, ptot, nslab = state[key]
        ges = ExitStack()
        psg1 = g.ps("qp_ps1", [1, 1], ges)
        psg2 = g.ps("qp_ps2", [128, 1], ges)
        e_pt = g.op("vector",
                    lambda e, n_=len(slabs), pt_=partials, o_=ptot: e.tensor_reduce(
                        o_[:], pt_[:, 0:n_], mybir.AxisListType.X, ALU.add),
                    waits=[ev_red_last], inc=True)
        e_m1 = mm(g, psg1[:], ptot[:], onescol[:], True, True, waits=[e_pt, e_on], inc=True)
        e_tt = g.op("vector", lambda e, t_=tot, p_=psg1: e.tensor_copy(t_[:], p_[:]),
                    waits=[e_m1], inc=True)
        e_m2 = mm(g, psg2[:], onesrow[:], tot[:], True, True, waits=[e_tt], inc=True)
        inv_n = 1.0 / (n_rc * 128 * n_cols)
        e_g1 = g.op("vector",
                    lambda e, g_=gg, p_=psg2, iv=inv_n: e.tensor_scalar(
                        g_[:], p_[:], iv, Q_EPS, ALU.mult, ALU.add),
                    waits=[e_m2], inc=True)
        e_g2 = g.op("vector", lambda e, gi_=ginv, g_=gg: e.reciprocal(gi_[:], g_[:]),
                    waits=[e_g1], inc=True)
        e_g3 = g.op("vector",
                    lambda e, gn_=gninv, gi_=ginv: e.tensor_scalar(
                        gn_[:], gi_[:], -1.0, None, ALU.mult),
                    waits=[e_g2], inc=True)
        state[key].append(e_g3)
        ges.close()
        yield
    ev_p = {}
    ev_n = {}
    ev_q = {}
    ev_st = {}
    u2 = 0
    for key, w_dram, scr_dram, n_rc, n_cols in specs:
        slabs, partials, ginv, gninv, gg, ptot, nslab, e_g3 = state[key]
        out_evs[key] = []
        for (j0, wd) in nslab:
            last_store = None
            for rc in range(n_rc):
                b = u2 % 2
                e_ld = g.load(stg[b][:, 0:wd],
                              w_dram[rc * 128:(rc + 1) * 128, j0:j0 + wd],
                              waits=[e_g3 if u2 < 2 else None, ev_p.get(u2 - 2),
                                     ev_n.get(u2 - 2)])
                ev_p[u2] = g.op(
                    "vector",
                    lambda e, w_=wd, b_=b, gi_=ginv: e.tensor_scalar(
                        pb[:, 0:w_], stg[b_][:, 0:w_], gi_[:, 0:1], 0.5,
                        ALU.mult, ALU.is_gt),
                    waits=[e_ld, e_g3, ev_q.get(u2 - 1)], inc=True)
                ev_n[u2] = g.op(
                    "gpsimd",
                    lambda e, w_=wd, b_=b, gn_=gninv: e.tensor_scalar(
                        nb[:, 0:w_], stg[b_][:, 0:w_], gn_[:, 0:1], 0.5,
                        ALU.mult, ALU.is_gt),
                    waits=[e_ld, e_g3, ev_q.get(u2 - 1)], inc=True)
                ev_q[u2] = g.op(
                    "gpsimd",
                    lambda e, w_=wd, b_=b: e.tensor_sub(qb[b_][:, 0:w_], pb[:, 0:w_],
                                                        nb[:, 0:w_]),
                    waits=[ev_n[u2], ev_p[u2], ev_st.get(u2 - 2)], inc=True)
                n_i = wd // 128
                i0 = j0 // 128
                dst = scr_dram[i0:i0 + n_i, :, rc * 128:(rc + 1) * 128].rearrange(
                    "i p j -> p i j")
                ev_st[u2] = g.load(dst, qb[b][:, 0:wd].rearrange("p (i j) -> p i j", j=128),
                                   waits=[ev_q[u2]])
                last_store = ev_st[u2]
                u2 += 1
                yield
            out_evs[key].append(last_store)


class PrepDriver:
    def __init__(self, gen):
        self.gen = gen

    def drive(self, n):
        if self.gen is None:
            return
        for _ in range(n):
            try:
                next(self.gen)
            except StopIteration:
                self.gen = None
                return

    def finish(self):
        if self.gen is not None:
            for _ in self.gen:
                pass
            self.gen = None


def emit_l0_ffn(g, c, W, prep_evs, prep=None):
    def sev(key, chunk):
        want = chunk // 8
        while prep is not None and prep.gen is not None and \
                len(prep_evs.get(key, [])) <= want:
            prep.drive(1)
        lst = prep_evs.get(key, [])
        return lst[min(want, len(lst) - 1)] if lst else None
    es = ExitStack()
    w1b = [g.sb("l0f_w1", [128, CH * 128], BF16, es) for _ in range(2)]
    w2b = [g.sb("l0f_w2", [128, CH * 128], BF16, es) for _ in range(2)]
    w3b = [g.sb("l0f_w3", [128, L0HC * 128], BF16, es) for _ in range(2)]
    sg = g.sb("l0f_sg", [128, TAU], F32, es)
    psG = [g.ps("l0f_g", [128, TAU], es) for _ in range(2)]
    psU = [g.ps("l0f_u", [128, TAU], es) for _ in range(2)]
    psD = [g.ps("l0f_d", [128, TAU], es) for _ in range(2)]
    ev_evac_all = []
    ev_gmm = {}
    ev_sg = {}
    ev_h = {}
    ev_w = {}
    ev_dmm = {}
    ev_dev = {}
    step = 0
    dstep = 0
    for tau in range(NTAU):
        t0 = tau * TAU
        for i in range(L0HC):
            b = step % 2
            if prep is not None:
                prep.drive(1)
            ev_w[step] = [
                g.load(w1b[b][:], W["w1q"][i], waits=[sev("w1", i), ev_gmm.get(step - 2)]),
                g.load(w2b[b][:], W["w2q"][i], waits=[sev("w2", i), ev_gmm.get(step - 2)]),
            ]
            lastg = lastu = None
            for chn in range(CH):
                lastg = mm(g, psG[b][:], w1b[b][:, chn * 128:(chn + 1) * 128],
                           xs(c, chn, t0, TAU), chn == 0, chn == CH - 1,
                           waits=(ev_w[step] + [ev_sg.get(step - 2)] if chn == 0 else ()),
                           inc=(chn == CH - 1))
                lastu = mm(g, psU[b][:], w2b[b][:, chn * 128:(chn + 1) * 128],
                           xs(c, chn, t0, TAU), chn == 0, chn == CH - 1,
                           waits=([ev_h.get(step - 2)] if chn == 0 else ()),
                           inc=(chn == CH - 1))
            ev_gmm[step] = lastu
            ev_sg[step] = g.op("scalar",
                               lambda e, b_=b: e.activation(sg[:], psG[b_][:], AF.Sigmoid),
                               waits=[lastg, ev_h.get(step - 1)], inc=True)
            ev_h[step] = g.op(
                "vector",
                lambda e, b_=b, ii=i: e.tensor_mul(hs_(c, ii), sg[:], psU[b_][:]),
                waits=[lastu, ev_sg[step]], inc=True)
            step += 1
        ev_h_tau = [ev_h[step - 1 - k] for k in range(L0HC)]
        for m in range(CH):
            b = dstep % 2
            if prep is not None:
                prep.drive(1)
            e_w3 = g.load(w3b[b][:], W["w3q"][m],
                          waits=[sev("w3", m), ev_dmm.get(dstep - 2)])
            last = None
            for ic in range(L0HC):
                last = mm(g, psD[b][:], w3b[b][:, ic * 128:(ic + 1) * 128], hs_(c, ic),
                          ic == 0, ic == L0HC - 1,
                          waits=(([e_w3, ev_dev.get(dstep - 2)] + ev_h_tau)
                                 if ic == 0 else ()),
                          inc=(ic == L0HC - 1))
            ev_dmm[dstep] = last
            ev_dev[dstep] = g.op(
                "vector",
                lambda e, b_=b, m_=m, t=t0: e.tensor_add(
                    xs(c, m_, t, TAU), psD[b_][:], xs(c, m_, t, TAU)),
                waits=[last], inc=True)
            ev_evac_all.append(ev_dev[dstep])
            dstep += 1
    es.close()
    return ev_evac_all


def emit_l1_attn(g, c, W):
    es = ExitStack()
    bq = g.sb("l1_bq", [128, CH], F32, es)
    bk = g.sb("l1_bk", [128, CH], F32, es)
    ev_b1 = g.load(bq[:], W["bq"])
    ev_b2 = g.load(bk[:], W["bk"])
    wqkvb = [g.sb("l1_wqkv", [128, CH * 384], BF16, es) for _ in range(2)]
    bvr = [g.sb("l1_bvr", [1, 128], BF16, es) for _ in range(2)]
    q_h = g.sb("l1_qh", [128, T], BF16, es)
    k_h = g.sb("l1_kh", [128, T], BF16, es)
    v_h = g.sb("l1_vh", [128, T], BF16, es)
    pTl = [g.sb("l1_pT", [128, 1024], BF16, es) for _ in range(2)]
    rcb = g.sb("l1_rcb", [128, 1024], F32, es)

    def load_head_w(h, waits):
        b = h % 2
        e1 = g.loadc(wqkvb[b][:], W["wqkv"][h], waits=waits)
        e2 = g.loadc(bvr[b][:], W["bvr"][h:h + 1, :], waits=waits)
        return [e1, e2]

    ev_wh = {0: load_head_w(0, [c.ev_init]), 1: load_head_w(1, [c.ev_init])}
    ev_projdone = {}
    ev_head_attn_done = {}
    for h in range(NH):
        b = h % 2
        pes = ExitStack()
        psp = [g.ps("l1_pp", [128, TAU], pes) for _ in range(2)]
        first_evs = []
        ev_pev = {}
        pstep = 0
        prev_attn = ev_head_attn_done.get(h - 1, [])
        for which, (off, bias_t) in enumerate(((0, bq), (128, bk))):
            for tau in range(NTAU):
                pb = pstep % 2
                last = None
                for chn in range(CH):
                    w0 = []
                    if chn == 0:
                        w0 = ev_wh[h] + [ev_pev.get(pstep - 2), ev_b1, ev_b2] + \
                             (prev_attn if pstep == 0 else []) + \
                             ([ev_projdone[h - 2]] if h >= 2 else [])
                    last = mm(g, psp[pb][:],
                              wqkvb[b][:, chn * 384 + off: chn * 384 + off + 128],
                              xs(c, chn, tau * TAU, TAU), chn == 0, chn == CH - 1,
                              waits=w0, inc=(chn == CH - 1))
                dst = q_h if which == 0 else k_h
                ev_pev[pstep] = g.op(
                    "vector",
                    lambda e, tt=tau, ps_=psp[pb], d=dst, bt=bias_t, hh=h: e.tensor_scalar_add(
                        d[:, tt * TAU:(tt + 1) * TAU], ps_[:], bt[:, hh:hh + 1]),
                    waits=[last], inc=True)
                first_evs.append(ev_pev[pstep])
                pstep += 1
        for grp in range(4):
            pb = pstep % 2
            last = None
            for j in range(4):
                kc = grp * 4 + j
                for chn in range(CH):
                    last = mm(g, psp[pb][:, j * 128:(j + 1) * 128],
                              xs(c, chn, kc * 128, 128),
                              wqkvb[b][:, chn * 384 + 256: chn * 384 + 384],
                              chn == 0, False,
                              waits=([ev_pev.get(pstep - 2)] if (j == 0 and chn == 0) else ()),
                              inc=False)
                last = mm(g, psp[pb][:, j * 128:(j + 1) * 128], c.ones1[:], bvr[b][:],
                          False, True, inc=(j == 3))
            ev_pev[pstep] = g.op(
                "vector",
                lambda e, gg=grp, ps_=psp[pb]: e.tensor_copy(v_h[:, gg * TAU:(gg + 1) * TAU],
                                                             ps_[:]),
                waits=[last], inc=True)
            first_evs.append(ev_pev[pstep])
            pstep += 1
        ev_projdone[h] = ev_pev[pstep - 1]
        pes.close()
        if h + 2 < NH:
            ev_wh[h + 2] = load_head_w(h + 2, [ev_projdone[h]])
        aes = ExitStack()
        psS = [g.ps("l1_s", [128, 1024], aes) for _ in range(2)]
        psR = g.ps("l1_r", [128, 1024], aes)
        psA = g.ps("l1_a", [128, 1024], aes)
        ev_half = []
        for half in range(2):
            q0 = half * 1024
            ev_S = {}
            ev_exp = {}
            ev_Rmm = {}
            ev_Amm = {}
            for kc in range(CH):
                p2 = kc % 2
                last = None
                for n in range(2):
                    w0 = []
                    if n == 0:
                        w0 = [ev_exp.get(kc - 2)]
                        if kc == 0:
                            w0 += first_evs + ev_half + prev_attn
                    last = mm(g, psS[p2][:, n * 512:(n + 1) * 512],
                              k_h[:, kc * 128:(kc + 1) * 128],
                              q_h[:, q0 + n * 512: q0 + (n + 1) * 512],
                              True, True, waits=w0, inc=(n == 1))
                ev_S[kc] = last
                ev_exp[kc] = g.op(
                    "scalar",
                    lambda e, pt_=pTl[p2], ss_=psS[p2]: e.activation(
                        pt_[:], ss_[:], AF.Exp, bias=c.cpool[:, 0:1],
                        scale=c.cpool[:, 1:2]),
                    waits=[ev_S[kc], ev_Rmm.get(kc - 2), ev_Amm.get(kc - 2)], inc=True)
                last = None
                for n in range(2):
                    last = mm(g, psR[:, n * 512:(n + 1) * 512], c.allones[:],
                              pTl[p2][:, n * 512:(n + 1) * 512], kc == 0, kc == CH - 1,
                              waits=([ev_exp[kc]] + (ev_half if kc == 0 else []))
                              if n == 0 else (),
                              inc=(n == 1))
                ev_Rmm[kc] = last
                last = None
                for n in range(2):
                    last = mm(g, psA[:, n * 512:(n + 1) * 512],
                              v_h[:, kc * 128:(kc + 1) * 128],
                              pTl[p2][:, n * 512:(n + 1) * 512], kc == 0, kc == CH - 1,
                              waits=() if n == 1 else [ev_exp[kc]],
                              inc=(n == 1))
                ev_Amm[kc] = last
            e_rc = g.op("vector", lambda e, pr_=psR: e.reciprocal(rcb[:], pr_[:]),
                        waits=[ev_Rmm[CH - 1]], inc=True)
            e_ao = g.op(
                "vector",
                lambda e, hh=h, q0_=q0, pa_=psA: e.tensor_mul(
                    c.ao[:, hh * T + q0_: hh * T + q0_ + 1024], pa_[:], rcb[:]),
                waits=[ev_Amm[CH - 1], e_rc], inc=True)
            ev_half = [e_rc, e_ao, ev_exp[CH - 1], ev_exp[CH - 2]]
        ev_head_attn_done[h] = ev_half
        aes.close()
    es.close()




def emit_l1_ffn(g, c, W):
    es = ExitStack()
    gs = g.sb("l1f_gs", [128, L1HC], F32, es)
    us = g.sb("l1f_us", [128, L1HC], F32, es)
    ds = g.sb("l1f_ds", [128, CH], F32, es)
    e_s1 = g.load(gs[:], W["gs"])
    e_s2 = g.load(us[:], W["us"])
    e_s3 = g.load(ds[:], W["ds"])
    ev_scales = [e_s1, e_s2, e_s3]
    gwb = [g.sb("l1f_gw", [128, CH * 128], BF16, es) for _ in range(2)]
    uwb = [g.sb("l1f_uw", [128, CH * 128], BF16, es) for _ in range(2)]
    dwb = [g.sb("l1f_dw", [128, L1HC * 128], BF16, es) for _ in range(2)]
    rs_bf = g.sb("l1f_rs", [1, TAU], BF16, es)
    rsh_bf = g.sb("l1f_rsh", [1, TAU], BF16, es)
    sgt = g.sb("l1f_sg", [128, TAU], F32, es)
    psRS = g.ps("l1f_prs", [128, TAU], es)
    psG = [g.ps("l1f_pg", [128, TAU], es) for _ in range(2)]
    psU = [g.ps("l1f_pu", [128, TAU], es) for _ in range(2)]
    psD = [g.ps("l1f_pd", [128, TAU], es) for _ in range(2)]
    ev_evac_all = []
    ev_gmm = {}
    ev_sg = {}
    ev_h = {}
    ev_dmm = {}
    ev_dev = {}
    ev_rs_prev = None
    ev_rsh_prev = None
    step = 0
    dstep = 0
    for tau in range(NTAU):
        t0 = tau * TAU
        last = None
        for chn in range(CH):
            last = mm(g, psRS[:], c.allones[:], xs(c, chn, t0, TAU), chn == 0, chn == CH - 1,
                      waits=([c.ev_stream, c.ev_init, ev_rsh_prev] if chn == 0 else ()),
                      inc=(chn == CH - 1))
        ev_rs = g.op("vector", lambda e: e.tensor_copy(rs_bf[:], psRS[0:1, :]),
                     waits=[last, ev_rs_prev], inc=True)
        ev_rs_prev = ev_rs
        for i in range(L1HC):
            b = step % 2
            ev_w = [
                g.loadc(gwb[b][:], W["gw"][i], waits=[ev_gmm.get(step - 2)]),
                g.loadc(uwb[b][:], W["uw"][i], waits=[ev_gmm.get(step - 2)]),
            ]
            lastg = lastu = None
            for chn in range(CH):
                lastg = mm(g, psG[b][:], gwb[b][:, chn * 128:(chn + 1) * 128],
                           xs(c, chn, t0, TAU), chn == 0, False,
                           waits=(ev_w + [ev_sg.get(step - 2), ev_rs] if chn == 0 else ()),
                           inc=False)
                lastu = mm(g, psU[b][:], uwb[b][:, chn * 128:(chn + 1) * 128],
                           xs(c, chn, t0, TAU), chn == 0, False,
                           waits=([ev_h.get(step - 2)] if chn == 0 else ()),
                           inc=False)
            lastg = mm(g, psG[b][:], c.mrow[:], rs_bf[:], False, True, inc=True)
            lastu = mm(g, psU[b][:], c.mrow[:], rs_bf[:], False, True, inc=True)
            ev_gmm[step] = lastu
            ev_sg[step] = g.op(
                "scalar",
                lambda e, b_=b, ii=i: e.activation(sgt[:], psG[b_][:], AF.Silu,
                                                   scale=gs[:, ii:ii + 1]),
                waits=[lastg, ev_h.get(step - 1), ev_scales[0]], inc=True)
            ev_h[step] = g.op(
                "vector",
                lambda e, b_=b, ii=i: e.scalar_tensor_tensor(
                    hs_(c, ii), psU[b_][:], us[:, ii:ii + 1], sgt[:], ALU.mult, ALU.mult),
                waits=[lastu, ev_sg[step], ev_scales[1]], inc=True)
            step += 1
        ev_h_tau = [ev_h[step - 1 - k] for k in range(L1HC)]
        last = None
        for ic in range(L1HC):
            last = mm(g, psRS[:], c.allones[:], hs_(c, ic), ic == 0, ic == L1HC - 1,
                      waits=(([ev_rs] + ev_h_tau) if ic == 0 else ()),
                      inc=(ic == L1HC - 1))
        ev_rsh = g.op("vector", lambda e: e.tensor_copy(rsh_bf[:], psRS[0:1, :]),
                      waits=[last, ev_rsh_prev], inc=True)
        ev_rsh_prev = ev_rsh
        for m in range(CH):
            b = dstep % 2
            e_w3 = g.loadc(dwb[b][:], W["dw"][m], waits=[ev_dmm.get(dstep - 2)])
            last = None
            for ic in range(L1HC):
                last = mm(g, psD[b][:], dwb[b][:, ic * 128:(ic + 1) * 128], hs_(c, ic),
                          ic == 0, False,
                          waits=(([e_w3, ev_dev.get(dstep - 2), ev_rsh] + ev_h_tau)
                                 if ic == 0 else ()),
                          inc=False)
            last = mm(g, psD[b][:], c.mrow[:], rsh_bf[:], False, True, inc=True)
            ev_dmm[dstep] = last
            ev_dev[dstep] = g.op(
                "vector",
                lambda e, b_=b, m_=m, t=t0: e.scalar_tensor_tensor(
                    xs(c, m_, t, TAU), psD[b_][:], ds[:, m_:m_ + 1], xs(c, m_, t, TAU),
                    ALU.mult, ALU.add),
                waits=[last, ev_scales[2]], inc=True)
            ev_evac_all.append(ev_dev[dstep])
            dstep += 1
    es.close()
    return ev_evac_all


def _declare_l0a(nc, first):
    t = {}
    t["xin"] = nc.dram_tensor("xin", [D, T], F32 if first else BF16, kind="ExternalInput").ap()
    t["wq"] = nc.dram_tensor("wq", [NH, 128, CH * 128], F32, kind="ExternalInput").ap()
    t["wkv"] = nc.dram_tensor("wkv", [128, LATC * CH * 128], F32, kind="ExternalInput").ap()
    t["wk"] = nc.dram_tensor("wk", [NH, 128, LATC * 128], F32, kind="ExternalInput").ap()
    t["wv"] = nc.dram_tensor("wv", [NH, 128, LATC * 128], F32, kind="ExternalInput").ap()
    t["wo"] = nc.dram_tensor("wo", [CH, 128, CH * 128], F32, kind="ExternalInput").ap()
    t["masks"] = nc.dram_tensor("masks", [3, 128, 128], F32, kind="ExternalInput").ap()
    return t


def _declare_l0f(nc):
    t = {}
    t["w1T"] = nc.dram_tensor("w1T", [D, L0H], F32, kind="ExternalInput").ap()
    t["w2T"] = nc.dram_tensor("w2T", [D, L0H], F32, kind="ExternalInput").ap()
    t["w3T"] = nc.dram_tensor("w3T", [L0H, D], F32, kind="ExternalInput").ap()
    t["w1q"] = nc.dram_tensor("w1q", [L0HC, 128, CH * 128], BF16, kind="Internal").ap()
    t["w2q"] = nc.dram_tensor("w2q", [L0HC, 128, CH * 128], BF16, kind="Internal").ap()
    t["w3q"] = nc.dram_tensor("w3q", [CH, 128, L0HC * 128], BF16, kind="Internal").ap()
    return t


def build_l0_ffn_launch(first=False, last=False):
    nc = bass.Bass("TRN2", target_bir_lowering=False, debug=False)
    t = _declare_l0f(nc)
    xin = nc.dram_tensor("xin", [D, T], F32 if first else BF16, kind="ExternalInput").ap()
    xo = nc.dram_tensor("xo", [D, T], F32 if last else BF16, kind="ExternalOutput").ap()
    g = G(nc)
    c = setup_common(g)
    emit_load_stream(g, c, xin, first)
    prep_evs = {}
    pd = PrepDriver(gen_quantize_prep(g, c, [
        ("w1", t["w1T"], t["w1q"], CH, L0H),
        ("w2", t["w2T"], t["w2q"], CH, L0H),
        ("w3", t["w3T"], t["w3q"], L0HC, D),
    ], prep_evs))
    pd.finish()
    g.barrier()
    evs = emit_l0_ffn(g, c, t, prep_evs)
    ev_out = emit_ln(g, c, evs, final_out=(xo if last else None))
    if not last:
        ev_last = emit_store_stream(g, c, xo, waits=ev_out)
    else:
        ev_last = ev_out[-1]
    g.op("gpsimd", lambda e: e.engine_nop(), waits=[ev_last] + list(ev_out), inc=True)
    sems = {}
    sem_es = ExitStack()
    for s in ALL_SEMS:
        sems[s] = sem_es.enter_context(nc.semaphore(s))
    g.build(sems)
    g.es.close()
    sem_es.close()
    return nc


def _declare_l1a(nc):
    t = {}
    t["wqkv"] = nc.dram_tensor("wqkv", [NH, 128, CH * 384], F32, kind="ExternalInput").ap()
    t["bq"] = nc.dram_tensor("bq", [128, CH], F32, kind="ExternalInput").ap()
    t["bk"] = nc.dram_tensor("bk", [128, CH], F32, kind="ExternalInput").ap()
    t["bvr"] = nc.dram_tensor("bvr", [NH, 128], F32, kind="ExternalInput").ap()
    t["wout"] = nc.dram_tensor("wout", [CH, 128, CH * 128], F32, kind="ExternalInput").ap()
    t["bo"] = nc.dram_tensor("bo", [128, CH], F32, kind="ExternalInput").ap()
    return t


def build_l1_attn_launch(first=False, last=False):
    nc = bass.Bass("TRN2", target_bir_lowering=False, debug=False)
    t = _declare_l1a(nc)
    xin = nc.dram_tensor("xin", [D, T], F32 if first else BF16, kind="ExternalInput").ap()
    xo = nc.dram_tensor("xo", [D, T], F32 if last else BF16, kind="ExternalOutput").ap()
    g = G(nc)
    c = setup_common(g)
    bo = g.sb("l1_bo", [128, CH], F32)
    ev_bo = g.load(bo[:], t["bo"])
    emit_load_stream(g, c, xin, first)
    emit_l1_attn(g, c, t)
    g.barrier()
    ev_out = emit_outproj(g, c, t["wout"], bo, final_out=(xo if last else None))
    if not last:
        ev_last = emit_store_stream(g, c, xo, waits=ev_out)
    else:
        ev_last = ev_out[-1]
    g.op("gpsimd", lambda e: e.engine_nop(), waits=[ev_last] + list(ev_out), inc=True)
    sems = {}
    sem_es = ExitStack()
    for s in ALL_SEMS:
        sems[s] = sem_es.enter_context(nc.semaphore(s))
    g.build(sems)
    g.es.close()
    sem_es.close()
    return nc


def _declare_l1f(nc):
    t = {}
    t["gw"] = nc.dram_tensor("gw", [L1HC, 128, CH * 128], I32, kind="ExternalInput").ap()
    t["uw"] = nc.dram_tensor("uw", [L1HC, 128, CH * 128], I32, kind="ExternalInput").ap()
    t["dw"] = nc.dram_tensor("dw", [CH, 128, L1HC * 128], I32, kind="ExternalInput").ap()
    t["gs"] = nc.dram_tensor("gs", [128, L1HC], F32, kind="ExternalInput").ap()
    t["us"] = nc.dram_tensor("us", [128, L1HC], F32, kind="ExternalInput").ap()
    t["ds"] = nc.dram_tensor("ds", [128, CH], F32, kind="ExternalInput").ap()
    return t


def build_l1_ffn_launch(first=False, last=False):
    nc = bass.Bass("TRN2", target_bir_lowering=False, debug=False)
    t = _declare_l1f(nc)
    xin = nc.dram_tensor("xin", [D, T], F32 if first else BF16, kind="ExternalInput").ap()
    xo = nc.dram_tensor("xo", [D, T], F32 if last else BF16, kind="ExternalOutput").ap()
    g = G(nc)
    c = setup_common(g)
    emit_load_stream(g, c, xin, first)
    evs = emit_l1_ffn(g, c, t)
    ev_out = emit_ln(g, c, evs, final_out=(xo if last else None))
    if not last:
        ev_last = emit_store_stream(g, c, xo, waits=ev_out)
    else:
        ev_last = ev_out[-1]
    g.op("gpsimd", lambda e: e.engine_nop(), waits=[ev_last] + list(ev_out), inc=True)
    sems = {}
    sem_es = ExitStack()
    for s in ALL_SEMS:
        sems[s] = sem_es.enter_context(nc.semaphore(s))
    g.build(sems)
    g.es.close()
    sem_es.close()
    return nc


def build_fused_launch():
    nc = bass.Bass("TRN2", target_bir_lowering=False, debug=False)
    t0a = _declare_l0a(nc, True)
    t0f = _declare_l0f(nc)
    t1a = _declare_l1a(nc)
    t1f = _declare_l1f(nc)
    xo = nc.dram_tensor("xo", [D, T], F32, kind="ExternalOutput").ap()
    g = G(nc)
    c = setup_common(g, t0a["masks"])
    bo = g.sb("l1_bo", [128, CH], F32)
    g.load(bo[:], t1a["bo"])
    emit_load_stream(g, c, t0a["xin"], True)
    prep_evs = {}
    prep = PrepDriver(gen_quantize_prep(g, c, [
        ("w1", t0f["w1T"], t0f["w1q"], CH, L0H),
        ("w2", t0f["w2T"], t0f["w2q"], CH, L0H),
        ("w3", t0f["w3T"], t0f["w3q"], L0HC, D),
    ], prep_evs))
    prep.drive(1)
    emit_l0_attn(g, c, t0a, prep=prep)
    prep.drive(3)
    g.barrier()
    emit_outproj(g, c, t0a["wo"], None, prep=prep)
    g.barrier()
    evs = emit_l0_ffn(g, c, t0f, prep_evs, prep=prep)
    prep.finish()
    ev_out = emit_ln(g, c, evs)
    g.barrier()
    emit_l1_attn(g, c, t1a)
    g.barrier()
    emit_outproj(g, c, t1a["wout"], bo)
    g.barrier()
    evs = emit_l1_ffn(g, c, t1f)
    ev_out = emit_ln(g, c, evs, final_out=xo)
    g.op("gpsimd", lambda e: e.engine_nop(), waits=list(ev_out), inc=True)
    sems = {}
    sem_es = ExitStack()
    for s in ALL_SEMS:
        sems[s] = sem_es.enter_context(nc.semaphore(s))
    g.build(sems)
    g.es.close()
    sem_es.close()
    return nc


def build_l0_attn_launch(first=True, last=False, debug=False):
    nc = bass.Bass("TRN2", target_bir_lowering=False, debug=False)
    t = _declare_l0a(nc, first)
    xo = nc.dram_tensor("xo", [D, T], F32 if last else BF16, kind="ExternalOutput").ap()
    if debug:
        dbg_x0 = nc.dram_tensor("dbg_x0", [D, T], BF16, kind="ExternalOutput").ap()
        dbg_ao = nc.dram_tensor("dbg_ao", [D, T], BF16, kind="ExternalOutput").ap()
        dbg_kv = nc.dram_tensor("dbg_kv", [LAT, T], BF16, kind="ExternalOutput").ap()
    g = G(nc)
    c = setup_common(g, t["masks"])
    emit_load_stream(g, c, t["xin"], first)
    if debug:
        for i in range(CH):
            g.store(dbg_x0[i * 128:(i + 1) * 128, :], xs(c, i, 0, T), waits=[c.ev_stream])
    emit_l0_attn(g, c, t, dbg_kv if debug else None)
    g.barrier()
    if debug:
        for i in range(CH):
            g.store(dbg_ao[i * 128:(i + 1) * 128, :], aos(c, i, 0, T))
        g.barrier()
    ev_out = emit_outproj(g, c, t["wo"], None, final_out=(xo if last else None))
    if not last:
        ev_last = emit_store_stream(g, c, xo, waits=ev_out)
    else:
        ev_last = ev_out[-1]
    g.op("gpsimd", lambda e: e.engine_nop(), waits=[ev_last] + list(ev_out), inc=True)
    sems = {}
    sem_es = ExitStack()
    for s in ALL_SEMS:
        sems[s] = sem_es.enter_context(nc.semaphore(s))
    g.build(sems)
    g.es.close()
    sem_es.close()
    return nc


def _pack_panels(w):
    K, M = w.shape
    return np.ascontiguousarray(
        w.reshape(K // 128, 128, M // 128, 128).transpose(2, 1, 0, 3).reshape(M // 128, 128, K))


def _make_masks():
    i = np.arange(128)
    m_diag = (i[:, None] <= i[None, :]).astype(np.float32)
    m_prev = (i[:, None] > i[None, :]).astype(np.float32)
    m_prev0 = m_prev.copy()
    m_prev0[0:16, :] = 1.0
    return np.stack([m_diag, m_prev, m_prev0])


def _l1a_packs(inputs):
    wqkv = inputs["L1_attn_w_qkv"]
    wt = np.ascontiguousarray(wqkv.T)
    per_head = []
    for h in range(NH):
        sl = np.concatenate([wt[:, h * 128:(h + 1) * 128],
                             wt[:, 2048 + h * 128: 2048 + (h + 1) * 128],
                             wt[:, 4096 + h * 128: 4096 + (h + 1) * 128]], axis=1)
        per_head.append(sl.reshape(CH, 128, 384).transpose(1, 0, 2).reshape(128, CH * 384))
    b = inputs["L1_attn_b_qkv"]
    return {
        "wqkv": np.ascontiguousarray(np.stack(per_head)),
        "bq": np.ascontiguousarray(b[0:2048].reshape(CH, 128).T),
        "bk": np.ascontiguousarray(b[2048:4096].reshape(CH, 128).T),
        "bvr": np.ascontiguousarray(b[4096:6144].reshape(NH, 128)),
        "wout": _pack_panels(np.ascontiguousarray(inputs["L1_attn_w_out"].T)),
        "bo": np.ascontiguousarray(inputs["L1_attn_b_out"].reshape(CH, 128).T),
    }


def _l1f_packs(inputs):
    return {
        "gw": _pack_panels(np.ascontiguousarray(inputs["L1_ffn_gate_w"].T)),
        "uw": _pack_panels(np.ascontiguousarray(inputs["L1_ffn_up_w"].T)),
        "dw": _pack_panels(np.ascontiguousarray(inputs["L1_ffn_down_w"].T)),
        "gs": np.ascontiguousarray(inputs["L1_ffn_gate_scale"].reshape(L1HC, 128).T),
        "us": np.ascontiguousarray(inputs["L1_ffn_up_scale"].reshape(L1HC, 128).T),
        "ds": np.ascontiguousarray(inputs["L1_ffn_down_scale"].reshape(CH, 128).T),
    }


def _l0a_packs(inputs):
    wkv = _pack_panels(inputs["L0_attn_Wkv"])
    return {
        "wq": _pack_panels(inputs["L0_attn_Wq"]),
        "wkv": np.ascontiguousarray(wkv.transpose(1, 0, 2).reshape(128, LATC * CH * 128)),
        "wk": _pack_panels(inputs["L0_attn_Wk"]),
        "wv": _pack_panels(inputs["L0_attn_Wv"]),
        "wo": _pack_panels(inputs["L0_attn_Wo"]),
        "masks": _make_masks(),
    }


_FUSED = True


def _all_packs(inputs):
    p = {}
    p.update(_l0a_packs(inputs))
    p.update({
        "w1T": np.ascontiguousarray(inputs["L0_ffn_w1"].T),
        "w2T": np.ascontiguousarray(inputs["L0_ffn_w2"].T),
        "w3T": np.ascontiguousarray(inputs["L0_ffn_w3"].T),
    })
    p.update(_l1a_packs(inputs))
    p.update(_l1f_packs(inputs))
    return p


def kernel(**inputs):
    x = inputs["L0_attn_x"]
    B = x.shape[0]
    cur = [np.ascontiguousarray(x[b].T) for b in range(B)]
    if _FUSED:
        packs = _all_packs(inputs)
        nc = build_fused_launch()
        in_maps = [dict(packs, xin=cur[b]) for b in range(B)]
        res = run_bass_kernel_spmd(nc, in_maps, core_ids=list(range(B)))
        cur = [np.asarray(res.results[b]["xo"]) for b in range(B)]
    else:
        l0a = _l0a_packs(inputs)
        l0f = {
            "w1T": np.ascontiguousarray(inputs["L0_ffn_w1"].T),
            "w2T": np.ascontiguousarray(inputs["L0_ffn_w2"].T),
            "w3T": np.ascontiguousarray(inputs["L0_ffn_w3"].T),
        }
        l1a = _l1a_packs(inputs)
        l1f = _l1f_packs(inputs)
        launches = [
            ("l0a", build_l0_attn_launch, l0a),
            ("l0f", build_l0_ffn_launch, l0f),
            ("l1a", build_l1_attn_launch, l1a),
            ("l1f", build_l1_ffn_launch, l1f),
        ]
        for li, (name, builder, packs) in enumerate(launches):
            nc = builder(first=(li == 0), last=(li == len(launches) - 1))
            in_maps = [dict(packs, xin=cur[b]) for b in range(B)]
            res = run_bass_kernel_spmd(nc, in_maps, core_ids=list(range(B)))
            cur = [np.asarray(res.results[b]["xo"]) for b in range(B)]
    return np.stack([cur[b].astype(np.float32).T for b in range(B)])


# version 15
# speedup vs baseline: 1.3818x; 1.3818x over previous
import numpy as np
import concourse.bass as bass
import concourse.mybir as mybir
from concourse.bass_utils import run_bass_kernel_spmd
from contextlib import ExitStack

F32 = mybir.dt.float32
BF16 = mybir.dt.bfloat16
I32 = mybir.dt.int32
AF = mybir.ActivationFunctionType
ALU = mybir.AluOpType

D = 2048
T = 2048
CH = 16
NH = 16
HD = 128
TAU = 512
NTAU = T // TAU
LAT = 512
LATC = LAT // 128
L0H = 5632
L0HC = L0H // 128
L1H = 8192
L1HC = L1H // 128
SM_SCALE = float(1.0 / np.sqrt(128.0))
SM_SHIFT = -12.0
LN_EPS = 1e-5
Q_EPS = 1e-8
INV_D = 1.0 / D

ENGS = ("sync", "tensor", "scalar", "vector", "gpsimd")
ESEM = {"tensor": "pe", "scalar": "act", "vector": "dve", "gpsimd": "pool"}
QROT = {"sync": ["qs%d" % i for i in range(4)],
        "gpsimd": ["qp%d" % i for i in range(6)],
        "scalar": ["qa%d" % i for i in range(2)]}
ALL_SEMS = ("pe", "act", "dve", "pool") + tuple(s for v in QROT.values() for s in v)


class G:

    def __init__(self, nc):
        self.nc = nc
        self.ops = []
        self.cnt = {s: 0 for s in ALL_SEMS}
        self.qrr = {}
        self.snapshots = {}
        self.barv = 0
        self.es = ExitStack()
        self.uid = 0

    def _nm(self, name):
        self.uid += 1
        return "%s_u%d" % (name, self.uid)

    def sb(self, name, shape, dtype, es=None):
        return (es or self.es).enter_context(self.nc.sbuf_tensor(self._nm(name), shape, dtype))

    def ps(self, name, shape, es=None, dtype=F32):
        return (es or self.es).enter_context(self.nc.psum_tensor(self._nm(name), shape, dtype))

    def op(self, eng, fn, waits=(), inc=False):
        w = [x for x in waits if x is not None]
        ev = None
        upd = None
        if inc:
            sem = ESEM[eng]
            self.cnt[sem] += 1
            ev = (sem, self.cnt[sem])
            upd = (sem, 1)
        self.ops.append((eng, fn, w, upd, self.barv))
        return ev

    def dma(self, eng, out, in_, waits=()):
        rot = QROT[eng]
        q = rot[self.qrr.setdefault(eng, 0) % len(rot)]
        self.qrr[eng] += 1
        self.cnt[q] += 16
        ev = (q, self.cnt[q])
        def fn(e, o=out, i=in_):
            return e.dma_start(out=o, in_=i)
        self.ops.append((eng, fn, [x for x in waits if x is not None], (q, 16), self.barv))
        return ev

    def load(self, out, in_, waits=()):
        return self.dma("sync", out, in_, waits)

    def loadc(self, out, in_, waits=()):
        return self.dma("gpsimd", out, in_, waits)

    def store(self, out, in_, waits=()):
        return self.dma("scalar", out, in_, waits)

    def barrier(self):
        self.barv += 1
        self.snapshots[self.barv] = dict(self.cnt)

    def build(self, sems):
        ops = self.ops
        snapshots = self.snapshots

        def mk(ename):
            def f(eng):
                waited = {}
                seen_bar = [0]
                own = ESEM.get(ename)

                def do_wait(s, v):
                    if v <= 0 or waited.get(s, 0) >= v:
                        return
                    eng.wait_ge(sems[s], v)
                    waited[s] = v

                for (e, fn, waits, upd, barv) in ops:
                    if e != ename:
                        continue
                    if barv > seen_bar[0]:
                        for s, v in snapshots[barv].items():
                            if s != own:
                                do_wait(s, v)
                        seen_bar[0] = barv
                    for (s, v) in waits:
                        do_wait(s, v)
                    ins = fn(eng)
                    if upd is not None:
                        ins.then_inc(sems[upd[0]], upd[1])
            return f

        with self.nc.Block() as block:
            for ename in ENGS:
                getattr(block, ename)(mk(ename))


def mm(g, out_ap, lhsT_ap, rhs_ap, start, stop, waits=(), inc=False):
    return g.op(
        "tensor",
        lambda e, o=out_ap, l=lhsT_ap, r=rhs_ap, s1=start, s2=stop:
            e.matmul(o, l, r, start=s1, stop=s2),
        waits, inc)


class Ctx:
    pass


def setup_common(g, masks_dram=None):
    c = Ctx()
    c.x = g.sb("x_sb", [128, CH * T], BF16)
    c.ao = g.sb("ao_sb", [128, CH * T], BF16)
    c.allones = g.sb("allones", [128, 128], BF16)
    c.ones1 = g.sb("ones1", [1, 128], BF16)
    c.mrow = g.sb("mrow", [1, 128], BF16)
    c.cpool = g.sb("cpool", [128, 8], F32)
    g.op("vector", lambda e: e.memset(c.allones[:], 1.0), inc=True)
    g.op("vector", lambda e: e.memset(c.ones1[:], 1.0), inc=True)
    g.op("vector", lambda e: e.memset(c.mrow[:], -128.0), inc=True)
    g.op("vector", lambda e: e.memset(c.cpool[:, 0:1], SM_SHIFT), inc=True)
    g.op("vector", lambda e: e.memset(c.cpool[:, 1:2], SM_SCALE), inc=True)
    g.op("vector", lambda e: e.memset(c.cpool[:, 2:3], LN_EPS), inc=True)
    c.ev_init = g.op("vector", lambda e: e.memset(c.cpool[:, 3:4], Q_EPS), inc=True)
    c.ev_masks = None
    if masks_dram is not None:
        c.masks = g.sb("masks", [128, 3 * 128], BF16)
        for i in range(3):
            c.ev_masks = g.loadc(c.masks[:, i * 128:(i + 1) * 128], masks_dram[i])
    return c


def xs(c, ch, t0, n):
    return c.x[:, ch * T + t0: ch * T + t0 + n]


def aos(c, ch, t0, n):
    return c.ao[:, ch * T + t0: ch * T + t0 + n]


def emit_load_stream(g, c, xin, fp32_src):
    ev = None
    if fp32_src:
        for ch in range(CH):
            ev = g.loadc(xs(c, ch, 0, T), xin[ch * 128:(ch + 1) * 128, :])
    else:
        for i in range(4):
            dst = c.x[:, i * 4 * T:(i + 1) * 4 * T].rearrange("p (c t) -> p c t", t=T)
            s = xin.rearrange("(c p) t -> c p t", p=128)[i * 4:(i + 1) * 4]
            ev = g.load(dst, s.rearrange("c p t -> p c t"))
    c.ev_stream = ev
    return ev


def emit_store_stream(g, c, xout, waits):
    ev = None
    for i in range(4):
        s = c.x[:, i * 4 * T:(i + 1) * 4 * T].rearrange("p (c t) -> p c t", t=T)
        d = xout.rearrange("(c p) t -> c p t", p=128)[i * 4:(i + 1) * 4]
        ev = g.load(d.rearrange("c p t -> p c t"), s, waits=waits)
    return ev


def emit_ln(g, c, ev_r_ready, final_out=None):
    es = ExitStack()
    psum_s = g.ps("ln_sum", [128, TAU], es)
    psum_q = g.ps("ln_sq", [128, TAU], es)
    r2 = [g.sb("ln_r2", [128, TAU], BF16, es) for _ in range(2)]
    m_bc = g.sb("ln_m", [128, TAU], F32, es)
    tmp = g.sb("ln_t", [128, TAU], F32, es)
    rstd = g.sb("ln_rs", [128, TAU], F32, es)
    t1 = g.sb("ln_t1", [128, TAU], F32, es)
    if final_out is not None:
        outf = [g.sb("ln_of", [128, TAU], F32, es) for _ in range(2)]
    ev_all = []
    prev_stats_consumed = []
    prev_dma = {}
    for tau in range(NTAU):
        t0 = tau * TAU
        ev_sq = [None] * CH
        ev_mmq = [None] * CH
        last_sum = None
        for chn in range(CH):
            ev_sq[chn] = g.op(
                "scalar",
                lambda e, ch=chn, t=t0: e.activation(r2[ch % 2][:], xs(c, ch, t, TAU), AF.Square),
                waits=list(ev_r_ready) + ([ev_mmq[chn - 2]] if chn >= 2 else []),
                inc=True)
            last_sum = mm(g, psum_s[:], c.allones[:], xs(c, chn, t0, TAU),
                          chn == 0, chn == CH - 1,
                          waits=(list(ev_r_ready) + prev_stats_consumed) if chn == 0 else (),
                          inc=(chn == CH - 1))
            ev_mmq[chn] = mm(g, psum_q[:], c.allones[:], r2[chn % 2][:],
                             chn == 0, chn == CH - 1,
                             waits=[ev_sq[chn]] + (prev_stats_consumed if chn == 0 else []),
                             inc=True)
        last_sq = ev_mmq[CH - 1]
        e1 = g.op("vector",
                  lambda e: e.tensor_scalar(m_bc[:], psum_s[:], INV_D, None, ALU.mult),
                  waits=[last_sum], inc=True)
        e2 = g.op("vector", lambda e: e.tensor_mul(tmp[:], m_bc[:], m_bc[:]),
                  waits=[e1], inc=True)
        e3 = g.op("vector",
                  lambda e: e.scalar_tensor_tensor(tmp[:], psum_q[:], INV_D, tmp[:],
                                                   ALU.mult, ALU.subtract),
                  waits=[last_sq, e2], inc=True)
        e4 = g.op("scalar", lambda e: e.activation(tmp[:], tmp[:], AF.Sqrt,
                                                   bias=c.cpool[:, 2:3]),
                  waits=[e3], inc=True)
        e5 = g.op("vector", lambda e: e.reciprocal(rstd[:], tmp[:]), waits=[e4], inc=True)
        prev_stats_consumed = [e1, e3]
        for chn in range(CH):
            ea = g.op("vector",
                      lambda e, ch=chn, t=t0: e.tensor_sub(t1[:], xs(c, ch, t, TAU), m_bc[:]),
                      waits=[e5, last_sq, last_sum, ev_sq[CH - 1]], inc=True)
            if final_out is None:
                eb = g.op("vector",
                          lambda e, ch=chn, t=t0: e.tensor_mul(xs(c, ch, t, TAU), t1[:], rstd[:]),
                          waits=[ea], inc=True)
                ev_all.append(eb)
            else:
                eb = g.op("vector",
                          lambda e, ch=chn: e.tensor_mul(outf[ch % 2][:], t1[:], rstd[:]),
                          waits=[ea, prev_dma.get(chn - 2)], inc=True)
                prev_dma[chn] = g.store(final_out[chn * 128:(chn + 1) * 128, t0:t0 + TAU],
                                        outf[chn % 2][:], waits=[eb])
                ev_all.append(prev_dma[chn])
        prev_dma = {chn - CH: v for chn, v in prev_dma.items()} if final_out is not None else {}
    es.close()
    return ev_all


def emit_outproj(g, c, w_dram, bias_col, final_out=None, prep=None):
    es = ExitStack()
    wom = [g.sb("wo_m", [128, CH * 128], BF16, es) for _ in range(2)]
    psY = [g.ps("ps_y", [128, T], es) for _ in range(2)]
    ev_w = {}
    ev_mm = {}
    ev_evac = {}
    for m in range(2):
        ev_w[m] = g.loadc(wom[m][:], w_dram[m])
    for m in range(CH):
        e_last = None
        for chn in range(CH):
            for tau in range(NTAU):
                e_last = mm(g, psY[m % 2][:, tau * TAU:(tau + 1) * TAU],
                            wom[m % 2][:, chn * 128:(chn + 1) * 128],
                            aos(c, chn, tau * TAU, TAU),
                            chn == 0, chn == CH - 1,
                            waits=([ev_w[m], ev_evac.get(m - 2)]
                                   if (chn == 0 and tau == 0) else ()),
                            inc=(chn == CH - 1 and tau == NTAU - 1))
        ev_mm[m] = e_last
        if m + 2 < CH:
            ev_w[m + 2] = g.loadc(wom[m % 2][:], w_dram[m + 2], waits=[ev_mm[m]])
        eev = None
        for tau in range(NTAU):
            if bias_col is None:
                eev = g.op("vector",
                           lambda e, mm_=m, tt=tau: e.tensor_add(
                               xs(c, mm_, tt * TAU, TAU),
                               psY[mm_ % 2][:, tt * TAU:(tt + 1) * TAU],
                               xs(c, mm_, tt * TAU, TAU)),
                           waits=[ev_mm[m]], inc=True)
            else:
                eev = g.op("vector",
                           lambda e, mm_=m, tt=tau: e.scalar_tensor_tensor(
                               xs(c, mm_, tt * TAU, TAU),
                               psY[mm_ % 2][:, tt * TAU:(tt + 1) * TAU],
                               bias_col[:, mm_:mm_ + 1], xs(c, mm_, tt * TAU, TAU),
                               ALU.add, ALU.add),
                           waits=[ev_mm[m]], inc=True)
        ev_evac[m] = eev
        if prep is not None:
            prep.drive(6)
    es.close()
    return emit_ln(g, c, [ev_evac[m] for m in range(CH)], final_out)


def emit_l0_attn(g, c, W, dbg_kv=None, prep=None):
    es = ExitStack()
    kv = g.sb("kv_sb", [128, LATC * T], BF16, es)
    kv_es = ExitStack()
    wkvb = g.sb("wkvb", [128, LATC * CH * 128], BF16, kv_es)
    pskv = [g.ps("ps_kv", [128, TAU], kv_es) for _ in range(2)]
    ev_wkv = g.loadc(wkvb[:], W["wkv"], waits=[c.ev_init])
    ev_kv_ev = {}
    ev_kv_mm_last = None
    step = 0
    for lc in range(LATC):
        for tau in range(NTAU):
            b = step % 2
            last = None
            for chn in range(CH):
                last = mm(g, pskv[b][:],
                          wkvb[:, (lc * CH + chn) * 128:(lc * CH + chn + 1) * 128],
                          xs(c, chn, tau * TAU, TAU), chn == 0, chn == CH - 1,
                          waits=([ev_wkv, c.ev_stream, ev_kv_ev.get(step - 2)]
                                 if chn == 0 else ()),
                          inc=(chn == CH - 1))
            ev_kv_mm_last = last
            ev_kv_ev[step] = g.op(
                "scalar",
                lambda e, l=lc, tt=tau, bb=b: e.activation(
                    kv[:, l * T + tt * TAU: l * T + (tt + 1) * TAU], pskv[bb][:], AF.Copy),
                waits=[last], inc=True)
            step += 1
    ev_kv_done = ev_kv_ev[step - 1]
    if dbg_kv is not None:
        for i in range(LATC):
            g.store(dbg_kv[i * 128:(i + 1) * 128, :], kv[:, i * T:(i + 1) * T],
                    waits=[ev_kv_done])
    kv_es.close()

    wqh = [g.sb("wq_h", [128, CH * 128], BF16, es) for _ in range(2)]
    wkh = [g.sb("wk_h", [128, LATC * 128], BF16, es) for _ in range(2)]
    wvh = [g.sb("wv_h", [128, LATC * 128], BF16, es) for _ in range(2)]
    q_h = g.sb("q_h", [128, T], BF16, es)
    k_h = g.sb("k_h", [128, T], BF16, es)
    v_h = g.sb("v_h", [128, T], BF16, es)
    pT = [g.sb("pT", [128, 384], BF16, es) for _ in range(2)]
    rcp = g.sb("rcp", [128, 128], F32, es)
    psp = [g.ps("ps_p", [128, TAU], es) for _ in range(2)]
    psS = [g.ps("ps_s", [128, 384], es) for _ in range(2)]
    psR_t = g.ps("ps_r", [128, 128], es)
    psA_t = g.ps("ps_a", [128, 128], es)
    psR = [psR_t[:], psR_t[:]]
    psA = [psA_t[:], psA_t[:]]

    def load_head_w(h, waits):
        b = h % 2
        e1 = g.loadc(wqh[b][:], W["wq"][h], waits=waits)
        e2 = g.loadc(wkh[b][:], W["wk"][h], waits=waits)
        e3 = g.loadc(wvh[b][:], W["wv"][h], waits=waits)
        return [e1, e2, e3]

    ev_wh = {0: load_head_w(0, [ev_kv_mm_last]), 1: load_head_w(1, [ev_kv_mm_last])}
    ev_pev = {}
    ev_projdone = {}
    ev_exp = {}
    ev_R = {}
    ev_AV = {}
    ev_rcp = {}
    ev_ao = {}
    gstep = 0
    for h in range(NH):
        b = h % 2
        first_evs = []
        for tau in range(NTAU):
            pb = gstep % 2
            last = None
            for chn in range(CH):
                w0 = []
                if chn == 0:
                    w0 = ev_wh[h] + [ev_pev.get(gstep - 2)]
                    if h == 0 and tau == 0:
                        w0.append(ev_kv_done)
                last = mm(g, psp[pb][:], wqh[b][:, chn * 128:(chn + 1) * 128],
                          xs(c, chn, tau * TAU, TAU), chn == 0, chn == CH - 1,
                          waits=w0, inc=(chn == CH - 1))
            ev_pev[gstep] = g.op(
                "scalar",
                lambda e, tt=tau, bb=pb: e.activation(q_h[:, tt * TAU:(tt + 1) * TAU],
                                                      psp[bb][:], AF.Copy),
                waits=[last], inc=True)
            first_evs.append(ev_pev[gstep])
            gstep += 1
        for tau in range(NTAU):
            pb = gstep % 2
            last = None
            for lc in range(LATC):
                last = mm(g, psp[pb][:], wkh[b][:, lc * 128:(lc + 1) * 128],
                          kv[:, lc * T + tau * TAU: lc * T + (tau + 1) * TAU],
                          lc == 0, lc == LATC - 1,
                          waits=([ev_pev.get(gstep - 2)] if lc == 0 else ()),
                          inc=(lc == LATC - 1))
            ev_pev[gstep] = g.op(
                "scalar",
                lambda e, tt=tau, bb=pb: e.activation(k_h[:, tt * TAU:(tt + 1) * TAU],
                                                      psp[bb][:], AF.Copy),
                waits=[last], inc=True)
            first_evs.append(ev_pev[gstep])
            gstep += 1
        for grp in range(4):
            pb = gstep % 2
            last = None
            for j in range(4):
                kc = grp * 4 + j
                for lc in range(LATC):
                    last = mm(g, psp[pb][:, j * 128:(j + 1) * 128],
                              kv[:, lc * T + kc * 128: lc * T + (kc + 1) * 128],
                              wvh[b][:, lc * 128:(lc + 1) * 128],
                              lc == 0, lc == LATC - 1,
                              waits=([ev_pev.get(gstep - 2)] if (j == 0 and lc == 0) else ()),
                              inc=(j == 3 and lc == LATC - 1))
            ev_pev[gstep] = g.op(
                "vector",
                lambda e, gg=grp, bb=pb: e.tensor_copy(v_h[:, gg * TAU:(gg + 1) * TAU],
                                                       psp[bb][:]),
                waits=[last], inc=True)
            first_evs.append(ev_pev[gstep])
            gstep += 1
        ev_projdone[h] = ev_pev[gstep - 1]
        if h + 2 < NH:
            ev_wh[h + 2] = load_head_w(h + 2, [ev_projdone[h]])
        if prep is not None:
            prep.drive(18)

        for q in range(CH):
            k_ = h * CH + q
            p2 = k_ % 2
            qsl = q_h[:, q * 128:(q + 1) * 128]
            tiles = [(psS[p2][:, 0:128], k_h[:, q * 128:(q + 1) * 128],
                      (128, 0, 128), c.allones[:], v_h[:, q * 128:(q + 1) * 128])]
            if q >= 1:
                tiles.append((psS[p2][:, 128:256], k_h[:, (q - 1) * 128:q * 128],
                              (128, 128, 256), c.allones[:], v_h[:, (q - 1) * 128:q * 128]))
            if q >= 2:
                tiles.append((psS[p2][0:16, 256:384], k_h[:, 0:16],
                              (16, 256, 384), c.allones[0:16, :], v_h[0:16, 0:128]))
            n_t = len(tiles)
            ev_S = None
            for i, (pslice, lh, _, _, _) in enumerate(tiles):
                w0 = []
                if i == 0:
                    w0 = ([ev_exp.get(k_ - 2)] + (first_evs if q == 0 else []))
                ev_S = mm(g, pslice, lh, qsl, True, True, waits=w0, inc=(i == n_t - 1))
            war_pT = [ev_AV.get(k_ - 2), ev_R.get(k_ - 2)]
            nse = 128 if q == 0 else 256
            e_exp = g.op(
                "scalar",
                lambda e, p=p2, n=nse: e.activation(pT[p][:, 0:n], psS[p][:, 0:n], AF.Exp,
                                                    bias=c.cpool[:, 0:1],
                                                    scale=c.cpool[:, 1:2]),
                waits=[ev_S] + war_pT, inc=True)
            if q >= 2:
                e_exp = g.op(
                    "scalar",
                    lambda e, p=p2: e.activation(pT[p][0:16, 256:384], psS[p][0:16, 256:384],
                                                 AF.Exp, bias=c.cpool[0:16, 0:1],
                                                 scale=c.cpool[0:16, 1:2]),
                    inc=True)
            ev_exp[k_] = e_exp
            e_m = g.op("vector",
                       lambda e, p=p2: e.tensor_mul(pT[p][:, 0:128], pT[p][:, 0:128],
                                                    c.masks[:, 0:128]),
                       waits=[ev_exp[k_], c.ev_masks], inc=True)
            if q >= 1:
                mi = 2 if q == 1 else 1
                e_m = g.op("vector",
                           lambda e, p=p2, m=mi: e.tensor_mul(
                               pT[p][:, 128:256], pT[p][:, 128:256],
                               c.masks[:, m * 128:(m + 1) * 128]),
                           inc=True)
            ev_mask = e_m
            ev_r = None
            for i, (_, _, (np_, c0, c1), ao_lh, _) in enumerate(tiles):
                pts = pT[p2][0:np_, c0:c1]
                ev_r = mm(g, psR[p2], ao_lh, pts, i == 0, i == n_t - 1,
                          waits=([ev_mask, ev_rcp.get(k_ - 1)] if i == 0 else ()),
                          inc=(i == n_t - 1))
            ev_R[k_] = ev_r
            ev_a = None
            for i, (_, _, (np_, c0, c1), _, v_lh) in enumerate(tiles):
                pts = pT[p2][0:np_, c0:c1]
                ev_a = mm(g, psA[p2], v_lh, pts, i == 0, i == n_t - 1,
                          waits=([ev_mask, ev_ao.get(k_ - 1)] if i == 0 else ()),
                          inc=(i == n_t - 1))
            ev_AV[k_] = ev_a
            ev_rcp[k_] = g.op("vector", lambda e, pr_=psR[p2]: e.reciprocal(rcp[:], pr_),
                              waits=[ev_R[k_]], inc=True)
            ev_ao[k_] = g.op(
                "vector",
                lambda e, pa_=psA[p2], hh=h, qq=q: e.tensor_mul(
                    c.ao[:, hh * T + qq * 128: hh * T + (qq + 1) * 128], pa_, rcp[:]),
                waits=[ev_AV[k_], ev_rcp[k_]], inc=True)
    es.close()




def hs_(c, i, n=TAU):
    return c.ao[:, i * TAU:(i) * TAU + n]


def gen_quantize_prep(g, c, specs, out_evs, pes=None):
    SW = 2048
    stg = [g.sb("qp_stg", [128, SW], F32, pes) for _ in range(2)]
    nb = g.sb("qp_n", [128, SW], BF16, pes)
    qb = [g.sb("qp_q", [128, SW], BF16, pes) for _ in range(2)]
    onescol = g.sb("qp_onc", [128, 1], F32, pes)
    onesrow = g.sb("qp_onr", [1, 128], F32, pes)
    tot = g.sb("qp_tot", [1, 1], F32, pes)
    state = {}
    for key, w_dram, scr_dram, n_rc, n_cols in specs:
        nslab = [(j, min(SW, n_cols - j)) for j in range(0, n_cols, SW)]
        slabs = [(rc, j0, w) for rc in range(n_rc) for (j0, w) in nslab]
        partials = g.sb("qp_part", [128, len(slabs)], F32, pes)
        thr = g.sb("qp_th", [128, 1], F32, pes)
        nthr = g.sb("qp_nth", [128, 1], F32, pes)
        gg = g.sb("qp_gg", [128, 1], F32, pes)
        ptot = g.sb("qp_pt", [128, 1], F32, pes)
        state[key] = [slabs, partials, thr, nthr, gg, ptot, nslab]
    e_on = g.op("vector", lambda e: e.memset(onescol[:], 1.0), inc=True)
    e_on = g.op("vector", lambda e: e.memset(onesrow[:], 1.0), inc=True)
    yield
    ev_ld = {}
    ev_red = {}
    u = 0
    for key, w_dram, scr_dram, n_rc, n_cols in specs:
        slabs, partials, ginv, gninv, gg, ptot, nslab = state[key]
        for si, (rc, j0, wd) in enumerate(slabs):
            b = u % 2
            ev_ld[u] = g.load(stg[b][:, 0:wd], w_dram[rc * 128:(rc + 1) * 128, j0:j0 + wd],
                              waits=[ev_red.get(u - 2)])
            ev_red[u] = g.op(
                "vector",
                lambda e, s=si, b_=b, w_=wd, pt_=partials: e.tensor_reduce(
                    pt_[:, s:s + 1], stg[b_][:, 0:w_], mybir.AxisListType.X, ALU.add,
                    apply_absolute_value=True),
                waits=[ev_ld[u]], inc=True)
            u += 1
            yield
    ev_red_last = ev_red[u - 1]
    for key, w_dram, scr_dram, n_rc, n_cols in specs:
        slabs, partials, thr, nthr, gg, ptot, nslab = state[key]
        ges = ExitStack()
        psg1 = g.ps("qp_ps1", [1, 1], ges)
        psg2 = g.ps("qp_ps2", [128, 1], ges)
        e_pt = g.op("vector",
                    lambda e, n_=len(slabs), pt_=partials, o_=ptot: e.tensor_reduce(
                        o_[:], pt_[:, 0:n_], mybir.AxisListType.X, ALU.add),
                    waits=[ev_red_last], inc=True)
        e_m1 = mm(g, psg1[:], ptot[:], onescol[:], True, True, waits=[e_pt, e_on], inc=True)
        e_tt = g.op("vector", lambda e, t_=tot, p_=psg1: e.tensor_copy(t_[:], p_[:]),
                    waits=[e_m1], inc=True)
        e_m2 = mm(g, psg2[:], onesrow[:], tot[:], True, True, waits=[e_tt], inc=True)
        inv_n = 1.0 / (n_rc * 128 * n_cols)
        e_g1 = g.op("vector",
                    lambda e, g_=gg, p_=psg2, iv=inv_n: e.tensor_scalar(
                        g_[:], p_[:], iv, Q_EPS, ALU.mult, ALU.add),
                    waits=[e_m2], inc=True)
        e_g2 = g.op("vector",
                    lambda e, t_=thr, g_=gg: e.tensor_scalar(t_[:], g_[:], 0.5, None,
                                                             ALU.mult),
                    waits=[e_g1], inc=True)
        e_g3 = g.op("vector",
                    lambda e, t_=nthr, g_=gg: e.tensor_scalar(t_[:], g_[:], -0.5, None,
                                                              ALU.mult),
                    waits=[e_g2], inc=True)
        state[key].append(e_g3)
        ges.close()
        yield
    ev_p = {}
    ev_n = {}
    ev_q = {}
    ev_st = {}
    u2 = 0
    for key, w_dram, scr_dram, n_rc, n_cols in specs:
        slabs, partials, thr, nthr, gg, ptot, nslab, e_g3 = state[key]
        out_evs[key] = []
        for (j0, wd) in nslab:
            last_store = None
            for rc in range(n_rc):
                b = u2 % 2
                e_ld = g.load(stg[b][:, 0:wd],
                              w_dram[rc * 128:(rc + 1) * 128, j0:j0 + wd],
                              waits=[e_g3 if u2 < 2 else None, ev_q.get(u2 - 2),
                                     ev_n.get(u2 - 2)])
                ev_n[u2] = g.op(
                    "vector",
                    lambda e, w_=wd, b_=b, t_=nthr: e.tensor_scalar(
                        nb[:, 0:w_], stg[b_][:, 0:w_], t_[:, 0:1], None, ALU.is_lt),
                    waits=[e_ld, e_g3, ev_q.get(u2 - 1)], inc=True)
                ev_q[u2] = g.op(
                    "vector",
                    lambda e, w_=wd, b_=b, t_=thr: e.scalar_tensor_tensor(
                        qb[b_][:, 0:w_], stg[b_][:, 0:w_], t_[:, 0:1], nb[:, 0:w_],
                        ALU.is_gt, ALU.subtract),
                    waits=[ev_n[u2], ev_st.get(u2 - 2)], inc=True)
                n_i = wd // 128
                i0 = j0 // 128
                dst = scr_dram[i0:i0 + n_i, :, rc * 128:(rc + 1) * 128].rearrange(
                    "i p j -> p i j")
                ev_st[u2] = g.load(dst, qb[b][:, 0:wd].rearrange("p (i j) -> p i j", j=128),
                                   waits=[ev_q[u2]])
                last_store = ev_st[u2]
                u2 += 1
                yield
            out_evs[key].append(last_store)


class PrepDriver:
    def __init__(self, gen):
        self.gen = gen

    def drive(self, n):
        if self.gen is None:
            return
        for _ in range(n):
            try:
                next(self.gen)
            except StopIteration:
                self.gen = None
                return

    def finish(self):
        if self.gen is not None:
            for _ in self.gen:
                pass
            self.gen = None


def emit_l0_ffn(g, c, W, prep_evs, prep=None):
    def sev(key, chunk):
        want = chunk // 16
        while prep is not None and prep.gen is not None and \
                len(prep_evs.get(key, [])) <= want:
            prep.drive(1)
        lst = prep_evs.get(key, [])
        return lst[min(want, len(lst) - 1)] if lst else None
    es = ExitStack()
    w1b = [g.sb("l0f_w1", [128, CH * 128], BF16, es) for _ in range(2)]
    w2b = [g.sb("l0f_w2", [128, CH * 128], BF16, es) for _ in range(2)]
    w3b = [g.sb("l0f_w3", [128, L0HC * 128], BF16, es) for _ in range(2)]
    sg = g.sb("l0f_sg", [128, TAU], F32, es)
    psG = [g.ps("l0f_g", [128, TAU], es) for _ in range(2)]
    psU = [g.ps("l0f_u", [128, TAU], es) for _ in range(2)]
    psD = [g.ps("l0f_d", [128, TAU], es) for _ in range(2)]
    ev_evac_all = []
    ev_gmm = {}
    ev_sg = {}
    ev_h = {}
    ev_w = {}
    ev_dmm = {}
    ev_dev = {}
    step = 0
    dstep = 0
    for tau in range(NTAU):
        t0 = tau * TAU
        for i in range(L0HC):
            b = step % 2
            if prep is not None:
                prep.drive(1)
            ev_w[step] = [
                g.load(w1b[b][:], W["w1q"][i], waits=[sev("w1", i), ev_gmm.get(step - 2)]),
                g.load(w2b[b][:], W["w2q"][i], waits=[sev("w2", i), ev_gmm.get(step - 2)]),
            ]
            lastg = lastu = None
            for chn in range(CH):
                lastg = mm(g, psG[b][:], w1b[b][:, chn * 128:(chn + 1) * 128],
                           xs(c, chn, t0, TAU), chn == 0, chn == CH - 1,
                           waits=(ev_w[step] + [ev_sg.get(step - 2)] if chn == 0 else ()),
                           inc=(chn == CH - 1))
                lastu = mm(g, psU[b][:], w2b[b][:, chn * 128:(chn + 1) * 128],
                           xs(c, chn, t0, TAU), chn == 0, chn == CH - 1,
                           waits=([ev_h.get(step - 2)] if chn == 0 else ()),
                           inc=(chn == CH - 1))
            ev_gmm[step] = lastu
            ev_sg[step] = g.op("scalar",
                               lambda e, b_=b: e.activation(sg[:], psG[b_][:], AF.Sigmoid),
                               waits=[lastg, ev_h.get(step - 1)], inc=True)
            ev_h[step] = g.op(
                "vector",
                lambda e, b_=b, ii=i: e.tensor_mul(hs_(c, ii), sg[:], psU[b_][:]),
                waits=[lastu, ev_sg[step]], inc=True)
            step += 1
        ev_h_tau = [ev_h[step - 1 - k] for k in range(L0HC)]
        for m in range(CH):
            b = dstep % 2
            if prep is not None:
                prep.drive(1)
            e_w3 = g.load(w3b[b][:], W["w3q"][m],
                          waits=[sev("w3", m), ev_dmm.get(dstep - 2)])
            last = None
            for ic in range(L0HC):
                last = mm(g, psD[b][:], w3b[b][:, ic * 128:(ic + 1) * 128], hs_(c, ic),
                          ic == 0, ic == L0HC - 1,
                          waits=(([e_w3, ev_dev.get(dstep - 2)] + ev_h_tau)
                                 if ic == 0 else ()),
                          inc=(ic == L0HC - 1))
            ev_dmm[dstep] = last
            ev_dev[dstep] = g.op(
                "vector",
                lambda e, b_=b, m_=m, t=t0: e.tensor_add(
                    xs(c, m_, t, TAU), psD[b_][:], xs(c, m_, t, TAU)),
                waits=[last], inc=True)
            ev_evac_all.append(ev_dev[dstep])
            dstep += 1
    es.close()
    return ev_evac_all


def emit_l1_attn(g, c, W):
    es = ExitStack()
    bq = g.sb("l1_bq", [128, CH], F32, es)
    bk = g.sb("l1_bk", [128, CH], F32, es)
    ev_b1 = g.load(bq[:], W["bq"])
    ev_b2 = g.load(bk[:], W["bk"])
    wqkvb = [g.sb("l1_wqkv", [128, CH * 384], BF16, es) for _ in range(2)]
    bvr = [g.sb("l1_bvr", [1, 128], BF16, es) for _ in range(2)]
    q_h = g.sb("l1_qh", [128, T], BF16, es)
    k_h = g.sb("l1_kh", [128, T], BF16, es)
    v_h = g.sb("l1_vh", [128, T], BF16, es)
    pTl = [g.sb("l1_pT", [128, 1024], BF16, es) for _ in range(2)]
    rcb = g.sb("l1_rcb", [128, 1024], F32, es)

    def load_head_w(h, waits):
        b = h % 2
        e1 = g.loadc(wqkvb[b][:], W["wqkv"][h], waits=waits)
        e2 = g.loadc(bvr[b][:], W["bvr"][h:h + 1, :], waits=waits)
        return [e1, e2]

    ev_wh = {0: load_head_w(0, [c.ev_init]), 1: load_head_w(1, [c.ev_init])}
    ev_projdone = {}
    ev_head_attn_done = {}
    for h in range(NH):
        b = h % 2
        pes = ExitStack()
        psp = [g.ps("l1_pp", [128, TAU], pes) for _ in range(2)]
        first_evs = []
        ev_pev = {}
        pstep = 0
        prev_attn = ev_head_attn_done.get(h - 1, [])
        for which, (off, bias_t) in enumerate(((0, bq), (128, bk))):
            for tau in range(NTAU):
                pb = pstep % 2
                last = None
                for chn in range(CH):
                    w0 = []
                    if chn == 0:
                        w0 = ev_wh[h] + [ev_pev.get(pstep - 2), ev_b1, ev_b2] + \
                             (prev_attn if pstep == 0 else []) + \
                             ([ev_projdone[h - 2]] if h >= 2 else [])
                    last = mm(g, psp[pb][:],
                              wqkvb[b][:, chn * 384 + off: chn * 384 + off + 128],
                              xs(c, chn, tau * TAU, TAU), chn == 0, chn == CH - 1,
                              waits=w0, inc=(chn == CH - 1))
                dst = q_h if which == 0 else k_h
                ev_pev[pstep] = g.op(
                    "vector",
                    lambda e, tt=tau, ps_=psp[pb], d=dst, bt=bias_t, hh=h: e.tensor_scalar_add(
                        d[:, tt * TAU:(tt + 1) * TAU], ps_[:], bt[:, hh:hh + 1]),
                    waits=[last], inc=True)
                first_evs.append(ev_pev[pstep])
                pstep += 1
        for grp in range(4):
            pb = pstep % 2
            last = None
            for j in range(4):
                kc = grp * 4 + j
                for chn in range(CH):
                    last = mm(g, psp[pb][:, j * 128:(j + 1) * 128],
                              xs(c, chn, kc * 128, 128),
                              wqkvb[b][:, chn * 384 + 256: chn * 384 + 384],
                              chn == 0, False,
                              waits=([ev_pev.get(pstep - 2)] if (j == 0 and chn == 0) else ()),
                              inc=False)
                last = mm(g, psp[pb][:, j * 128:(j + 1) * 128], c.ones1[:], bvr[b][:],
                          False, True, inc=(j == 3))
            ev_pev[pstep] = g.op(
                "vector",
                lambda e, gg=grp, ps_=psp[pb]: e.tensor_copy(v_h[:, gg * TAU:(gg + 1) * TAU],
                                                             ps_[:]),
                waits=[last], inc=True)
            first_evs.append(ev_pev[pstep])
            pstep += 1
        ev_projdone[h] = ev_pev[pstep - 1]
        pes.close()
        if h + 2 < NH:
            ev_wh[h + 2] = load_head_w(h + 2, [ev_projdone[h]])
        aes = ExitStack()
        psS = [g.ps("l1_s", [128, 1024], aes) for _ in range(2)]
        psR = g.ps("l1_r", [128, 1024], aes)
        psA = g.ps("l1_a", [128, 1024], aes)
        ev_half = []
        for half in range(2):
            q0 = half * 1024
            ev_S = {}
            ev_exp = {}
            ev_Rmm = {}
            ev_Amm = {}
            for kc in range(CH):
                p2 = kc % 2
                last = None
                for n in range(2):
                    w0 = []
                    if n == 0:
                        w0 = [ev_exp.get(kc - 2)]
                        if kc == 0:
                            w0 += first_evs + ev_half + prev_attn
                    last = mm(g, psS[p2][:, n * 512:(n + 1) * 512],
                              k_h[:, kc * 128:(kc + 1) * 128],
                              q_h[:, q0 + n * 512: q0 + (n + 1) * 512],
                              True, True, waits=w0, inc=(n == 1))
                ev_S[kc] = last
                ev_exp[kc] = g.op(
                    "scalar",
                    lambda e, pt_=pTl[p2], ss_=psS[p2]: e.activation(
                        pt_[:], ss_[:], AF.Exp, bias=c.cpool[:, 0:1],
                        scale=c.cpool[:, 1:2]),
                    waits=[ev_S[kc], ev_Rmm.get(kc - 2), ev_Amm.get(kc - 2)], inc=True)
                last = None
                for n in range(2):
                    last = mm(g, psR[:, n * 512:(n + 1) * 512], c.allones[:],
                              pTl[p2][:, n * 512:(n + 1) * 512], kc == 0, kc == CH - 1,
                              waits=([ev_exp[kc]] + (ev_half if kc == 0 else []))
                              if n == 0 else (),
                              inc=(n == 1))
                ev_Rmm[kc] = last
                last = None
                for n in range(2):
                    last = mm(g, psA[:, n * 512:(n + 1) * 512],
                              v_h[:, kc * 128:(kc + 1) * 128],
                              pTl[p2][:, n * 512:(n + 1) * 512], kc == 0, kc == CH - 1,
                              waits=() if n == 1 else [ev_exp[kc]],
                              inc=(n == 1))
                ev_Amm[kc] = last
            e_rc = g.op("vector", lambda e, pr_=psR: e.reciprocal(rcb[:], pr_[:]),
                        waits=[ev_Rmm[CH - 1]], inc=True)
            e_ao = g.op(
                "vector",
                lambda e, hh=h, q0_=q0, pa_=psA: e.tensor_mul(
                    c.ao[:, hh * T + q0_: hh * T + q0_ + 1024], pa_[:], rcb[:]),
                waits=[ev_Amm[CH - 1], e_rc], inc=True)
            ev_half = [e_rc, e_ao, ev_exp[CH - 1], ev_exp[CH - 2]]
        ev_head_attn_done[h] = ev_half
        aes.close()
    es.close()




def emit_l1_ffn(g, c, W):
    es = ExitStack()
    gs = g.sb("l1f_gs", [128, L1HC], F32, es)
    us = g.sb("l1f_us", [128, L1HC], F32, es)
    ds = g.sb("l1f_ds", [128, CH], F32, es)
    e_s1 = g.load(gs[:], W["gs"])
    e_s2 = g.load(us[:], W["us"])
    e_s3 = g.load(ds[:], W["ds"])
    ev_scales = [e_s1, e_s2, e_s3]
    gwb = [g.sb("l1f_gw", [128, CH * 128], BF16, es) for _ in range(2)]
    uwb = [g.sb("l1f_uw", [128, CH * 128], BF16, es) for _ in range(2)]
    dwb = [g.sb("l1f_dw", [128, L1HC * 128], BF16, es) for _ in range(2)]
    rs_bf = g.sb("l1f_rs", [1, TAU], BF16, es)
    rsh_bf = g.sb("l1f_rsh", [1, TAU], BF16, es)
    sgt = g.sb("l1f_sg", [128, TAU], F32, es)
    psRS = g.ps("l1f_prs", [128, TAU], es)
    psG = [g.ps("l1f_pg", [128, TAU], es) for _ in range(2)]
    psU = [g.ps("l1f_pu", [128, TAU], es) for _ in range(2)]
    psD = [g.ps("l1f_pd", [128, TAU], es) for _ in range(2)]
    ev_evac_all = []
    ev_gmm = {}
    ev_sg = {}
    ev_h = {}
    ev_dmm = {}
    ev_dev = {}
    ev_rs_prev = None
    ev_rsh_prev = None
    step = 0
    dstep = 0
    for tau in range(NTAU):
        t0 = tau * TAU
        last = None
        for chn in range(CH):
            last = mm(g, psRS[:], c.allones[:], xs(c, chn, t0, TAU), chn == 0, chn == CH - 1,
                      waits=([c.ev_stream, c.ev_init, ev_rsh_prev] if chn == 0 else ()),
                      inc=(chn == CH - 1))
        ev_rs = g.op("vector", lambda e: e.tensor_copy(rs_bf[:], psRS[0:1, :]),
                     waits=[last, ev_rs_prev], inc=True)
        ev_rs_prev = ev_rs
        for i in range(L1HC):
            b = step % 2
            ev_w = [
                g.loadc(gwb[b][:], W["gw"][i], waits=[ev_gmm.get(step - 2)]),
                g.loadc(uwb[b][:], W["uw"][i], waits=[ev_gmm.get(step - 2)]),
            ]
            for chn in range(CH):
                mm(g, psG[b][:], gwb[b][:, chn * 128:(chn + 1) * 128],
                   xs(c, chn, t0, TAU), chn == 0, False,
                   waits=(ev_w + [ev_sg.get(step - 2), ev_rs] if chn == 0 else ()),
                   inc=False)
            lastg = mm(g, psG[b][:], c.mrow[:], rs_bf[:], False, True, inc=True)
            for chn in range(CH):
                mm(g, psU[b][:], uwb[b][:, chn * 128:(chn + 1) * 128],
                   xs(c, chn, t0, TAU), chn == 0, False,
                   waits=([ev_h.get(step - 2)] if chn == 0 else ()),
                   inc=False)
            lastu = mm(g, psU[b][:], c.mrow[:], rs_bf[:], False, True, inc=True)
            ev_gmm[step] = lastu
            ev_sg[step] = g.op(
                "scalar",
                lambda e, b_=b, ii=i: e.activation(sgt[:], psG[b_][:], AF.Silu,
                                                   scale=gs[:, ii:ii + 1]),
                waits=[lastg, ev_h.get(step - 1), ev_scales[0]], inc=True)
            ev_h[step] = g.op(
                "vector",
                lambda e, b_=b, ii=i: e.scalar_tensor_tensor(
                    hs_(c, ii), psU[b_][:], us[:, ii:ii + 1], sgt[:], ALU.mult, ALU.mult),
                waits=[lastu, ev_sg[step], ev_scales[1]], inc=True)
            step += 1
        ev_h_tau = [ev_h[step - 1 - k] for k in range(L1HC)]
        last = None
        for ic in range(L1HC):
            last = mm(g, psRS[:], c.allones[:], hs_(c, ic), ic == 0, ic == L1HC - 1,
                      waits=(([ev_rs] + ev_h_tau) if ic == 0 else ()),
                      inc=(ic == L1HC - 1))
        ev_rsh = g.op("vector", lambda e: e.tensor_copy(rsh_bf[:], psRS[0:1, :]),
                      waits=[last, ev_rsh_prev], inc=True)
        ev_rsh_prev = ev_rsh
        for m in range(CH):
            b = dstep % 2
            e_w3 = g.loadc(dwb[b][:], W["dw"][m], waits=[ev_dmm.get(dstep - 2)])
            last = None
            for ic in range(L1HC):
                last = mm(g, psD[b][:], dwb[b][:, ic * 128:(ic + 1) * 128], hs_(c, ic),
                          ic == 0, False,
                          waits=(([e_w3, ev_dev.get(dstep - 2), ev_rsh] + ev_h_tau)
                                 if ic == 0 else ()),
                          inc=False)
            last = mm(g, psD[b][:], c.mrow[:], rsh_bf[:], False, True, inc=True)
            ev_dmm[dstep] = last
            ev_dev[dstep] = g.op(
                "vector",
                lambda e, b_=b, m_=m, t=t0: e.scalar_tensor_tensor(
                    xs(c, m_, t, TAU), psD[b_][:], ds[:, m_:m_ + 1], xs(c, m_, t, TAU),
                    ALU.mult, ALU.add),
                waits=[last, ev_scales[2]], inc=True)
            ev_evac_all.append(ev_dev[dstep])
            dstep += 1
    es.close()
    return ev_evac_all


def _declare_l0a(nc, first):
    t = {}
    t["xin"] = nc.dram_tensor("xin", [D, T], F32 if first else BF16, kind="ExternalInput").ap()
    t["wq"] = nc.dram_tensor("wq", [NH, 128, CH * 128], F32, kind="ExternalInput").ap()
    t["wkv"] = nc.dram_tensor("wkv", [128, LATC * CH * 128], F32, kind="ExternalInput").ap()
    t["wk"] = nc.dram_tensor("wk", [NH, 128, LATC * 128], F32, kind="ExternalInput").ap()
    t["wv"] = nc.dram_tensor("wv", [NH, 128, LATC * 128], F32, kind="ExternalInput").ap()
    t["wo"] = nc.dram_tensor("wo", [CH, 128, CH * 128], F32, kind="ExternalInput").ap()
    t["masks"] = nc.dram_tensor("masks", [3, 128, 128], F32, kind="ExternalInput").ap()
    return t


def _declare_l0f(nc):
    t = {}
    t["w1T"] = nc.dram_tensor("w1T", [D, L0H], F32, kind="ExternalInput").ap()
    t["w2T"] = nc.dram_tensor("w2T", [D, L0H], F32, kind="ExternalInput").ap()
    t["w3T"] = nc.dram_tensor("w3T", [L0H, D], F32, kind="ExternalInput").ap()
    t["w1q"] = nc.dram_tensor("w1q", [L0HC, 128, CH * 128], BF16, kind="Internal").ap()
    t["w2q"] = nc.dram_tensor("w2q", [L0HC, 128, CH * 128], BF16, kind="Internal").ap()
    t["w3q"] = nc.dram_tensor("w3q", [CH, 128, L0HC * 128], BF16, kind="Internal").ap()
    return t


def build_l0_ffn_launch(first=False, last=False):
    nc = bass.Bass("TRN2", target_bir_lowering=False, debug=False)
    t = _declare_l0f(nc)
    xin = nc.dram_tensor("xin", [D, T], F32 if first else BF16, kind="ExternalInput").ap()
    xo = nc.dram_tensor("xo", [D, T], F32 if last else BF16, kind="ExternalOutput").ap()
    g = G(nc)
    c = setup_common(g)
    emit_load_stream(g, c, xin, first)
    prep_evs = {}
    prep_es = ExitStack()
    pd = PrepDriver(gen_quantize_prep(g, c, [
        ("w1", t["w1T"], t["w1q"], CH, L0H),
        ("w2", t["w2T"], t["w2q"], CH, L0H),
        ("w3", t["w3T"], t["w3q"], L0HC, D),
    ], prep_evs, prep_es))
    pd.finish()
    g.barrier()
    evs = emit_l0_ffn(g, c, t, prep_evs)
    ev_out = emit_ln(g, c, evs, final_out=(xo if last else None))
    if not last:
        ev_last = emit_store_stream(g, c, xo, waits=ev_out)
    else:
        ev_last = ev_out[-1]
    g.op("gpsimd", lambda e: e.engine_nop(), waits=[ev_last] + list(ev_out), inc=True)
    sems = {}
    sem_es = ExitStack()
    for s in ALL_SEMS:
        sems[s] = sem_es.enter_context(nc.semaphore(s))
    g.build(sems)
    g.es.close()
    sem_es.close()
    return nc


def _declare_l1a(nc):
    t = {}
    t["wqkv"] = nc.dram_tensor("wqkv", [NH, 128, CH * 384], F32, kind="ExternalInput").ap()
    t["bq"] = nc.dram_tensor("bq", [128, CH], F32, kind="ExternalInput").ap()
    t["bk"] = nc.dram_tensor("bk", [128, CH], F32, kind="ExternalInput").ap()
    t["bvr"] = nc.dram_tensor("bvr", [NH, 128], F32, kind="ExternalInput").ap()
    t["wout"] = nc.dram_tensor("wout", [CH, 128, CH * 128], F32, kind="ExternalInput").ap()
    t["bo"] = nc.dram_tensor("bo", [128, CH], F32, kind="ExternalInput").ap()
    return t


def build_l1_attn_launch(first=False, last=False):
    nc = bass.Bass("TRN2", target_bir_lowering=False, debug=False)
    t = _declare_l1a(nc)
    xin = nc.dram_tensor("xin", [D, T], F32 if first else BF16, kind="ExternalInput").ap()
    xo = nc.dram_tensor("xo", [D, T], F32 if last else BF16, kind="ExternalOutput").ap()
    g = G(nc)
    c = setup_common(g)
    bo = g.sb("l1_bo", [128, CH], F32)
    ev_bo = g.load(bo[:], t["bo"])
    emit_load_stream(g, c, xin, first)
    emit_l1_attn(g, c, t)
    g.barrier()
    ev_out = emit_outproj(g, c, t["wout"], bo, final_out=(xo if last else None))
    if not last:
        ev_last = emit_store_stream(g, c, xo, waits=ev_out)
    else:
        ev_last = ev_out[-1]
    g.op("gpsimd", lambda e: e.engine_nop(), waits=[ev_last] + list(ev_out), inc=True)
    sems = {}
    sem_es = ExitStack()
    for s in ALL_SEMS:
        sems[s] = sem_es.enter_context(nc.semaphore(s))
    g.build(sems)
    g.es.close()
    sem_es.close()
    return nc


def _declare_l1f(nc):
    t = {}
    t["gw"] = nc.dram_tensor("gw", [L1HC, 128, CH * 128], I32, kind="ExternalInput").ap()
    t["uw"] = nc.dram_tensor("uw", [L1HC, 128, CH * 128], I32, kind="ExternalInput").ap()
    t["dw"] = nc.dram_tensor("dw", [CH, 128, L1HC * 128], I32, kind="ExternalInput").ap()
    t["gs"] = nc.dram_tensor("gs", [128, L1HC], F32, kind="ExternalInput").ap()
    t["us"] = nc.dram_tensor("us", [128, L1HC], F32, kind="ExternalInput").ap()
    t["ds"] = nc.dram_tensor("ds", [128, CH], F32, kind="ExternalInput").ap()
    return t


def build_l1_ffn_launch(first=False, last=False):
    nc = bass.Bass("TRN2", target_bir_lowering=False, debug=False)
    t = _declare_l1f(nc)
    xin = nc.dram_tensor("xin", [D, T], F32 if first else BF16, kind="ExternalInput").ap()
    xo = nc.dram_tensor("xo", [D, T], F32 if last else BF16, kind="ExternalOutput").ap()
    g = G(nc)
    c = setup_common(g)
    emit_load_stream(g, c, xin, first)
    evs = emit_l1_ffn(g, c, t)
    ev_out = emit_ln(g, c, evs, final_out=(xo if last else None))
    if not last:
        ev_last = emit_store_stream(g, c, xo, waits=ev_out)
    else:
        ev_last = ev_out[-1]
    g.op("gpsimd", lambda e: e.engine_nop(), waits=[ev_last] + list(ev_out), inc=True)
    sems = {}
    sem_es = ExitStack()
    for s in ALL_SEMS:
        sems[s] = sem_es.enter_context(nc.semaphore(s))
    g.build(sems)
    g.es.close()
    sem_es.close()
    return nc


def build_fused_launch():
    nc = bass.Bass("TRN2", target_bir_lowering=False, debug=False)
    t0a = _declare_l0a(nc, True)
    t0f = _declare_l0f(nc)
    t1a = _declare_l1a(nc)
    t1f = _declare_l1f(nc)
    xo = nc.dram_tensor("xo", [D, T], F32, kind="ExternalOutput").ap()
    g = G(nc)
    c = setup_common(g, t0a["masks"])
    bo = g.sb("l1_bo", [128, CH], F32)
    g.load(bo[:], t1a["bo"])
    emit_load_stream(g, c, t0a["xin"], True)
    prep_evs = {}
    prep_es = ExitStack()
    prep = PrepDriver(gen_quantize_prep(g, c, [
        ("w1", t0f["w1T"], t0f["w1q"], CH, L0H),
        ("w2", t0f["w2T"], t0f["w2q"], CH, L0H),
        ("w3", t0f["w3T"], t0f["w3q"], L0HC, D),
    ], prep_evs, prep_es))
    prep.drive(1)
    emit_l0_attn(g, c, t0a, prep=prep)
    prep.drive(3)
    g.barrier()
    emit_outproj(g, c, t0a["wo"], None, prep=prep)
    g.barrier()
    evs = emit_l0_ffn(g, c, t0f, prep_evs, prep=prep)
    prep.finish()
    ev_out = emit_ln(g, c, evs)
    prep_es.close()
    g.barrier()
    emit_l1_attn(g, c, t1a)
    g.barrier()
    emit_outproj(g, c, t1a["wout"], bo)
    g.barrier()
    evs = emit_l1_ffn(g, c, t1f)
    ev_out = emit_ln(g, c, evs, final_out=xo)
    g.op("gpsimd", lambda e: e.engine_nop(), waits=list(ev_out), inc=True)
    sems = {}
    sem_es = ExitStack()
    for s in ALL_SEMS:
        sems[s] = sem_es.enter_context(nc.semaphore(s))
    g.build(sems)
    g.es.close()
    sem_es.close()
    return nc


def build_l0_attn_launch(first=True, last=False, debug=False):
    nc = bass.Bass("TRN2", target_bir_lowering=False, debug=False)
    t = _declare_l0a(nc, first)
    xo = nc.dram_tensor("xo", [D, T], F32 if last else BF16, kind="ExternalOutput").ap()
    if debug:
        dbg_x0 = nc.dram_tensor("dbg_x0", [D, T], BF16, kind="ExternalOutput").ap()
        dbg_ao = nc.dram_tensor("dbg_ao", [D, T], BF16, kind="ExternalOutput").ap()
        dbg_kv = nc.dram_tensor("dbg_kv", [LAT, T], BF16, kind="ExternalOutput").ap()
    g = G(nc)
    c = setup_common(g, t["masks"])
    emit_load_stream(g, c, t["xin"], first)
    if debug:
        for i in range(CH):
            g.store(dbg_x0[i * 128:(i + 1) * 128, :], xs(c, i, 0, T), waits=[c.ev_stream])
    emit_l0_attn(g, c, t, dbg_kv if debug else None)
    g.barrier()
    if debug:
        for i in range(CH):
            g.store(dbg_ao[i * 128:(i + 1) * 128, :], aos(c, i, 0, T))
        g.barrier()
    ev_out = emit_outproj(g, c, t["wo"], None, final_out=(xo if last else None))
    if not last:
        ev_last = emit_store_stream(g, c, xo, waits=ev_out)
    else:
        ev_last = ev_out[-1]
    g.op("gpsimd", lambda e: e.engine_nop(), waits=[ev_last] + list(ev_out), inc=True)
    sems = {}
    sem_es = ExitStack()
    for s in ALL_SEMS:
        sems[s] = sem_es.enter_context(nc.semaphore(s))
    g.build(sems)
    g.es.close()
    sem_es.close()
    return nc


def _pack_panels(w):
    K, M = w.shape
    return np.ascontiguousarray(
        w.reshape(K // 128, 128, M // 128, 128).transpose(2, 1, 0, 3).reshape(M // 128, 128, K))


def _make_masks():
    i = np.arange(128)
    m_diag = (i[:, None] <= i[None, :]).astype(np.float32)
    m_prev = (i[:, None] > i[None, :]).astype(np.float32)
    m_prev0 = m_prev.copy()
    m_prev0[0:16, :] = 1.0
    return np.stack([m_diag, m_prev, m_prev0])


def _l1a_packs(inputs):
    wqkv = inputs["L1_attn_w_qkv"]
    wt = np.ascontiguousarray(wqkv.T)
    per_head = []
    for h in range(NH):
        sl = np.concatenate([wt[:, h * 128:(h + 1) * 128],
                             wt[:, 2048 + h * 128: 2048 + (h + 1) * 128],
                             wt[:, 4096 + h * 128: 4096 + (h + 1) * 128]], axis=1)
        per_head.append(sl.reshape(CH, 128, 384).transpose(1, 0, 2).reshape(128, CH * 384))
    b = inputs["L1_attn_b_qkv"]
    return {
        "wqkv": np.ascontiguousarray(np.stack(per_head)),
        "bq": np.ascontiguousarray(b[0:2048].reshape(CH, 128).T),
        "bk": np.ascontiguousarray(b[2048:4096].reshape(CH, 128).T),
        "bvr": np.ascontiguousarray(b[4096:6144].reshape(NH, 128)),
        "wout": _pack_panels(np.ascontiguousarray(inputs["L1_attn_w_out"].T)),
        "bo": np.ascontiguousarray(inputs["L1_attn_b_out"].reshape(CH, 128).T),
    }


def _l1f_packs(inputs):
    return {
        "gw": _pack_panels(np.ascontiguousarray(inputs["L1_ffn_gate_w"].T)),
        "uw": _pack_panels(np.ascontiguousarray(inputs["L1_ffn_up_w"].T)),
        "dw": _pack_panels(np.ascontiguousarray(inputs["L1_ffn_down_w"].T)),
        "gs": np.ascontiguousarray(inputs["L1_ffn_gate_scale"].reshape(L1HC, 128).T),
        "us": np.ascontiguousarray(inputs["L1_ffn_up_scale"].reshape(L1HC, 128).T),
        "ds": np.ascontiguousarray(inputs["L1_ffn_down_scale"].reshape(CH, 128).T),
    }


def _l0a_packs(inputs):
    wkv = _pack_panels(inputs["L0_attn_Wkv"])
    return {
        "wq": _pack_panels(inputs["L0_attn_Wq"]),
        "wkv": np.ascontiguousarray(wkv.transpose(1, 0, 2).reshape(128, LATC * CH * 128)),
        "wk": _pack_panels(inputs["L0_attn_Wk"]),
        "wv": _pack_panels(inputs["L0_attn_Wv"]),
        "wo": _pack_panels(inputs["L0_attn_Wo"]),
        "masks": _make_masks(),
    }


_FUSED = True


def _all_packs(inputs):
    p = {}
    p.update(_l0a_packs(inputs))
    p.update({
        "w1T": np.ascontiguousarray(inputs["L0_ffn_w1"].T),
        "w2T": np.ascontiguousarray(inputs["L0_ffn_w2"].T),
        "w3T": np.ascontiguousarray(inputs["L0_ffn_w3"].T),
    })
    p.update(_l1a_packs(inputs))
    p.update(_l1f_packs(inputs))
    return p


def kernel(**inputs):
    x = inputs["L0_attn_x"]
    B = x.shape[0]
    cur = [np.ascontiguousarray(x[b].T) for b in range(B)]
    if _FUSED:
        packs = _all_packs(inputs)
        nc = build_fused_launch()
        in_maps = [dict(packs, xin=cur[b]) for b in range(B)]
        res = run_bass_kernel_spmd(nc, in_maps, core_ids=list(range(B)))
        cur = [np.asarray(res.results[b]["xo"]) for b in range(B)]
    else:
        l0a = _l0a_packs(inputs)
        l0f = {
            "w1T": np.ascontiguousarray(inputs["L0_ffn_w1"].T),
            "w2T": np.ascontiguousarray(inputs["L0_ffn_w2"].T),
            "w3T": np.ascontiguousarray(inputs["L0_ffn_w3"].T),
        }
        l1a = _l1a_packs(inputs)
        l1f = _l1f_packs(inputs)
        launches = [
            ("l0a", build_l0_attn_launch, l0a),
            ("l0f", build_l0_ffn_launch, l0f),
            ("l1a", build_l1_attn_launch, l1a),
            ("l1f", build_l1_ffn_launch, l1f),
        ]
        for li, (name, builder, packs) in enumerate(launches):
            nc = builder(first=(li == 0), last=(li == len(launches) - 1))
            in_maps = [dict(packs, xin=cur[b]) for b in range(B)]
            res = run_bass_kernel_spmd(nc, in_maps, core_ids=list(range(B)))
            cur = [np.asarray(res.results[b]["xo"]) for b in range(B)]
    return np.stack([cur[b].astype(np.float32).T for b in range(B)])


# version 18
# speedup vs baseline: 1.4243x; 1.0307x over previous
import numpy as np
import concourse.bass as bass
import concourse.mybir as mybir
from concourse.bass_utils import run_bass_kernel_spmd
from contextlib import ExitStack

F32 = mybir.dt.float32
BF16 = mybir.dt.bfloat16
I32 = mybir.dt.int32
AF = mybir.ActivationFunctionType
ALU = mybir.AluOpType

D = 2048
T = 2048
CH = 16
NH = 16
HD = 128
TAU = 512
NTAU = T // TAU
LAT = 512
LATC = LAT // 128
L0H = 5632
L0HC = L0H // 128
L1H = 8192
L1HC = L1H // 128
SM_SCALE = float(1.0 / np.sqrt(128.0))
SM_SHIFT = -12.0
LN_EPS = 1e-5
Q_EPS = 1e-8
INV_D = 1.0 / D

ENGS = ("sync", "tensor", "scalar", "vector", "gpsimd")
ESEM = {"tensor": "pe", "scalar": "act", "vector": "dve", "gpsimd": "pool"}
QROT = {"sync": ["qs%d" % i for i in range(4)],
        "gpsimd": ["qp%d" % i for i in range(6)],
        "scalar": ["qa%d" % i for i in range(2)]}
ALL_SEMS = ("pe", "act", "dve", "pool") + tuple(s for v in QROT.values() for s in v)


class G:

    def __init__(self, nc):
        self.nc = nc
        self.ops = []
        self.cnt = {s: 0 for s in ALL_SEMS}
        self.qrr = {}
        self.snapshots = {}
        self.barv = 0
        self.es = ExitStack()
        self.uid = 0

    def _nm(self, name):
        self.uid += 1
        return "%s_u%d" % (name, self.uid)

    def sb(self, name, shape, dtype, es=None):
        return (es or self.es).enter_context(self.nc.sbuf_tensor(self._nm(name), shape, dtype))

    def ps(self, name, shape, es=None, dtype=F32):
        return (es or self.es).enter_context(self.nc.psum_tensor(self._nm(name), shape, dtype))

    def op(self, eng, fn, waits=(), inc=False):
        w = [x for x in waits if x is not None]
        ev = None
        upd = None
        if inc:
            sem = ESEM[eng]
            self.cnt[sem] += 1
            ev = (sem, self.cnt[sem])
            upd = (sem, 1)
        self.ops.append((eng, fn, w, upd, self.barv))
        return ev

    def dma(self, eng, out, in_, waits=()):
        rot = QROT[eng]
        q = rot[self.qrr.setdefault(eng, 0) % len(rot)]
        self.qrr[eng] += 1
        self.cnt[q] += 16
        ev = (q, self.cnt[q])
        def fn(e, o=out, i=in_):
            return e.dma_start(out=o, in_=i)
        self.ops.append((eng, fn, [x for x in waits if x is not None], (q, 16), self.barv))
        return ev

    def load(self, out, in_, waits=()):
        return self.dma("sync", out, in_, waits)

    def loadc(self, out, in_, waits=()):
        return self.dma("gpsimd", out, in_, waits)

    def store(self, out, in_, waits=()):
        return self.dma("scalar", out, in_, waits)

    def barrier(self):
        self.barv += 1
        self.snapshots[self.barv] = dict(self.cnt)

    def build(self, sems):
        ops = self.ops
        snapshots = self.snapshots

        def mk(ename):
            def f(eng):
                waited = {}
                seen_bar = [0]
                own = ESEM.get(ename)

                def do_wait(s, v):
                    if v <= 0 or waited.get(s, 0) >= v:
                        return
                    eng.wait_ge(sems[s], v)
                    waited[s] = v

                for (e, fn, waits, upd, barv) in ops:
                    if e != ename:
                        continue
                    if barv > seen_bar[0]:
                        for s, v in snapshots[barv].items():
                            if s != own:
                                do_wait(s, v)
                        seen_bar[0] = barv
                    for (s, v) in waits:
                        do_wait(s, v)
                    ins = fn(eng)
                    if upd is not None:
                        ins.then_inc(sems[upd[0]], upd[1])
            return f

        with self.nc.Block() as block:
            for ename in ENGS:
                getattr(block, ename)(mk(ename))


def mm(g, out_ap, lhsT_ap, rhs_ap, start, stop, waits=(), inc=False):
    return g.op(
        "tensor",
        lambda e, o=out_ap, l=lhsT_ap, r=rhs_ap, s1=start, s2=stop:
            e.matmul(o, l, r, start=s1, stop=s2),
        waits, inc)


class Ctx:
    pass


def setup_common(g, masks_dram=None, ident_dram=None):
    c = Ctx()
    c.x = g.sb("x_sb", [128, CH * T], BF16)
    c.ao = g.sb("ao_sb", [128, CH * T], BF16)
    c.allones = g.sb("allones", [128, 128], BF16)
    c.ones1 = g.sb("ones1", [1, 128], BF16)
    c.mrow = g.sb("mrow", [1, 128], BF16)
    c.cpool = g.sb("cpool", [128, 8], F32)
    g.op("vector", lambda e: e.memset(c.allones[:], 1.0), inc=True)
    g.op("vector", lambda e: e.memset(c.ones1[:], 1.0), inc=True)
    g.op("vector", lambda e: e.memset(c.mrow[:], -128.0), inc=True)
    g.op("vector", lambda e: e.memset(c.cpool[:, 0:1], SM_SHIFT), inc=True)
    g.op("vector", lambda e: e.memset(c.cpool[:, 1:2], SM_SCALE), inc=True)
    g.op("vector", lambda e: e.memset(c.cpool[:, 2:3], LN_EPS), inc=True)
    c.ev_init = g.op("vector", lambda e: e.memset(c.cpool[:, 3:4], Q_EPS), inc=True)
    c.ev_masks = None
    if masks_dram is not None:
        c.masks = g.sb("masks", [128, 3 * 128], BF16)
        for i in range(3):
            c.ev_masks = g.loadc(c.masks[:, i * 128:(i + 1) * 128], masks_dram[i])
    if ident_dram is not None:
        c.ident = g.sb("ident", [128, 128], BF16)
        c.ev_masks = g.loadc(c.ident[:], ident_dram)
    return c


def xs(c, ch, t0, n):
    return c.x[:, ch * T + t0: ch * T + t0 + n]


def aos(c, ch, t0, n):
    return c.ao[:, ch * T + t0: ch * T + t0 + n]


def emit_load_stream(g, c, xin, fp32_src):
    ev = None
    if fp32_src:
        for ch in range(CH):
            ev = g.loadc(xs(c, ch, 0, T), xin[ch * 128:(ch + 1) * 128, :])
    else:
        for i in range(4):
            dst = c.x[:, i * 4 * T:(i + 1) * 4 * T].rearrange("p (c t) -> p c t", t=T)
            s = xin.rearrange("(c p) t -> c p t", p=128)[i * 4:(i + 1) * 4]
            ev = g.load(dst, s.rearrange("c p t -> p c t"))
    c.ev_stream = ev
    return ev


def emit_store_stream(g, c, xout, waits):
    ev = None
    for i in range(4):
        s = c.x[:, i * 4 * T:(i + 1) * 4 * T].rearrange("p (c t) -> p c t", t=T)
        d = xout.rearrange("(c p) t -> c p t", p=128)[i * 4:(i + 1) * 4]
        ev = g.load(d.rearrange("c p t -> p c t"), s, waits=waits)
    return ev


def emit_ln(g, c, ev_r_ready, final_out=None):
    es = ExitStack()
    psum_s = g.ps("ln_sum", [128, TAU], es)
    psum_q = g.ps("ln_sq", [128, TAU], es)
    r2 = [g.sb("ln_r2", [128, TAU], BF16, es) for _ in range(2)]
    m_bc = g.sb("ln_m", [128, TAU], F32, es)
    tmp = g.sb("ln_t", [128, TAU], F32, es)
    rstd = g.sb("ln_rs", [128, TAU], F32, es)
    t1 = g.sb("ln_t1", [128, TAU], F32, es)
    if final_out is not None:
        outf = [g.sb("ln_of", [128, TAU], F32, es) for _ in range(2)]
    ev_all = []
    prev_stats_consumed = []
    prev_dma = {}
    for tau in range(NTAU):
        t0 = tau * TAU
        ev_sq = [None] * CH
        ev_mmq = [None] * CH
        last_sum = None
        for chn in range(CH):
            ev_sq[chn] = g.op(
                "scalar",
                lambda e, ch=chn, t=t0: e.activation(r2[ch % 2][:], xs(c, ch, t, TAU), AF.Square),
                waits=list(ev_r_ready) + ([ev_mmq[chn - 2]] if chn >= 2 else []),
                inc=True)
            last_sum = mm(g, psum_s[:], c.allones[:], xs(c, chn, t0, TAU),
                          chn == 0, chn == CH - 1,
                          waits=(list(ev_r_ready) + prev_stats_consumed) if chn == 0 else (),
                          inc=(chn == CH - 1))
            ev_mmq[chn] = mm(g, psum_q[:], c.allones[:], r2[chn % 2][:],
                             chn == 0, chn == CH - 1,
                             waits=[ev_sq[chn]] + (prev_stats_consumed if chn == 0 else []),
                             inc=True)
        last_sq = ev_mmq[CH - 1]
        e1 = g.op("vector",
                  lambda e: e.tensor_scalar(m_bc[:], psum_s[:], INV_D, None, ALU.mult),
                  waits=[last_sum], inc=True)
        e2 = g.op("vector", lambda e: e.tensor_mul(tmp[:], m_bc[:], m_bc[:]),
                  waits=[e1], inc=True)
        e3 = g.op("vector",
                  lambda e: e.scalar_tensor_tensor(tmp[:], psum_q[:], INV_D, tmp[:],
                                                   ALU.mult, ALU.subtract),
                  waits=[last_sq, e2], inc=True)
        e4 = g.op("scalar", lambda e: e.activation(tmp[:], tmp[:], AF.Sqrt,
                                                   bias=c.cpool[:, 2:3]),
                  waits=[e3], inc=True)
        e5 = g.op("vector", lambda e: e.reciprocal(rstd[:], tmp[:]), waits=[e4], inc=True)
        prev_stats_consumed = [e1, e3]
        for chn in range(CH):
            ea = g.op("vector",
                      lambda e, ch=chn, t=t0: e.tensor_sub(t1[:], xs(c, ch, t, TAU), m_bc[:]),
                      waits=[e5, last_sq, last_sum, ev_sq[CH - 1]], inc=True)
            if final_out is None:
                eb = g.op("vector",
                          lambda e, ch=chn, t=t0: e.tensor_mul(xs(c, ch, t, TAU), t1[:], rstd[:]),
                          waits=[ea], inc=True)
                ev_all.append(eb)
            else:
                eb = g.op("vector",
                          lambda e, ch=chn: e.tensor_mul(outf[ch % 2][:], t1[:], rstd[:]),
                          waits=[ea, prev_dma.get(chn - 2)], inc=True)
                prev_dma[chn] = g.store(final_out[chn * 128:(chn + 1) * 128, t0:t0 + TAU],
                                        outf[chn % 2][:], waits=[eb])
                ev_all.append(prev_dma[chn])
        prev_dma = {chn - CH: v for chn, v in prev_dma.items()} if final_out is not None else {}
    es.close()
    return ev_all


def emit_outproj(g, c, w_dram, bias_col, final_out=None, prep=None):
    es = ExitStack()
    wom = [g.sb("wo_m", [128, CH * 128], BF16, es) for _ in range(2)]
    psY = [g.ps("ps_y", [128, T], es) for _ in range(2)]
    ev_w = {}
    ev_mm = {}
    ev_evac = {}
    for m in range(2):
        ev_w[m] = g.loadc(wom[m][:], w_dram[m])
    for m in range(CH):
        e_last = None
        for chn in range(CH):
            for tau in range(NTAU):
                e_last = mm(g, psY[m % 2][:, tau * TAU:(tau + 1) * TAU],
                            wom[m % 2][:, chn * 128:(chn + 1) * 128],
                            aos(c, chn, tau * TAU, TAU),
                            chn == 0, chn == CH - 1,
                            waits=([ev_w[m], ev_evac.get(m - 2)]
                                   if (chn == 0 and tau == 0) else ()),
                            inc=(chn == CH - 1 and tau == NTAU - 1))
        ev_mm[m] = e_last
        if m + 2 < CH:
            ev_w[m + 2] = g.loadc(wom[m % 2][:], w_dram[m + 2], waits=[ev_mm[m]])
        eev = None
        for tau in range(NTAU):
            if bias_col is None:
                eev = g.op("vector",
                           lambda e, mm_=m, tt=tau: e.tensor_add(
                               xs(c, mm_, tt * TAU, TAU),
                               psY[mm_ % 2][:, tt * TAU:(tt + 1) * TAU],
                               xs(c, mm_, tt * TAU, TAU)),
                           waits=[ev_mm[m]], inc=True)
            else:
                eev = g.op("vector",
                           lambda e, mm_=m, tt=tau: e.scalar_tensor_tensor(
                               xs(c, mm_, tt * TAU, TAU),
                               psY[mm_ % 2][:, tt * TAU:(tt + 1) * TAU],
                               bias_col[:, mm_:mm_ + 1], xs(c, mm_, tt * TAU, TAU),
                               ALU.add, ALU.add),
                           waits=[ev_mm[m]], inc=True)
        ev_evac[m] = eev
        if prep is not None:
            prep.drive(6)
    es.close()
    return emit_ln(g, c, [ev_evac[m] for m in range(CH)], final_out)


def emit_l0_attn(g, c, W, dbg_kv=None, prep=None):
    es = ExitStack()
    kv = g.sb("kv_sb", [128, LATC * T], BF16, es)
    kv_es = ExitStack()
    wkvb = g.sb("wkvb", [128, LATC * CH * 128], BF16, kv_es)
    pskv = [g.ps("ps_kv", [128, TAU], kv_es) for _ in range(2)]
    ev_wkv = g.loadc(wkvb[:], W["wkv"], waits=[c.ev_init])
    ev_kv_ev = {}
    ev_kv_mm_last = None
    step = 0
    for lc in range(LATC):
        for tau in range(NTAU):
            b = step % 2
            last = None
            for chn in range(CH):
                last = mm(g, pskv[b][:],
                          wkvb[:, (lc * CH + chn) * 128:(lc * CH + chn + 1) * 128],
                          xs(c, chn, tau * TAU, TAU), chn == 0, chn == CH - 1,
                          waits=([ev_wkv, c.ev_stream, ev_kv_ev.get(step - 2)]
                                 if chn == 0 else ()),
                          inc=(chn == CH - 1))
            ev_kv_mm_last = last
            ev_kv_ev[step] = g.op(
                "scalar",
                lambda e, l=lc, tt=tau, bb=b: e.activation(
                    kv[:, l * T + tt * TAU: l * T + (tt + 1) * TAU], pskv[bb][:], AF.Copy),
                waits=[last], inc=True)
            step += 1
    ev_kv_done = ev_kv_ev[step - 1]
    if dbg_kv is not None:
        for i in range(LATC):
            g.store(dbg_kv[i * 128:(i + 1) * 128, :], kv[:, i * T:(i + 1) * T],
                    waits=[ev_kv_done])
    kv_es.close()

    wqh = [g.sb("wq_h", [128, CH * 128], BF16, es) for _ in range(2)]
    wkh = [g.sb("wk_h", [128, LATC * 128], BF16, es) for _ in range(2)]
    wvh = [g.sb("wv_h", [128, LATC * 128], BF16, es) for _ in range(2)]
    q_h = g.sb("q_h", [128, T], BF16, es)
    k_h = g.sb("k_h", [128, T], BF16, es)
    v_h = g.sb("v_h", [128, T], BF16, es)
    pT = [g.sb("pT", [128, 384], BF16, es) for _ in range(4)]
    rcp = g.sb("rcp", [128, 128], F32, es)
    psp = [g.ps("ps_p", [128, TAU], es) for _ in range(2)]
    psS = [g.ps("ps_s", [128, 384], es) for _ in range(4)]
    psR = [g.ps("ps_r", [128, 128], es)[:]]
    psA = [g.ps("ps_a", [128, 128], es)[:]]

    def load_head_w(h, waits):
        b = h % 2
        e1 = g.loadc(wqh[b][:], W["wq"][h], waits=waits)
        e2 = g.loadc(wkh[b][:], W["wk"][h], waits=waits)
        e3 = g.loadc(wvh[b][:], W["wv"][h], waits=waits)
        return [e1, e2, e3]

    ev_wh = {0: load_head_w(0, [ev_kv_mm_last]), 1: load_head_w(1, [ev_kv_mm_last])}
    ev_pev = {}
    ev_projdone = {}
    ev_exp = {}
    ev_R = {}
    ev_AV = {}
    ev_rcp = {}
    ev_ao = {}
    gstep = 0
    for h in range(NH):
        b = h % 2
        first_evs = []
        for tau in range(NTAU):
            pb = gstep % 2
            last = None
            for chn in range(CH):
                w0 = []
                if chn == 0:
                    w0 = ev_wh[h] + [ev_pev.get(gstep - 2)]
                    if h == 0 and tau == 0:
                        w0.append(ev_kv_done)
                last = mm(g, psp[pb][:], wqh[b][:, chn * 128:(chn + 1) * 128],
                          xs(c, chn, tau * TAU, TAU), chn == 0, chn == CH - 1,
                          waits=w0, inc=(chn == CH - 1))
            ev_pev[gstep] = g.op(
                "scalar",
                lambda e, tt=tau, bb=pb: e.activation(q_h[:, tt * TAU:(tt + 1) * TAU],
                                                      psp[bb][:], AF.Copy),
                waits=[last], inc=True)
            first_evs.append(ev_pev[gstep])
            gstep += 1
        for tau in range(NTAU):
            pb = gstep % 2
            last = None
            for lc in range(LATC):
                last = mm(g, psp[pb][:], wkh[b][:, lc * 128:(lc + 1) * 128],
                          kv[:, lc * T + tau * TAU: lc * T + (tau + 1) * TAU],
                          lc == 0, lc == LATC - 1,
                          waits=([ev_pev.get(gstep - 2)] if lc == 0 else ()),
                          inc=(lc == LATC - 1))
            ev_pev[gstep] = g.op(
                "scalar",
                lambda e, tt=tau, bb=pb: e.activation(k_h[:, tt * TAU:(tt + 1) * TAU],
                                                      psp[bb][:], AF.Copy),
                waits=[last], inc=True)
            first_evs.append(ev_pev[gstep])
            gstep += 1
        for grp in range(4):
            pb = gstep % 2
            last = None
            for j in range(4):
                kc = grp * 4 + j
                for lc in range(LATC):
                    last = mm(g, psp[pb][:, j * 128:(j + 1) * 128],
                              kv[:, lc * T + kc * 128: lc * T + (kc + 1) * 128],
                              wvh[b][:, lc * 128:(lc + 1) * 128],
                              lc == 0, lc == LATC - 1,
                              waits=([ev_pev.get(gstep - 2)] if (j == 0 and lc == 0) else ()),
                              inc=(j == 3 and lc == LATC - 1))
            ev_pev[gstep] = g.op(
                "vector",
                lambda e, gg=grp, bb=pb: e.tensor_copy(v_h[:, gg * TAU:(gg + 1) * TAU],
                                                       psp[bb][:]),
                waits=[last], inc=True)
            first_evs.append(ev_pev[gstep])
            gstep += 1
        ev_projdone[h] = ev_pev[gstep - 1]
        if h + 2 < NH:
            ev_wh[h + 2] = load_head_w(h + 2, [ev_projdone[h]])
        if prep is not None:
            prep.drive(17)

        for q in range(CH):
            k_ = h * CH + q
            p2 = k_ % 4
            qsl = q_h[:, q * 128:(q + 1) * 128]
            tiles = [(psS[p2][:, 0:128], k_h[:, q * 128:(q + 1) * 128], 0,
                      (128, 0, 128), c.allones[:], v_h[:, q * 128:(q + 1) * 128])]
            if q >= 1:
                tiles.append((psS[p2][:, 128:256], k_h[:, (q - 1) * 128:q * 128],
                              2 if q == 1 else 1,
                              (128, 128, 256), c.allones[:], v_h[:, (q - 1) * 128:q * 128]))
            if q >= 2:
                tiles.append((psS[p2][0:16, 256:384], k_h[:, 0:16], None,
                              (16, 256, 384), c.allones[0:16, :], v_h[0:16, 0:128]))
            n_t = len(tiles)
            ev_S = None
            for i, (pslice, lh, mi, _, _, _) in enumerate(tiles):
                w0 = []
                if i == 0:
                    w0 = ([ev_exp.get(k_ - 4)] + (first_evs if q == 0 else []))
                ev_S = mm(g, pslice, lh, qsl, True, mi is None, waits=w0,
                          inc=(mi is None and i == n_t - 1))
                if mi is not None:
                    ev_S = mm(g, pslice, c.masks[:, mi * 128:(mi + 1) * 128], c.ident[:],
                              False, True,
                              waits=([c.ev_masks] if k_ == 0 else ()),
                              inc=(i == n_t - 1))
            war_pT = [ev_AV.get(k_ - 4), ev_R.get(k_ - 4)]
            nse = 128 if q == 0 else 256
            e_exp = g.op(
                "scalar",
                lambda e, p=p2, n=nse: e.activation(pT[p][:, 0:n], psS[p][:, 0:n], AF.Exp,
                                                    bias=c.cpool[:, 0:1],
                                                    scale=c.cpool[:, 1:2]),
                waits=[ev_S] + war_pT, inc=True)
            if q >= 2:
                e_exp = g.op(
                    "scalar",
                    lambda e, p=p2: e.activation(pT[p][0:16, 256:384], psS[p][0:16, 256:384],
                                                 AF.Exp, bias=c.cpool[0:16, 0:1],
                                                 scale=c.cpool[0:16, 1:2]),
                    inc=True)
            ev_exp[k_] = e_exp
            ev_r = None
            for i, (_, _, _, (np_, c0, c1), ao_lh, _) in enumerate(tiles):
                pts = pT[p2][0:np_, c0:c1]
                ev_r = mm(g, psR[0], ao_lh, pts, i == 0, i == n_t - 1,
                          waits=([ev_exp[k_], ev_rcp.get(k_ - 1)] if i == 0 else ()),
                          inc=(i == n_t - 1))
            ev_R[k_] = ev_r
            ev_a = None
            for i, (_, _, _, (np_, c0, c1), _, v_lh) in enumerate(tiles):
                pts = pT[p2][0:np_, c0:c1]
                ev_a = mm(g, psA[0], v_lh, pts, i == 0, i == n_t - 1,
                          waits=([ev_exp[k_], ev_ao.get(k_ - 1)] if i == 0 else ()),
                          inc=(i == n_t - 1))
            ev_AV[k_] = ev_a
            ev_rcp[k_] = g.op("vector", lambda e, pr_=psR[0]: e.reciprocal(rcp[:], pr_),
                              waits=[ev_R[k_]], inc=True)
            ev_ao[k_] = g.op(
                "vector",
                lambda e, pa_=psA[0], hh=h, qq=q: e.tensor_mul(
                    c.ao[:, hh * T + qq * 128: hh * T + (qq + 1) * 128], pa_, rcp[:]),
                waits=[ev_AV[k_], ev_rcp[k_]], inc=True)
    es.close()




def hs_(c, i, n=TAU):
    return c.ao[:, i * TAU:(i) * TAU + n]


def gen_quantize_prep(g, c, specs, out_evs, pes=None, ctl=None):
    SW = 2048
    stg = [g.sb("qp_stg", [128, SW], F32, pes) for _ in range(2)]
    nb = g.sb("qp_n", [128, SW], BF16, pes)
    qb = [g.sb("qp_q", [128, SW], BF16, pes) for _ in range(2)]
    onescol = g.sb("qp_onc", [128, 1], F32, pes)
    onesrow = g.sb("qp_onr", [1, 128], F32, pes)
    tot = g.sb("qp_tot", [1, 1], F32, pes)
    state = {}
    for key, w_dram, scr_dram, n_rc, n_cols in specs:
        nslab = [(j, min(SW, n_cols - j)) for j in range(0, n_cols, SW)]
        slabs = [(rc, j0, w) for rc in range(n_rc) for (j0, w) in nslab]
        partials = g.sb("qp_part", [128, len(slabs)], F32, pes)
        thr = g.sb("qp_th", [128, 1], F32, pes)
        nthr = g.sb("qp_nth", [128, 1], F32, pes)
        gg = g.sb("qp_gg", [128, 1], F32, pes)
        ptot = g.sb("qp_pt", [128, 1], F32, pes)
        state[key] = [slabs, partials, thr, nthr, gg, ptot, nslab]
    e_on = g.op("vector", lambda e: e.memset(onescol[:], 1.0), inc=True)
    e_on = g.op("vector", lambda e: e.memset(onesrow[:], 1.0), inc=True)
    yield
    ev_ld = {}
    ev_red = {}
    u = 0
    for key, w_dram, scr_dram, n_rc, n_cols in specs:
        slabs, partials, ginv, gninv, gg, ptot, nslab = state[key]
        for si, (rc, j0, wd) in enumerate(slabs):
            b = u % 2
            ev_ld[u] = g.load(stg[b][:, 0:wd], w_dram[rc * 128:(rc + 1) * 128, j0:j0 + wd],
                              waits=[ev_red.get(u - 2)])
            ev_red[u] = g.op(
                "vector",
                lambda e, s=si, b_=b, w_=wd, pt_=partials: e.tensor_reduce(
                    pt_[:, s:s + 1], stg[b_][:, 0:w_], mybir.AxisListType.X, ALU.add,
                    apply_absolute_value=True),
                waits=[ev_ld[u]], inc=True)
            u += 1
            yield
    ev_red_last = ev_red[u - 1]
    while ctl is not None and not ctl.get("gamma", False):
        yield "wait"
    for key, w_dram, scr_dram, n_rc, n_cols in specs:
        slabs, partials, thr, nthr, gg, ptot, nslab = state[key]
        ges = ExitStack()
        psg1 = g.ps("qp_ps1", [1, 1], ges)
        psg2 = g.ps("qp_ps2", [128, 1], ges)
        e_pt = g.op("vector",
                    lambda e, n_=len(slabs), pt_=partials, o_=ptot: e.tensor_reduce(
                        o_[:], pt_[:, 0:n_], mybir.AxisListType.X, ALU.add),
                    waits=[ev_red_last], inc=True)
        e_m1 = mm(g, psg1[:], ptot[:], onescol[:], True, True, waits=[e_pt, e_on], inc=True)
        e_tt = g.op("vector", lambda e, t_=tot, p_=psg1: e.tensor_copy(t_[:], p_[:]),
                    waits=[e_m1], inc=True)
        e_m2 = mm(g, psg2[:], onesrow[:], tot[:], True, True, waits=[e_tt], inc=True)
        inv_n = 1.0 / (n_rc * 128 * n_cols)
        e_g1 = g.op("vector",
                    lambda e, g_=gg, p_=psg2, iv=inv_n: e.tensor_scalar(
                        g_[:], p_[:], iv, Q_EPS, ALU.mult, ALU.add),
                    waits=[e_m2], inc=True)
        e_g2 = g.op("vector",
                    lambda e, t_=thr, g_=gg: e.tensor_scalar(t_[:], g_[:], 0.5, None,
                                                             ALU.mult),
                    waits=[e_g1], inc=True)
        e_g3 = g.op("vector",
                    lambda e, t_=nthr, g_=gg: e.tensor_scalar(t_[:], g_[:], -0.5, None,
                                                              ALU.mult),
                    waits=[e_g2], inc=True)
        state[key].append(e_g3)
        ges.close()
        yield
    ev_p = {}
    ev_n = {}
    ev_q = {}
    ev_st = {}
    u2 = 0
    for key, w_dram, scr_dram, n_rc, n_cols in specs:
        slabs, partials, thr, nthr, gg, ptot, nslab, e_g3 = state[key]
        out_evs[key] = []
        for (j0, wd) in nslab:
            last_store = None
            for rc in range(n_rc):
                b = u2 % 2
                e_ld = g.load(stg[b][:, 0:wd],
                              w_dram[rc * 128:(rc + 1) * 128, j0:j0 + wd],
                              waits=[e_g3 if u2 < 2 else None, ev_q.get(u2 - 2),
                                     ev_n.get(u2 - 2)])
                ev_n[u2] = g.op(
                    "vector",
                    lambda e, w_=wd, b_=b, t_=nthr: e.tensor_scalar(
                        nb[:, 0:w_], stg[b_][:, 0:w_], t_[:, 0:1], None, ALU.is_lt),
                    waits=[e_ld, e_g3, ev_q.get(u2 - 1)], inc=True)
                ev_q[u2] = g.op(
                    "vector",
                    lambda e, w_=wd, b_=b, t_=thr: e.scalar_tensor_tensor(
                        qb[b_][:, 0:w_], stg[b_][:, 0:w_], t_[:, 0:1], nb[:, 0:w_],
                        ALU.is_gt, ALU.subtract),
                    waits=[ev_n[u2], ev_st.get(u2 - 2)], inc=True)
                n_i = wd // 128
                i0 = j0 // 128
                dst = scr_dram[i0:i0 + n_i, :, rc * 128:(rc + 1) * 128].rearrange(
                    "i p j -> p i j")
                ev_st[u2] = g.load(dst, qb[b][:, 0:wd].rearrange("p (i j) -> p i j", j=128),
                                   waits=[ev_q[u2]])
                last_store = ev_st[u2]
                u2 += 1
                yield
            out_evs[key].append(last_store)


class PrepDriver:
    def __init__(self, gen):
        self.gen = gen

    def drive(self, n):
        if self.gen is None:
            return
        for _ in range(n):
            try:
                tag = next(self.gen)
            except StopIteration:
                self.gen = None
                return
            if tag == "wait":
                return

    def finish(self):
        if self.gen is not None:
            for _ in self.gen:
                pass
            self.gen = None


def emit_l0_ffn(g, c, W, prep_evs, prep=None):
    def sev(key, chunk):
        want = chunk // 16
        while prep is not None and prep.gen is not None and \
                len(prep_evs.get(key, [])) <= want:
            prep.drive(1)
        lst = prep_evs.get(key, [])
        return lst[min(want, len(lst) - 1)] if lst else None
    es = ExitStack()
    w1b = [g.sb("l0f_w1", [128, CH * 128], BF16, es) for _ in range(2)]
    w2b = [g.sb("l0f_w2", [128, CH * 128], BF16, es) for _ in range(2)]
    w3b = [g.sb("l0f_w3", [128, L0HC * 128], BF16, es) for _ in range(2)]
    sg = g.sb("l0f_sg", [128, TAU], F32, es)
    psG = [g.ps("l0f_g", [128, TAU], es) for _ in range(2)]
    psU = [g.ps("l0f_u", [128, TAU], es) for _ in range(2)]
    psD = [g.ps("l0f_d", [128, TAU], es) for _ in range(2)]
    ev_evac_all = []
    ev_gmm = {}
    ev_sg = {}
    ev_h = {}
    ev_w = {}
    ev_dmm = {}
    ev_dev = {}
    step = 0
    dstep = 0
    for tau in range(NTAU):
        t0 = tau * TAU
        for i in range(L0HC):
            b = step % 2
            if prep is not None:
                prep.drive(1)
            ev_w[step] = [
                g.load(w1b[b][:], W["w1q"][i], waits=[sev("w1", i), ev_gmm.get(step - 2)]),
                g.load(w2b[b][:], W["w2q"][i], waits=[sev("w2", i), ev_gmm.get(step - 2)]),
            ]
            lastg = lastu = None
            for chn in range(CH):
                lastg = mm(g, psG[b][:], w1b[b][:, chn * 128:(chn + 1) * 128],
                           xs(c, chn, t0, TAU), chn == 0, chn == CH - 1,
                           waits=(ev_w[step] + [ev_sg.get(step - 2)] if chn == 0 else ()),
                           inc=(chn == CH - 1))
                lastu = mm(g, psU[b][:], w2b[b][:, chn * 128:(chn + 1) * 128],
                           xs(c, chn, t0, TAU), chn == 0, chn == CH - 1,
                           waits=([ev_h.get(step - 2)] if chn == 0 else ()),
                           inc=(chn == CH - 1))
            ev_gmm[step] = lastu
            ev_sg[step] = g.op("scalar",
                               lambda e, b_=b: e.activation(sg[:], psG[b_][:], AF.Sigmoid),
                               waits=[lastg, ev_h.get(step - 1)], inc=True)
            ev_h[step] = g.op(
                "vector",
                lambda e, b_=b, ii=i: e.tensor_mul(hs_(c, ii), sg[:], psU[b_][:]),
                waits=[lastu, ev_sg[step]], inc=True)
            step += 1
        ev_h_tau = [ev_h[step - 1 - k] for k in range(L0HC)]
        for m in range(CH):
            b = dstep % 2
            if prep is not None:
                prep.drive(1)
            e_w3 = g.load(w3b[b][:], W["w3q"][m],
                          waits=[sev("w3", m), ev_dmm.get(dstep - 2)])
            last = None
            for ic in range(L0HC):
                last = mm(g, psD[b][:], w3b[b][:, ic * 128:(ic + 1) * 128], hs_(c, ic),
                          ic == 0, ic == L0HC - 1,
                          waits=(([e_w3, ev_dev.get(dstep - 2)] + ev_h_tau)
                                 if ic == 0 else ()),
                          inc=(ic == L0HC - 1))
            ev_dmm[dstep] = last
            ev_dev[dstep] = g.op(
                "vector",
                lambda e, b_=b, m_=m, t=t0: e.tensor_add(
                    xs(c, m_, t, TAU), psD[b_][:], xs(c, m_, t, TAU)),
                waits=[last], inc=True)
            ev_evac_all.append(ev_dev[dstep])
            dstep += 1
    es.close()
    return ev_evac_all


def emit_l1_attn(g, c, W):
    es = ExitStack()
    bq = g.sb("l1_bq", [128, CH], F32, es)
    bk = g.sb("l1_bk", [128, CH], F32, es)
    ev_b1 = g.load(bq[:], W["bq"])
    ev_b2 = g.load(bk[:], W["bk"])
    wqkvb = [g.sb("l1_wqkv", [128, CH * 384], BF16, es) for _ in range(2)]
    bvr = [g.sb("l1_bvr", [1, 128], BF16, es) for _ in range(2)]
    q_h = g.sb("l1_qh", [128, T], BF16, es)
    k_h = g.sb("l1_kh", [128, T], BF16, es)
    v_h = g.sb("l1_vh", [128, T], BF16, es)
    pTl = [g.sb("l1_pT", [128, 1024], BF16, es) for _ in range(2)]
    rcb = g.sb("l1_rcb", [128, 1024], F32, es)

    def load_head_w(h, waits):
        b = h % 2
        e1 = g.loadc(wqkvb[b][:], W["wqkv"][h], waits=waits)
        e2 = g.loadc(bvr[b][:], W["bvr"][h:h + 1, :], waits=waits)
        return [e1, e2]

    ev_wh = {0: load_head_w(0, [c.ev_init]), 1: load_head_w(1, [c.ev_init])}
    ev_projdone = {}
    ev_head_attn_done = {}
    for h in range(NH):
        b = h % 2
        pes = ExitStack()
        psp = [g.ps("l1_pp", [128, TAU], pes) for _ in range(2)]
        first_evs = []
        ev_pev = {}
        pstep = 0
        prev_attn = ev_head_attn_done.get(h - 1, [])
        for which, (off, bias_t) in enumerate(((0, bq), (128, bk))):
            for tau in range(NTAU):
                pb = pstep % 2
                last = None
                for chn in range(CH):
                    w0 = []
                    if chn == 0:
                        w0 = ev_wh[h] + [ev_pev.get(pstep - 2), ev_b1, ev_b2] + \
                             (prev_attn if pstep == 0 else []) + \
                             ([ev_projdone[h - 2]] if h >= 2 else [])
                    last = mm(g, psp[pb][:],
                              wqkvb[b][:, chn * 384 + off: chn * 384 + off + 128],
                              xs(c, chn, tau * TAU, TAU), chn == 0, chn == CH - 1,
                              waits=w0, inc=(chn == CH - 1))
                dst = q_h if which == 0 else k_h
                ev_pev[pstep] = g.op(
                    "vector",
                    lambda e, tt=tau, ps_=psp[pb], d=dst, bt=bias_t, hh=h: e.tensor_scalar_add(
                        d[:, tt * TAU:(tt + 1) * TAU], ps_[:], bt[:, hh:hh + 1]),
                    waits=[last], inc=True)
                first_evs.append(ev_pev[pstep])
                pstep += 1
        for grp in range(4):
            pb = pstep % 2
            last = None
            for j in range(4):
                kc = grp * 4 + j
                for chn in range(CH):
                    last = mm(g, psp[pb][:, j * 128:(j + 1) * 128],
                              xs(c, chn, kc * 128, 128),
                              wqkvb[b][:, chn * 384 + 256: chn * 384 + 384],
                              chn == 0, False,
                              waits=([ev_pev.get(pstep - 2)] if (j == 0 and chn == 0) else ()),
                              inc=False)
                last = mm(g, psp[pb][:, j * 128:(j + 1) * 128], c.ones1[:], bvr[b][:],
                          False, True, inc=(j == 3))
            ev_pev[pstep] = g.op(
                "vector",
                lambda e, gg=grp, ps_=psp[pb]: e.tensor_copy(v_h[:, gg * TAU:(gg + 1) * TAU],
                                                             ps_[:]),
                waits=[last], inc=True)
            first_evs.append(ev_pev[pstep])
            pstep += 1
        ev_projdone[h] = ev_pev[pstep - 1]
        pes.close()
        if h + 2 < NH:
            ev_wh[h + 2] = load_head_w(h + 2, [ev_projdone[h]])
        aes = ExitStack()
        psS = [g.ps("l1_s", [128, 1024], aes) for _ in range(2)]
        psR = g.ps("l1_r", [128, 1024], aes)
        psA = g.ps("l1_a", [128, 1024], aes)
        ev_half = []
        for half in range(2):
            q0 = half * 1024
            ev_S = {}
            ev_exp = {}
            ev_Rmm = {}
            ev_Amm = {}
            for kc in range(CH):
                p2 = kc % 2
                last = None
                for n in range(2):
                    w0 = []
                    if n == 0:
                        w0 = [ev_exp.get(kc - 2)]
                        if kc == 0:
                            w0 += first_evs + ev_half + prev_attn
                    last = mm(g, psS[p2][:, n * 512:(n + 1) * 512],
                              k_h[:, kc * 128:(kc + 1) * 128],
                              q_h[:, q0 + n * 512: q0 + (n + 1) * 512],
                              True, True, waits=w0, inc=(n == 1))
                ev_S[kc] = last
                ev_exp[kc] = g.op(
                    "scalar",
                    lambda e, pt_=pTl[p2], ss_=psS[p2]: e.activation(
                        pt_[:], ss_[:], AF.Exp, bias=c.cpool[:, 0:1],
                        scale=c.cpool[:, 1:2]),
                    waits=[ev_S[kc], ev_Rmm.get(kc - 2), ev_Amm.get(kc - 2)], inc=True)
                last = None
                for n in range(2):
                    last = mm(g, psR[:, n * 512:(n + 1) * 512], c.allones[:],
                              pTl[p2][:, n * 512:(n + 1) * 512], kc == 0, kc == CH - 1,
                              waits=([ev_exp[kc]] + (ev_half if kc == 0 else []))
                              if n == 0 else (),
                              inc=(n == 1))
                ev_Rmm[kc] = last
                last = None
                for n in range(2):
                    last = mm(g, psA[:, n * 512:(n + 1) * 512],
                              v_h[:, kc * 128:(kc + 1) * 128],
                              pTl[p2][:, n * 512:(n + 1) * 512], kc == 0, kc == CH - 1,
                              waits=() if n == 1 else [ev_exp[kc]],
                              inc=(n == 1))
                ev_Amm[kc] = last
            e_rc = g.op("vector", lambda e, pr_=psR: e.reciprocal(rcb[:], pr_[:]),
                        waits=[ev_Rmm[CH - 1]], inc=True)
            e_ao = g.op(
                "vector",
                lambda e, hh=h, q0_=q0, pa_=psA: e.tensor_mul(
                    c.ao[:, hh * T + q0_: hh * T + q0_ + 1024], pa_[:], rcb[:]),
                waits=[ev_Amm[CH - 1], e_rc], inc=True)
            ev_half = [e_rc, e_ao, ev_exp[CH - 1], ev_exp[CH - 2]]
        ev_head_attn_done[h] = ev_half
        aes.close()
    es.close()




def emit_l1_ffn(g, c, W):
    es = ExitStack()
    gs = g.sb("l1f_gs", [128, L1HC], F32, es)
    us = g.sb("l1f_us", [128, L1HC], F32, es)
    ds = g.sb("l1f_ds", [128, CH], F32, es)
    e_s1 = g.load(gs[:], W["gs"])
    e_s2 = g.load(us[:], W["us"])
    e_s3 = g.load(ds[:], W["ds"])
    ev_scales = [e_s1, e_s2, e_s3]
    gwb = [g.sb("l1f_gw", [128, CH * 128], BF16, es) for _ in range(2)]
    uwb = [g.sb("l1f_uw", [128, CH * 128], BF16, es) for _ in range(2)]
    dwb = [g.sb("l1f_dw", [128, L1HC * 128], BF16, es) for _ in range(2)]
    rs_bc = g.sb("l1f_rs", [128, TAU], F32, es)
    rsh_bc = g.sb("l1f_rsh", [128, TAU], F32, es)
    sgt = g.sb("l1f_sg", [128, TAU], F32, es)
    gin = g.sb("l1f_gin", [128, TAU], F32, es)
    uin = g.sb("l1f_uin", [128, TAU], F32, es)
    dtmp = g.sb("l1f_dt", [128, TAU], F32, es)
    psRS = g.ps("l1f_prs", [128, TAU], es)
    psG = [g.ps("l1f_pg", [128, TAU], es) for _ in range(2)]
    psU = [g.ps("l1f_pu", [128, TAU], es) for _ in range(2)]
    psD = [g.ps("l1f_pd", [128, TAU], es) for _ in range(2)]
    ev_evac_all = []
    ev_gmm = {}
    ev_sg = {}
    ev_h = {}
    ev_gsub = {}
    ev_usub = {}
    ev_dsub = {}
    ev_dmm = {}
    ev_dev = {}
    ev_rs_prev = None
    ev_rsh_prev = None
    step = 0
    dstep = 0
    for tau in range(NTAU):
        t0 = tau * TAU
        last = None
        for chn in range(CH):
            last = mm(g, psRS[:], c.allones[:], xs(c, chn, t0, TAU), chn == 0, chn == CH - 1,
                      waits=([c.ev_stream, c.ev_init, ev_rsh_prev] if chn == 0 else ()),
                      inc=(chn == CH - 1))
        ev_rs = g.op("vector",
                     lambda e: e.tensor_scalar(rs_bc[:], psRS[:], 128.0, None, ALU.mult),
                     waits=[last, ev_rs_prev], inc=True)
        ev_rs_prev = ev_rs
        for i in range(L1HC):
            b = step % 2
            ev_w = [
                g.loadc(gwb[b][:], W["gw"][i], waits=[ev_gmm.get(step - 2)]),
                g.loadc(uwb[b][:], W["uw"][i], waits=[ev_gmm.get(step - 2)]),
            ]
            lastg = lastu = None
            for chn in range(CH):
                lastg = mm(g, psG[b][:], gwb[b][:, chn * 128:(chn + 1) * 128],
                           xs(c, chn, t0, TAU), chn == 0, chn == CH - 1,
                           waits=(ev_w + [ev_gsub.get(step - 2)] if chn == 0 else ()),
                           inc=(chn == CH - 1))
            for chn in range(CH):
                lastu = mm(g, psU[b][:], uwb[b][:, chn * 128:(chn + 1) * 128],
                           xs(c, chn, t0, TAU), chn == 0, chn == CH - 1,
                           waits=([ev_usub.get(step - 2)] if chn == 0 else ()),
                           inc=(chn == CH - 1))
            ev_gmm[step] = lastu
            ev_gsub[step] = g.op(
                "vector",
                lambda e, b_=b: e.tensor_sub(gin[:], psG[b_][:], rs_bc[:]),
                waits=[lastg, ev_rs, ev_sg.get(step - 1)], inc=True)
            ev_sg[step] = g.op(
                "scalar",
                lambda e, ii=i: e.activation(sgt[:], gin[:], AF.Silu,
                                             scale=gs[:, ii:ii + 1]),
                waits=[ev_gsub[step], ev_scales[0], ev_h.get(step - 1)], inc=True)
            ev_usub[step] = g.op(
                "vector",
                lambda e, b_=b: e.tensor_sub(uin[:], psU[b_][:], rs_bc[:]),
                waits=[lastu, ev_rs, ev_h.get(step - 1)], inc=True)
            ev_h[step] = g.op(
                "vector",
                lambda e, ii=i: e.scalar_tensor_tensor(
                    hs_(c, ii), uin[:], us[:, ii:ii + 1], sgt[:], ALU.mult, ALU.mult),
                waits=[ev_usub[step], ev_sg[step], ev_scales[1]], inc=True)
            step += 1
        ev_h_tau = [ev_h[step - 1 - k] for k in range(L1HC)]
        last = None
        for ic in range(L1HC):
            last = mm(g, psRS[:], c.allones[:], hs_(c, ic), ic == 0, ic == L1HC - 1,
                      waits=(([ev_rs] + ev_h_tau) if ic == 0 else ()),
                      inc=(ic == L1HC - 1))
        ev_rsh = g.op("vector",
                      lambda e: e.tensor_scalar(rsh_bc[:], psRS[:], 128.0, None, ALU.mult),
                      waits=[last, ev_rsh_prev], inc=True)
        ev_rsh_prev = ev_rsh
        for m in range(CH):
            b = dstep % 2
            e_w3 = g.loadc(dwb[b][:], W["dw"][m], waits=[ev_dmm.get(dstep - 2)])
            last = None
            for ic in range(L1HC):
                last = mm(g, psD[b][:], dwb[b][:, ic * 128:(ic + 1) * 128], hs_(c, ic),
                          ic == 0, ic == L1HC - 1,
                          waits=(([e_w3, ev_dsub.get(dstep - 2)] + ev_h_tau)
                                 if ic == 0 else ()),
                          inc=(ic == L1HC - 1))
            ev_dmm[dstep] = last
            ev_dsub[dstep] = g.op(
                "vector",
                lambda e, b_=b: e.tensor_sub(dtmp[:], psD[b_][:], rsh_bc[:]),
                waits=[last, ev_rsh, ev_dev.get(dstep - 1)], inc=True)
            ev_dev[dstep] = g.op(
                "vector",
                lambda e, m_=m, t=t0: e.scalar_tensor_tensor(
                    xs(c, m_, t, TAU), dtmp[:], ds[:, m_:m_ + 1], xs(c, m_, t, TAU),
                    ALU.mult, ALU.add),
                waits=[ev_dsub[dstep], ev_scales[2]], inc=True)
            ev_evac_all.append(ev_dev[dstep])
            dstep += 1
    es.close()
    return ev_evac_all


def _declare_l0a(nc, first):
    t = {}
    t["xin"] = nc.dram_tensor("xin", [D, T], F32 if first else BF16, kind="ExternalInput").ap()
    t["wq"] = nc.dram_tensor("wq", [NH, 128, CH * 128], F32, kind="ExternalInput").ap()
    t["wkv"] = nc.dram_tensor("wkv", [128, LATC * CH * 128], F32, kind="ExternalInput").ap()
    t["wk"] = nc.dram_tensor("wk", [NH, 128, LATC * 128], F32, kind="ExternalInput").ap()
    t["wv"] = nc.dram_tensor("wv", [NH, 128, LATC * 128], F32, kind="ExternalInput").ap()
    t["wo"] = nc.dram_tensor("wo", [CH, 128, CH * 128], F32, kind="ExternalInput").ap()
    t["masks"] = nc.dram_tensor("masks", [3, 128, 128], F32, kind="ExternalInput").ap()
    t["ident"] = nc.dram_tensor("ident", [128, 128], F32, kind="ExternalInput").ap()
    return t


def _declare_l0f(nc):
    t = {}
    t["w1T"] = nc.dram_tensor("w1T", [D, L0H], F32, kind="ExternalInput").ap()
    t["w2T"] = nc.dram_tensor("w2T", [D, L0H], F32, kind="ExternalInput").ap()
    t["w3T"] = nc.dram_tensor("w3T", [L0H, D], F32, kind="ExternalInput").ap()
    t["w1q"] = nc.dram_tensor("w1q", [L0HC, 128, CH * 128], BF16, kind="Internal").ap()
    t["w2q"] = nc.dram_tensor("w2q", [L0HC, 128, CH * 128], BF16, kind="Internal").ap()
    t["w3q"] = nc.dram_tensor("w3q", [CH, 128, L0HC * 128], BF16, kind="Internal").ap()
    return t


def build_l0_ffn_launch(first=False, last=False):
    nc = bass.Bass("TRN2", target_bir_lowering=False, debug=False)
    t = _declare_l0f(nc)
    xin = nc.dram_tensor("xin", [D, T], F32 if first else BF16, kind="ExternalInput").ap()
    xo = nc.dram_tensor("xo", [D, T], F32 if last else BF16, kind="ExternalOutput").ap()
    g = G(nc)
    c = setup_common(g)
    emit_load_stream(g, c, xin, first)
    prep_evs = {}
    prep_es = ExitStack()
    pd = PrepDriver(gen_quantize_prep(g, c, [
        ("w1", t["w1T"], t["w1q"], CH, L0H),
        ("w2", t["w2T"], t["w2q"], CH, L0H),
        ("w3", t["w3T"], t["w3q"], L0HC, D),
    ], prep_evs, prep_es, {"gamma": True}))
    pd.finish()
    g.barrier()
    evs = emit_l0_ffn(g, c, t, prep_evs)
    ev_out = emit_ln(g, c, evs, final_out=(xo if last else None))
    if not last:
        ev_last = emit_store_stream(g, c, xo, waits=ev_out)
    else:
        ev_last = ev_out[-1]
    g.op("gpsimd", lambda e: e.engine_nop(), waits=[ev_last] + list(ev_out), inc=True)
    sems = {}
    sem_es = ExitStack()
    for s in ALL_SEMS:
        sems[s] = sem_es.enter_context(nc.semaphore(s))
    g.build(sems)
    g.es.close()
    sem_es.close()
    return nc


def _declare_l1a(nc):
    t = {}
    t["wqkv"] = nc.dram_tensor("wqkv", [NH, 128, CH * 384], F32, kind="ExternalInput").ap()
    t["bq"] = nc.dram_tensor("bq", [128, CH], F32, kind="ExternalInput").ap()
    t["bk"] = nc.dram_tensor("bk", [128, CH], F32, kind="ExternalInput").ap()
    t["bvr"] = nc.dram_tensor("bvr", [NH, 128], F32, kind="ExternalInput").ap()
    t["wout"] = nc.dram_tensor("wout", [CH, 128, CH * 128], F32, kind="ExternalInput").ap()
    t["bo"] = nc.dram_tensor("bo", [128, CH], F32, kind="ExternalInput").ap()
    return t


def build_l1_attn_launch(first=False, last=False):
    nc = bass.Bass("TRN2", target_bir_lowering=False, debug=False)
    t = _declare_l1a(nc)
    xin = nc.dram_tensor("xin", [D, T], F32 if first else BF16, kind="ExternalInput").ap()
    xo = nc.dram_tensor("xo", [D, T], F32 if last else BF16, kind="ExternalOutput").ap()
    g = G(nc)
    c = setup_common(g)
    bo = g.sb("l1_bo", [128, CH], F32)
    ev_bo = g.load(bo[:], t["bo"])
    emit_load_stream(g, c, xin, first)
    emit_l1_attn(g, c, t)
    g.barrier()
    ev_out = emit_outproj(g, c, t["wout"], bo, final_out=(xo if last else None))
    if not last:
        ev_last = emit_store_stream(g, c, xo, waits=ev_out)
    else:
        ev_last = ev_out[-1]
    g.op("gpsimd", lambda e: e.engine_nop(), waits=[ev_last] + list(ev_out), inc=True)
    sems = {}
    sem_es = ExitStack()
    for s in ALL_SEMS:
        sems[s] = sem_es.enter_context(nc.semaphore(s))
    g.build(sems)
    g.es.close()
    sem_es.close()
    return nc


def _declare_l1f(nc):
    t = {}
    t["gw"] = nc.dram_tensor("gw", [L1HC, 128, CH * 128], I32, kind="ExternalInput").ap()
    t["uw"] = nc.dram_tensor("uw", [L1HC, 128, CH * 128], I32, kind="ExternalInput").ap()
    t["dw"] = nc.dram_tensor("dw", [CH, 128, L1HC * 128], I32, kind="ExternalInput").ap()
    t["gs"] = nc.dram_tensor("gs", [128, L1HC], F32, kind="ExternalInput").ap()
    t["us"] = nc.dram_tensor("us", [128, L1HC], F32, kind="ExternalInput").ap()
    t["ds"] = nc.dram_tensor("ds", [128, CH], F32, kind="ExternalInput").ap()
    return t


def build_l1_ffn_launch(first=False, last=False):
    nc = bass.Bass("TRN2", target_bir_lowering=False, debug=False)
    t = _declare_l1f(nc)
    xin = nc.dram_tensor("xin", [D, T], F32 if first else BF16, kind="ExternalInput").ap()
    xo = nc.dram_tensor("xo", [D, T], F32 if last else BF16, kind="ExternalOutput").ap()
    g = G(nc)
    c = setup_common(g)
    emit_load_stream(g, c, xin, first)
    evs = emit_l1_ffn(g, c, t)
    ev_out = emit_ln(g, c, evs, final_out=(xo if last else None))
    if not last:
        ev_last = emit_store_stream(g, c, xo, waits=ev_out)
    else:
        ev_last = ev_out[-1]
    g.op("gpsimd", lambda e: e.engine_nop(), waits=[ev_last] + list(ev_out), inc=True)
    sems = {}
    sem_es = ExitStack()
    for s in ALL_SEMS:
        sems[s] = sem_es.enter_context(nc.semaphore(s))
    g.build(sems)
    g.es.close()
    sem_es.close()
    return nc


def build_fused_launch():
    nc = bass.Bass("TRN2", target_bir_lowering=False, debug=False)
    t0a = _declare_l0a(nc, True)
    t0f = _declare_l0f(nc)
    t1a = _declare_l1a(nc)
    t1f = _declare_l1f(nc)
    xo = nc.dram_tensor("xo", [D, T], F32, kind="ExternalOutput").ap()
    g = G(nc)
    c = setup_common(g, t0a["masks"], t0a["ident"])
    bo = g.sb("l1_bo", [128, CH], F32)
    g.load(bo[:], t1a["bo"])
    emit_load_stream(g, c, t0a["xin"], True)
    prep_evs = {}
    prep_es = ExitStack()
    prep_ctl = {}
    prep = PrepDriver(gen_quantize_prep(g, c, [
        ("w1", t0f["w1T"], t0f["w1q"], CH, L0H),
        ("w2", t0f["w2T"], t0f["w2q"], CH, L0H),
        ("w3", t0f["w3T"], t0f["w3q"], L0HC, D),
    ], prep_evs, prep_es, prep_ctl))
    prep.drive(1)
    emit_l0_attn(g, c, t0a, prep=prep)
    prep_ctl["gamma"] = True
    prep.drive(6)
    g.barrier()
    emit_outproj(g, c, t0a["wo"], None, prep=prep)
    g.barrier()
    evs = emit_l0_ffn(g, c, t0f, prep_evs, prep=prep)
    prep.finish()
    ev_out = emit_ln(g, c, evs)
    prep_es.close()
    g.barrier()
    emit_l1_attn(g, c, t1a)
    g.barrier()
    emit_outproj(g, c, t1a["wout"], bo)
    g.barrier()
    evs = emit_l1_ffn(g, c, t1f)
    ev_out = emit_ln(g, c, evs, final_out=xo)
    g.op("gpsimd", lambda e: e.engine_nop(), waits=list(ev_out), inc=True)
    sems = {}
    sem_es = ExitStack()
    for s in ALL_SEMS:
        sems[s] = sem_es.enter_context(nc.semaphore(s))
    g.build(sems)
    g.es.close()
    sem_es.close()
    return nc


def build_l0_attn_launch(first=True, last=False, debug=False):
    nc = bass.Bass("TRN2", target_bir_lowering=False, debug=False)
    t = _declare_l0a(nc, first)
    xo = nc.dram_tensor("xo", [D, T], F32 if last else BF16, kind="ExternalOutput").ap()
    if debug:
        dbg_x0 = nc.dram_tensor("dbg_x0", [D, T], BF16, kind="ExternalOutput").ap()
        dbg_ao = nc.dram_tensor("dbg_ao", [D, T], BF16, kind="ExternalOutput").ap()
        dbg_kv = nc.dram_tensor("dbg_kv", [LAT, T], BF16, kind="ExternalOutput").ap()
    g = G(nc)
    c = setup_common(g, t["masks"], t["ident"])
    emit_load_stream(g, c, t["xin"], first)
    if debug:
        for i in range(CH):
            g.store(dbg_x0[i * 128:(i + 1) * 128, :], xs(c, i, 0, T), waits=[c.ev_stream])
    emit_l0_attn(g, c, t, dbg_kv if debug else None)
    g.barrier()
    if debug:
        for i in range(CH):
            g.store(dbg_ao[i * 128:(i + 1) * 128, :], aos(c, i, 0, T))
        g.barrier()
    ev_out = emit_outproj(g, c, t["wo"], None, final_out=(xo if last else None))
    if not last:
        ev_last = emit_store_stream(g, c, xo, waits=ev_out)
    else:
        ev_last = ev_out[-1]
    g.op("gpsimd", lambda e: e.engine_nop(), waits=[ev_last] + list(ev_out), inc=True)
    sems = {}
    sem_es = ExitStack()
    for s in ALL_SEMS:
        sems[s] = sem_es.enter_context(nc.semaphore(s))
    g.build(sems)
    g.es.close()
    sem_es.close()
    return nc


def _pack_panels(w):
    K, M = w.shape
    return np.ascontiguousarray(
        w.reshape(K // 128, 128, M // 128, 128).transpose(2, 1, 0, 3).reshape(M // 128, 128, K))


def _make_masks():
    i = np.arange(128)
    NEG = -32768.0
    m_diag = np.where(i[:, None] <= i[None, :], 0.0, NEG).astype(np.float32)
    m_prev = np.where(i[:, None] > i[None, :], 0.0, NEG).astype(np.float32)
    m_prev0 = m_prev.copy()
    m_prev0[0:16, :] = 0.0
    return np.ascontiguousarray(np.stack([m_diag.T, m_prev.T, m_prev0.T]))


def _l1a_packs(inputs):
    wqkv = inputs["L1_attn_w_qkv"]
    wt = np.ascontiguousarray(wqkv.T)
    per_head = []
    for h in range(NH):
        sl = np.concatenate([wt[:, h * 128:(h + 1) * 128],
                             wt[:, 2048 + h * 128: 2048 + (h + 1) * 128],
                             wt[:, 4096 + h * 128: 4096 + (h + 1) * 128]], axis=1)
        per_head.append(sl.reshape(CH, 128, 384).transpose(1, 0, 2).reshape(128, CH * 384))
    b = inputs["L1_attn_b_qkv"]
    return {
        "wqkv": np.ascontiguousarray(np.stack(per_head)),
        "bq": np.ascontiguousarray(b[0:2048].reshape(CH, 128).T),
        "bk": np.ascontiguousarray(b[2048:4096].reshape(CH, 128).T),
        "bvr": np.ascontiguousarray(b[4096:6144].reshape(NH, 128)),
        "wout": _pack_panels(np.ascontiguousarray(inputs["L1_attn_w_out"].T)),
        "bo": np.ascontiguousarray(inputs["L1_attn_b_out"].reshape(CH, 128).T),
    }


def _l1f_packs(inputs):
    return {
        "gw": _pack_panels(np.ascontiguousarray(inputs["L1_ffn_gate_w"].T)),
        "uw": _pack_panels(np.ascontiguousarray(inputs["L1_ffn_up_w"].T)),
        "dw": _pack_panels(np.ascontiguousarray(inputs["L1_ffn_down_w"].T)),
        "gs": np.ascontiguousarray(inputs["L1_ffn_gate_scale"].reshape(L1HC, 128).T),
        "us": np.ascontiguousarray(inputs["L1_ffn_up_scale"].reshape(L1HC, 128).T),
        "ds": np.ascontiguousarray(inputs["L1_ffn_down_scale"].reshape(CH, 128).T),
    }


def _l0a_packs(inputs):
    wkv = _pack_panels(inputs["L0_attn_Wkv"])
    return {
        "wq": _pack_panels(inputs["L0_attn_Wq"]),
        "wkv": np.ascontiguousarray(wkv.transpose(1, 0, 2).reshape(128, LATC * CH * 128)),
        "wk": _pack_panels(inputs["L0_attn_Wk"]),
        "wv": _pack_panels(inputs["L0_attn_Wv"]),
        "wo": _pack_panels(inputs["L0_attn_Wo"]),
        "masks": _make_masks(),
        "ident": np.eye(128, dtype=np.float32),
    }


_FUSED = True


def _all_packs(inputs):
    p = {}
    p.update(_l0a_packs(inputs))
    p.update({
        "w1T": np.ascontiguousarray(inputs["L0_ffn_w1"].T),
        "w2T": np.ascontiguousarray(inputs["L0_ffn_w2"].T),
        "w3T": np.ascontiguousarray(inputs["L0_ffn_w3"].T),
    })
    p.update(_l1a_packs(inputs))
    p.update(_l1f_packs(inputs))
    return p


def kernel(**inputs):
    x = inputs["L0_attn_x"]
    B = x.shape[0]
    cur = [np.ascontiguousarray(x[b].T) for b in range(B)]
    if _FUSED:
        packs = _all_packs(inputs)
        nc = build_fused_launch()
        in_maps = [dict(packs, xin=cur[b]) for b in range(B)]
        res = run_bass_kernel_spmd(nc, in_maps, core_ids=list(range(B)))
        cur = [np.asarray(res.results[b]["xo"]) for b in range(B)]
    else:
        l0a = _l0a_packs(inputs)
        l0f = {
            "w1T": np.ascontiguousarray(inputs["L0_ffn_w1"].T),
            "w2T": np.ascontiguousarray(inputs["L0_ffn_w2"].T),
            "w3T": np.ascontiguousarray(inputs["L0_ffn_w3"].T),
        }
        l1a = _l1a_packs(inputs)
        l1f = _l1f_packs(inputs)
        launches = [
            ("l0a", build_l0_attn_launch, l0a),
            ("l0f", build_l0_ffn_launch, l0f),
            ("l1a", build_l1_attn_launch, l1a),
            ("l1f", build_l1_ffn_launch, l1f),
        ]
        for li, (name, builder, packs) in enumerate(launches):
            nc = builder(first=(li == 0), last=(li == len(launches) - 1))
            in_maps = [dict(packs, xin=cur[b]) for b in range(B)]
            res = run_bass_kernel_spmd(nc, in_maps, core_ids=list(range(B)))
            cur = [np.asarray(res.results[b]["xo"]) for b in range(B)]
    return np.stack([cur[b].astype(np.float32).T for b in range(B)])


# version 19
# speedup vs baseline: 1.4848x; 1.0425x over previous
import numpy as np
import concourse.bass as bass
import concourse.mybir as mybir
from concourse.bass_utils import run_bass_kernel_spmd
from contextlib import ExitStack

F32 = mybir.dt.float32
BF16 = mybir.dt.bfloat16
I32 = mybir.dt.int32
AF = mybir.ActivationFunctionType
ALU = mybir.AluOpType

D = 2048
T = 2048
CH = 16
NH = 16
HD = 128
TAU = 512
NTAU = T // TAU
LAT = 512
LATC = LAT // 128
L0H = 5632
L0HC = L0H // 128
L1H = 8192
L1HC = L1H // 128
SM_SCALE = float(1.0 / np.sqrt(128.0))
SM_SHIFT = -12.0
LN_EPS = 1e-5
Q_EPS = 1e-8
INV_D = 1.0 / D

ENGS = ("sync", "tensor", "scalar", "vector", "gpsimd")
ESEM = {"tensor": "pe", "scalar": "act", "vector": "dve", "gpsimd": "pool"}
QROT = {"sync": ["qs%d" % i for i in range(4)],
        "gpsimd": ["qp%d" % i for i in range(6)],
        "scalar": ["qa%d" % i for i in range(2)]}
ALL_SEMS = ("pe", "act", "dve", "pool") + tuple(s for v in QROT.values() for s in v)


class G:

    def __init__(self, nc):
        self.nc = nc
        self.ops = []
        self.cnt = {s: 0 for s in ALL_SEMS}
        self.qrr = {}
        self.snapshots = {}
        self.barv = 0
        self.es = ExitStack()
        self.uid = 0

    def _nm(self, name):
        self.uid += 1
        return "%s_u%d" % (name, self.uid)

    def sb(self, name, shape, dtype, es=None):
        return (es or self.es).enter_context(self.nc.sbuf_tensor(self._nm(name), shape, dtype))

    def ps(self, name, shape, es=None, dtype=F32):
        return (es or self.es).enter_context(self.nc.psum_tensor(self._nm(name), shape, dtype))

    def op(self, eng, fn, waits=(), inc=False):
        w = [x for x in waits if x is not None]
        ev = None
        upd = None
        if inc:
            sem = ESEM[eng]
            self.cnt[sem] += 1
            ev = (sem, self.cnt[sem])
            upd = (sem, 1)
        self.ops.append((eng, fn, w, upd, self.barv))
        return ev

    def dma(self, eng, out, in_, waits=()):
        rot = QROT[eng]
        q = rot[self.qrr.setdefault(eng, 0) % len(rot)]
        self.qrr[eng] += 1
        self.cnt[q] += 16
        ev = (q, self.cnt[q])
        def fn(e, o=out, i=in_):
            return e.dma_start(out=o, in_=i)
        self.ops.append((eng, fn, [x for x in waits if x is not None], (q, 16), self.barv))
        return ev

    def load(self, out, in_, waits=()):
        return self.dma("sync", out, in_, waits)

    def loadc(self, out, in_, waits=()):
        return self.dma("gpsimd", out, in_, waits)

    def store(self, out, in_, waits=()):
        return self.dma("scalar", out, in_, waits)

    def barrier(self):
        self.barv += 1
        self.snapshots[self.barv] = dict(self.cnt)

    def build(self, sems):
        ops = self.ops
        snapshots = self.snapshots

        def mk(ename):
            def f(eng):
                waited = {}
                seen_bar = [0]
                own = ESEM.get(ename)

                def do_wait(s, v):
                    if v <= 0 or waited.get(s, 0) >= v:
                        return
                    eng.wait_ge(sems[s], v)
                    waited[s] = v

                for (e, fn, waits, upd, barv) in ops:
                    if e != ename:
                        continue
                    if barv > seen_bar[0]:
                        for s, v in snapshots[barv].items():
                            if s != own:
                                do_wait(s, v)
                        seen_bar[0] = barv
                    for (s, v) in waits:
                        do_wait(s, v)
                    ins = fn(eng)
                    if upd is not None:
                        ins.then_inc(sems[upd[0]], upd[1])
            return f

        with self.nc.Block() as block:
            for ename in ENGS:
                getattr(block, ename)(mk(ename))


def mm(g, out_ap, lhsT_ap, rhs_ap, start, stop, waits=(), inc=False):
    return g.op(
        "tensor",
        lambda e, o=out_ap, l=lhsT_ap, r=rhs_ap, s1=start, s2=stop:
            e.matmul(o, l, r, start=s1, stop=s2),
        waits, inc)


class Ctx:
    pass


def setup_common(g, masks_dram=None, ident_dram=None):
    c = Ctx()
    c.x = g.sb("x_sb", [128, CH * T], BF16)
    c.ao = g.sb("ao_sb", [128, CH * T], BF16)
    c.allones = g.sb("allones", [128, 128], BF16)
    c.ones1 = g.sb("ones1", [1, 128], BF16)
    c.mrow = g.sb("mrow", [1, 128], BF16)
    c.cpool = g.sb("cpool", [128, 8], F32)
    g.op("vector", lambda e: e.memset(c.allones[:], 1.0), inc=True)
    g.op("vector", lambda e: e.memset(c.ones1[:], 1.0), inc=True)
    g.op("vector", lambda e: e.memset(c.mrow[:], -128.0), inc=True)
    g.op("vector", lambda e: e.memset(c.cpool[:, 0:1], SM_SHIFT), inc=True)
    g.op("vector", lambda e: e.memset(c.cpool[:, 1:2], SM_SCALE), inc=True)
    g.op("vector", lambda e: e.memset(c.cpool[:, 2:3], LN_EPS), inc=True)
    c.ev_init = g.op("vector", lambda e: e.memset(c.cpool[:, 3:4], Q_EPS), inc=True)
    c.ev_masks = None
    if masks_dram is not None:
        c.masks = g.sb("masks", [128, 3 * 128], BF16)
        for i in range(3):
            c.ev_masks = g.loadc(c.masks[:, i * 128:(i + 1) * 128], masks_dram[i])
    if ident_dram is not None:
        c.ident = g.sb("ident", [128, 128], BF16)
        c.ev_masks = g.loadc(c.ident[:], ident_dram)
    return c


def xs(c, ch, t0, n):
    return c.x[:, ch * T + t0: ch * T + t0 + n]


def aos(c, ch, t0, n):
    return c.ao[:, ch * T + t0: ch * T + t0 + n]


def emit_load_stream(g, c, xin, fp32_src):
    ev = None
    if fp32_src:
        for ch in range(CH):
            ev = g.loadc(xs(c, ch, 0, T), xin[ch * 128:(ch + 1) * 128, :])
    else:
        for i in range(4):
            dst = c.x[:, i * 4 * T:(i + 1) * 4 * T].rearrange("p (c t) -> p c t", t=T)
            s = xin.rearrange("(c p) t -> c p t", p=128)[i * 4:(i + 1) * 4]
            ev = g.load(dst, s.rearrange("c p t -> p c t"))
    c.ev_stream = ev
    return ev


def emit_store_stream(g, c, xout, waits):
    ev = None
    for i in range(4):
        s = c.x[:, i * 4 * T:(i + 1) * 4 * T].rearrange("p (c t) -> p c t", t=T)
        d = xout.rearrange("(c p) t -> c p t", p=128)[i * 4:(i + 1) * 4]
        ev = g.load(d.rearrange("c p t -> p c t"), s, waits=waits)
    return ev


def emit_ln(g, c, ev_r_ready, final_out=None):
    es = ExitStack()
    psum_s = g.ps("ln_sum", [128, TAU], es)
    psum_q = g.ps("ln_sq", [128, TAU], es)
    r2 = [g.sb("ln_r2", [128, TAU], BF16, es) for _ in range(2)]
    m_bc = g.sb("ln_m", [128, TAU], F32, es)
    tmp = g.sb("ln_t", [128, TAU], F32, es)
    rstd = g.sb("ln_rs", [128, TAU], F32, es)
    t1 = g.sb("ln_t1", [128, TAU], F32, es)
    if final_out is not None:
        outf = [g.sb("ln_of", [128, TAU], F32, es) for _ in range(2)]
    ev_all = []
    prev_stats_consumed = []
    prev_dma = {}
    for tau in range(NTAU):
        t0 = tau * TAU
        ev_sq = [None] * CH
        ev_mmq = [None] * CH
        last_sum = None
        for chn in range(CH):
            ev_sq[chn] = g.op(
                "scalar",
                lambda e, ch=chn, t=t0: e.activation(r2[ch % 2][:], xs(c, ch, t, TAU), AF.Square),
                waits=list(ev_r_ready) + ([ev_mmq[chn - 2]] if chn >= 2 else []),
                inc=True)
            last_sum = mm(g, psum_s[:], c.allones[:], xs(c, chn, t0, TAU),
                          chn == 0, chn == CH - 1,
                          waits=(list(ev_r_ready) + prev_stats_consumed) if chn == 0 else (),
                          inc=(chn == CH - 1))
            ev_mmq[chn] = mm(g, psum_q[:], c.allones[:], r2[chn % 2][:],
                             chn == 0, chn == CH - 1,
                             waits=[ev_sq[chn]] + (prev_stats_consumed if chn == 0 else []),
                             inc=True)
        last_sq = ev_mmq[CH - 1]
        e1 = g.op("vector",
                  lambda e: e.tensor_scalar(m_bc[:], psum_s[:], INV_D, None, ALU.mult),
                  waits=[last_sum], inc=True)
        e2 = g.op("vector", lambda e: e.tensor_mul(tmp[:], m_bc[:], m_bc[:]),
                  waits=[e1], inc=True)
        e3 = g.op("vector",
                  lambda e: e.scalar_tensor_tensor(tmp[:], psum_q[:], INV_D, tmp[:],
                                                   ALU.mult, ALU.subtract),
                  waits=[last_sq, e2], inc=True)
        e4 = g.op("scalar", lambda e: e.activation(tmp[:], tmp[:], AF.Sqrt,
                                                   bias=c.cpool[:, 2:3]),
                  waits=[e3], inc=True)
        e5 = g.op("vector", lambda e: e.reciprocal(rstd[:], tmp[:]), waits=[e4], inc=True)
        prev_stats_consumed = [e1, e3]
        for chn in range(CH):
            ea = g.op("vector",
                      lambda e, ch=chn, t=t0: e.tensor_sub(t1[:], xs(c, ch, t, TAU), m_bc[:]),
                      waits=[e5, last_sq, last_sum, ev_sq[CH - 1]], inc=True)
            if final_out is None:
                eb = g.op("vector",
                          lambda e, ch=chn, t=t0: e.tensor_mul(xs(c, ch, t, TAU), t1[:], rstd[:]),
                          waits=[ea], inc=True)
                ev_all.append(eb)
            else:
                eb = g.op("vector",
                          lambda e, ch=chn: e.tensor_mul(outf[ch % 2][:], t1[:], rstd[:]),
                          waits=[ea, prev_dma.get(chn - 2)], inc=True)
                prev_dma[chn] = g.store(final_out[chn * 128:(chn + 1) * 128, t0:t0 + TAU],
                                        outf[chn % 2][:], waits=[eb])
                ev_all.append(prev_dma[chn])
        prev_dma = {chn - CH: v for chn, v in prev_dma.items()} if final_out is not None else {}
    es.close()
    return ev_all


def emit_outproj(g, c, w_dram, bias_col, final_out=None, prep=None):
    es = ExitStack()
    wom = [g.sb("wo_m", [128, CH * 128], BF16, es) for _ in range(2)]
    psY = [g.ps("ps_y", [128, T], es) for _ in range(2)]
    ev_w = {}
    ev_mm = {}
    ev_evac = {}
    for m in range(2):
        ev_w[m] = g.loadc(wom[m][:], w_dram[m])
    for m in range(CH):
        e_last = None
        for chn in range(CH):
            for tau in range(NTAU):
                e_last = mm(g, psY[m % 2][:, tau * TAU:(tau + 1) * TAU],
                            wom[m % 2][:, chn * 128:(chn + 1) * 128],
                            aos(c, chn, tau * TAU, TAU),
                            chn == 0, chn == CH - 1,
                            waits=([ev_w[m], ev_evac.get(m - 2)]
                                   if (chn == 0 and tau == 0) else ()),
                            inc=(chn == CH - 1 and tau == NTAU - 1))
        ev_mm[m] = e_last
        if m + 2 < CH:
            ev_w[m + 2] = g.loadc(wom[m % 2][:], w_dram[m + 2], waits=[ev_mm[m]])
        eev = None
        for tau in range(NTAU):
            if bias_col is None:
                eev = g.op("vector",
                           lambda e, mm_=m, tt=tau: e.tensor_add(
                               xs(c, mm_, tt * TAU, TAU),
                               psY[mm_ % 2][:, tt * TAU:(tt + 1) * TAU],
                               xs(c, mm_, tt * TAU, TAU)),
                           waits=[ev_mm[m]], inc=True)
            else:
                eev = g.op("vector",
                           lambda e, mm_=m, tt=tau: e.scalar_tensor_tensor(
                               xs(c, mm_, tt * TAU, TAU),
                               psY[mm_ % 2][:, tt * TAU:(tt + 1) * TAU],
                               bias_col[:, mm_:mm_ + 1], xs(c, mm_, tt * TAU, TAU),
                               ALU.add, ALU.add),
                           waits=[ev_mm[m]], inc=True)
        ev_evac[m] = eev
        if prep is not None:
            prep.drive(6)
    es.close()
    return emit_ln(g, c, [ev_evac[m] for m in range(CH)], final_out)


def emit_l0_attn(g, c, W, dbg_kv=None, prep=None, prep_ctl=None):
    es = ExitStack()
    kv = g.sb("kv_sb", [128, LATC * T], BF16, es)
    kv_es = ExitStack()
    wkvb = g.sb("wkvb", [128, LATC * CH * 128], BF16, kv_es)
    pskv = [g.ps("ps_kv", [128, TAU], kv_es) for _ in range(2)]
    ev_wkv = g.loadc(wkvb[:], W["wkv"], waits=[c.ev_init])
    ev_kv_ev = {}
    ev_kv_mm_last = None
    step = 0
    for lc in range(LATC):
        for tau in range(NTAU):
            b = step % 2
            last = None
            for chn in range(CH):
                last = mm(g, pskv[b][:],
                          wkvb[:, (lc * CH + chn) * 128:(lc * CH + chn + 1) * 128],
                          xs(c, chn, tau * TAU, TAU), chn == 0, chn == CH - 1,
                          waits=([ev_wkv, c.ev_stream, ev_kv_ev.get(step - 2)]
                                 if chn == 0 else ()),
                          inc=(chn == CH - 1))
            ev_kv_mm_last = last
            ev_kv_ev[step] = g.op(
                "scalar",
                lambda e, l=lc, tt=tau, bb=b: e.activation(
                    kv[:, l * T + tt * TAU: l * T + (tt + 1) * TAU], pskv[bb][:], AF.Copy),
                waits=[last], inc=True)
            step += 1
    ev_kv_done = ev_kv_ev[step - 1]
    if dbg_kv is not None:
        for i in range(LATC):
            g.store(dbg_kv[i * 128:(i + 1) * 128, :], kv[:, i * T:(i + 1) * T],
                    waits=[ev_kv_done])
    kv_es.close()

    wqh = [g.sb("wq_h", [128, CH * 128], BF16, es) for _ in range(2)]
    wkh = [g.sb("wk_h", [128, LATC * 128], BF16, es) for _ in range(2)]
    wvh = [g.sb("wv_h", [128, LATC * 128], BF16, es) for _ in range(2)]
    q_h = g.sb("q_h", [128, T], BF16, es)
    k_h = g.sb("k_h", [128, T], BF16, es)
    v_h = g.sb("v_h", [128, T], BF16, es)
    pT = [g.sb("pT", [128, 384], BF16, es) for _ in range(3)]
    rcp = g.sb("rcp", [128, 128], F32, es)
    psp = [g.ps("ps_p", [128, TAU], es) for _ in range(2)]
    psS = [g.ps("ps_s", [128, 384], es) for _ in range(3)]
    psR = [g.ps("ps_r", [128, 128], es)[:]]
    psA = [g.ps("ps_a", [128, 128], es)[:]]

    def load_head_w(h, waits):
        b = h % 2
        e1 = g.loadc(wqh[b][:], W["wq"][h], waits=waits)
        e2 = g.loadc(wkh[b][:], W["wk"][h], waits=waits)
        e3 = g.loadc(wvh[b][:], W["wv"][h], waits=waits)
        return [e1, e2, e3]

    ev_wh = {0: load_head_w(0, [ev_kv_mm_last]), 1: load_head_w(1, [ev_kv_mm_last])}
    ev_pev = {}
    ev_projdone = {}
    ev_exp = {}
    ev_R = {}
    ev_AV = {}
    ev_rcp = {}
    ev_ao = {}
    gstep = 0
    for h in range(NH):
        b = h % 2
        first_evs = []
        for tau in range(NTAU):
            pb = gstep % 2
            last = None
            for chn in range(CH):
                w0 = []
                if chn == 0:
                    w0 = ev_wh[h] + [ev_pev.get(gstep - 2)]
                    if h == 0 and tau == 0:
                        w0.append(ev_kv_done)
                last = mm(g, psp[pb][:], wqh[b][:, chn * 128:(chn + 1) * 128],
                          xs(c, chn, tau * TAU, TAU), chn == 0, chn == CH - 1,
                          waits=w0, inc=(chn == CH - 1))
            ev_pev[gstep] = g.op(
                "scalar",
                lambda e, tt=tau, bb=pb: e.activation(q_h[:, tt * TAU:(tt + 1) * TAU],
                                                      psp[bb][:], AF.Copy),
                waits=[last], inc=True)
            first_evs.append(ev_pev[gstep])
            gstep += 1
        for tau in range(NTAU):
            pb = gstep % 2
            last = None
            for lc in range(LATC):
                last = mm(g, psp[pb][:], wkh[b][:, lc * 128:(lc + 1) * 128],
                          kv[:, lc * T + tau * TAU: lc * T + (tau + 1) * TAU],
                          lc == 0, lc == LATC - 1,
                          waits=([ev_pev.get(gstep - 2)] if lc == 0 else ()),
                          inc=(lc == LATC - 1))
            ev_pev[gstep] = g.op(
                "scalar",
                lambda e, tt=tau, bb=pb: e.activation(k_h[:, tt * TAU:(tt + 1) * TAU],
                                                      psp[bb][:], AF.Copy),
                waits=[last], inc=True)
            first_evs.append(ev_pev[gstep])
            gstep += 1
        for grp in range(4):
            pb = gstep % 2
            last = None
            for j in range(4):
                kc = grp * 4 + j
                for lc in range(LATC):
                    last = mm(g, psp[pb][:, j * 128:(j + 1) * 128],
                              kv[:, lc * T + kc * 128: lc * T + (kc + 1) * 128],
                              wvh[b][:, lc * 128:(lc + 1) * 128],
                              lc == 0, lc == LATC - 1,
                              waits=([ev_pev.get(gstep - 2)] if (j == 0 and lc == 0) else ()),
                              inc=(j == 3 and lc == LATC - 1))
            ev_pev[gstep] = g.op(
                "vector",
                lambda e, gg=grp, bb=pb: e.tensor_copy(v_h[:, gg * TAU:(gg + 1) * TAU],
                                                       psp[bb][:]),
                waits=[last], inc=True)
            first_evs.append(ev_pev[gstep])
            gstep += 1
        ev_projdone[h] = ev_pev[gstep - 1]
        if h + 2 < NH:
            ev_wh[h + 2] = load_head_w(h + 2, [ev_projdone[h]])
        if prep is not None:
            if h == 11 and prep_ctl is not None:
                prep_ctl["gamma"] = True
            prep.drive(20)

        for q in range(CH):
            k_ = h * CH + q
            p2 = k_ % 3
            qsl = q_h[:, q * 128:(q + 1) * 128]
            tiles = [(psS[p2][:, 0:128], k_h[:, q * 128:(q + 1) * 128], 0,
                      (128, 0, 128), c.allones[:], v_h[:, q * 128:(q + 1) * 128])]
            if q >= 1:
                tiles.append((psS[p2][:, 128:256], k_h[:, (q - 1) * 128:q * 128],
                              2 if q == 1 else 1,
                              (128, 128, 256), c.allones[:], v_h[:, (q - 1) * 128:q * 128]))
            if q >= 2:
                tiles.append((psS[p2][0:16, 256:384], k_h[:, 0:16], None,
                              (16, 256, 384), c.allones[0:16, :], v_h[0:16, 0:128]))
            n_t = len(tiles)
            ev_S = None
            for i, (pslice, lh, mi, _, _, _) in enumerate(tiles):
                w0 = []
                if i == 0:
                    w0 = ([ev_exp.get(k_ - 3)] + (first_evs if q == 0 else []))
                ev_S = mm(g, pslice, lh, qsl, True, mi is None, waits=w0,
                          inc=(mi is None and i == n_t - 1))
                if mi is not None:
                    ev_S = mm(g, pslice, c.masks[:, mi * 128:(mi + 1) * 128], c.ident[:],
                              False, True,
                              waits=([c.ev_masks] if k_ == 0 else ()),
                              inc=(i == n_t - 1))
            war_pT = [ev_AV.get(k_ - 3), ev_R.get(k_ - 3)]
            nse = 128 if q == 0 else 256
            e_exp = g.op(
                "scalar",
                lambda e, p=p2, n=nse: e.activation(pT[p][:, 0:n], psS[p][:, 0:n], AF.Exp,
                                                    bias=c.cpool[:, 0:1],
                                                    scale=c.cpool[:, 1:2]),
                waits=[ev_S] + war_pT, inc=True)
            if q >= 2:
                e_exp = g.op(
                    "scalar",
                    lambda e, p=p2: e.activation(pT[p][0:16, 256:384], psS[p][0:16, 256:384],
                                                 AF.Exp, bias=c.cpool[0:16, 0:1],
                                                 scale=c.cpool[0:16, 1:2]),
                    inc=True)
            ev_exp[k_] = e_exp
            ev_r = None
            for i, (_, _, _, (np_, c0, c1), ao_lh, _) in enumerate(tiles):
                pts = pT[p2][0:np_, c0:c1]
                ev_r = mm(g, psR[0], ao_lh, pts, i == 0, i == n_t - 1,
                          waits=([ev_exp[k_], ev_rcp.get(k_ - 1)] if i == 0 else ()),
                          inc=(i == n_t - 1))
            ev_R[k_] = ev_r
            ev_a = None
            for i, (_, _, _, (np_, c0, c1), _, v_lh) in enumerate(tiles):
                pts = pT[p2][0:np_, c0:c1]
                ev_a = mm(g, psA[0], v_lh, pts, i == 0, i == n_t - 1,
                          waits=([ev_exp[k_], ev_ao.get(k_ - 1)] if i == 0 else ()),
                          inc=(i == n_t - 1))
            ev_AV[k_] = ev_a
            ev_rcp[k_] = g.op("vector", lambda e, pr_=psR[0]: e.reciprocal(rcp[:], pr_),
                              waits=[ev_R[k_]], inc=True)
            ev_ao[k_] = g.op(
                "vector",
                lambda e, pa_=psA[0], hh=h, qq=q: e.tensor_mul(
                    c.ao[:, hh * T + qq * 128: hh * T + (qq + 1) * 128], pa_, rcp[:]),
                waits=[ev_AV[k_], ev_rcp[k_]], inc=True)
    es.close()




def hs_(c, i, n=TAU):
    return c.ao[:, i * TAU:(i) * TAU + n]


def gen_quantize_prep(g, c, specs, out_evs, pes=None, ctl=None):
    SW = 2048
    stg = [g.sb("qp_stg", [128, SW], F32, pes) for _ in range(2)]
    nb = g.sb("qp_n", [128, SW], BF16, pes)
    qb = [g.sb("qp_q", [128, SW], BF16, pes) for _ in range(2)]
    onescol = g.sb("qp_onc", [128, 1], F32, pes)
    onesrow = g.sb("qp_onr", [1, 128], F32, pes)
    tot = g.sb("qp_tot", [1, 1], F32, pes)
    state = {}
    for key, w_dram, scr_dram, n_rc, n_cols in specs:
        nslab = [(j, min(SW, n_cols - j)) for j in range(0, n_cols, SW)]
        slabs = [(rc, j0, w) for rc in range(n_rc) for (j0, w) in nslab]
        partials = g.sb("qp_part", [128, len(slabs)], F32, pes)
        thr = g.sb("qp_th", [128, 1], F32, pes)
        nthr = g.sb("qp_nth", [128, 1], F32, pes)
        gg = g.sb("qp_gg", [128, 1], F32, pes)
        ptot = g.sb("qp_pt", [128, 1], F32, pes)
        state[key] = [slabs, partials, thr, nthr, gg, ptot, nslab]
    e_on = g.op("vector", lambda e: e.memset(onescol[:], 1.0), inc=True)
    e_on = g.op("vector", lambda e: e.memset(onesrow[:], 1.0), inc=True)
    yield
    ev_ld = {}
    ev_red = {}
    u = 0
    for key, w_dram, scr_dram, n_rc, n_cols in specs:
        slabs, partials, ginv, gninv, gg, ptot, nslab = state[key]
        for si, (rc, j0, wd) in enumerate(slabs):
            b = u % 2
            ev_ld[u] = g.load(stg[b][:, 0:wd], w_dram[rc * 128:(rc + 1) * 128, j0:j0 + wd],
                              waits=[ev_red.get(u - 2)])
            ev_red[u] = g.op(
                "vector",
                lambda e, s=si, b_=b, w_=wd, pt_=partials: e.tensor_reduce(
                    pt_[:, s:s + 1], stg[b_][:, 0:w_], mybir.AxisListType.X, ALU.add,
                    apply_absolute_value=True),
                waits=[ev_ld[u]], inc=True)
            u += 1
            yield
    ev_red_last = ev_red[u - 1]
    while ctl is not None and not ctl.get("gamma", False):
        yield "wait"
    for key, w_dram, scr_dram, n_rc, n_cols in specs:
        slabs, partials, thr, nthr, gg, ptot, nslab = state[key]
        ges = ExitStack()
        psg = g.ps("qp_ps", [128, 2], ges)
        psg1 = psg[0:1, 0:1]
        psg2 = psg[:, 1:2]
        e_pt = g.op("vector",
                    lambda e, n_=len(slabs), pt_=partials, o_=ptot: e.tensor_reduce(
                        o_[:], pt_[:, 0:n_], mybir.AxisListType.X, ALU.add),
                    waits=[ev_red_last], inc=True)
        e_m1 = mm(g, psg1, ptot[:], onescol[:], True, True, waits=[e_pt, e_on], inc=True)
        e_tt = g.op("vector", lambda e, t_=tot, p_=psg1: e.tensor_copy(t_[:], p_),
                    waits=[e_m1], inc=True)
        e_m2 = mm(g, psg2, onesrow[:], tot[:], True, True, waits=[e_tt], inc=True)
        inv_n = 1.0 / (n_rc * 128 * n_cols)
        e_g1 = g.op("vector",
                    lambda e, g_=gg, p_=psg2, iv=inv_n: e.tensor_scalar(
                        g_[:], p_, iv, Q_EPS, ALU.mult, ALU.add),
                    waits=[e_m2], inc=True)
        e_g2 = g.op("vector",
                    lambda e, t_=thr, g_=gg: e.tensor_scalar(t_[:], g_[:], 0.5, None,
                                                             ALU.mult),
                    waits=[e_g1], inc=True)
        e_g3 = g.op("vector",
                    lambda e, t_=nthr, g_=gg: e.tensor_scalar(t_[:], g_[:], -0.5, None,
                                                              ALU.mult),
                    waits=[e_g2], inc=True)
        state[key].append(e_g3)
        ges.close()
        yield
    ev_p = {}
    ev_n = {}
    ev_q = {}
    ev_st = {}
    u2 = 0
    work = []
    np_max = max(len(s[6]) for s in (state[k[0]] for k in specs[:2]))
    for pi in range(np_max):
        for key, w_dram, scr_dram, n_rc, n_cols in specs[:2]:
            if pi < len(state[key][6]):
                work.append((key, w_dram, scr_dram, n_rc, pi))
    for key, w_dram, scr_dram, n_rc, n_cols in specs[2:]:
        for pi in range(len(state[key][6])):
            work.append((key, w_dram, scr_dram, n_rc, pi))
    for key in ("w1", "w2", "w3"):
        out_evs[key] = []
    for key, w_dram, scr_dram, n_rc, pi in work:
        slabs, partials, thr, nthr, gg, ptot, nslab, e_g3 = state[key]
        if True:
            (j0, wd) = nslab[pi]
            last_store = None
            for rc in range(n_rc):
                b = u2 % 2
                e_ld = g.load(stg[b][:, 0:wd],
                              w_dram[rc * 128:(rc + 1) * 128, j0:j0 + wd],
                              waits=[e_g3 if u2 < 2 else None, ev_q.get(u2 - 2),
                                     ev_n.get(u2 - 2)])
                ev_n[u2] = g.op(
                    "vector",
                    lambda e, w_=wd, b_=b, t_=nthr: e.tensor_scalar(
                        nb[:, 0:w_], stg[b_][:, 0:w_], t_[:, 0:1], None, ALU.is_lt),
                    waits=[e_ld, e_g3, ev_q.get(u2 - 1)], inc=True)
                ev_q[u2] = g.op(
                    "vector",
                    lambda e, w_=wd, b_=b, t_=thr: e.scalar_tensor_tensor(
                        qb[b_][:, 0:w_], stg[b_][:, 0:w_], t_[:, 0:1], nb[:, 0:w_],
                        ALU.is_gt, ALU.subtract),
                    waits=[ev_n[u2], ev_st.get(u2 - 2)], inc=True)
                n_i = wd // 128
                i0 = j0 // 128
                dst = scr_dram[i0:i0 + n_i, :, rc * 128:(rc + 1) * 128].rearrange(
                    "i p j -> p i j")
                ev_st[u2] = g.load(dst, qb[b][:, 0:wd].rearrange("p (i j) -> p i j", j=128),
                                   waits=[ev_q[u2]])
                last_store = ev_st[u2]
                u2 += 1
                yield
            out_evs[key].append(last_store)


class PrepDriver:
    def __init__(self, gen):
        self.gen = gen

    def drive(self, n):
        if self.gen is None:
            return
        for _ in range(n):
            try:
                tag = next(self.gen)
            except StopIteration:
                self.gen = None
                return
            if tag == "wait":
                return

    def finish(self):
        if self.gen is not None:
            for _ in self.gen:
                pass
            self.gen = None


def emit_l0_ffn(g, c, W, prep_evs, prep=None):
    def sev(key, chunk):
        want = chunk // 16
        while prep is not None and prep.gen is not None and \
                len(prep_evs.get(key, [])) <= want:
            prep.drive(1)
        lst = prep_evs.get(key, [])
        return lst[min(want, len(lst) - 1)] if lst else None
    es = ExitStack()
    w1b = [g.sb("l0f_w1", [128, CH * 128], BF16, es) for _ in range(2)]
    w2b = [g.sb("l0f_w2", [128, CH * 128], BF16, es) for _ in range(2)]
    w3b = [g.sb("l0f_w3", [128, L0HC * 128], BF16, es) for _ in range(2)]
    sg = g.sb("l0f_sg", [128, TAU], F32, es)
    psG = [g.ps("l0f_g", [128, TAU], es) for _ in range(2)]
    psU = [g.ps("l0f_u", [128, TAU], es) for _ in range(2)]
    psD = [g.ps("l0f_d", [128, TAU], es) for _ in range(2)]
    ev_evac_all = []
    ev_gmm = {}
    ev_sg = {}
    ev_h = {}
    ev_w = {}
    ev_dmm = {}
    ev_dev = {}
    step = 0
    dstep = 0
    for tau in range(NTAU):
        t0 = tau * TAU
        for i in range(L0HC):
            b = step % 2
            if prep is not None:
                prep.drive(1)
            ev_w[step] = [
                g.load(w1b[b][:], W["w1q"][i], waits=[sev("w1", i), ev_gmm.get(step - 2)]),
                g.load(w2b[b][:], W["w2q"][i], waits=[sev("w2", i), ev_gmm.get(step - 2)]),
            ]
            lastg = lastu = None
            for chn in range(CH):
                lastg = mm(g, psG[b][:], w1b[b][:, chn * 128:(chn + 1) * 128],
                           xs(c, chn, t0, TAU), chn == 0, chn == CH - 1,
                           waits=(ev_w[step] + [ev_sg.get(step - 2)] if chn == 0 else ()),
                           inc=(chn == CH - 1))
                lastu = mm(g, psU[b][:], w2b[b][:, chn * 128:(chn + 1) * 128],
                           xs(c, chn, t0, TAU), chn == 0, chn == CH - 1,
                           waits=([ev_h.get(step - 2)] if chn == 0 else ()),
                           inc=(chn == CH - 1))
            ev_gmm[step] = lastu
            ev_sg[step] = g.op("scalar",
                               lambda e, b_=b: e.activation(sg[:], psG[b_][:], AF.Sigmoid),
                               waits=[lastg, ev_h.get(step - 1)], inc=True)
            ev_h[step] = g.op(
                "vector",
                lambda e, b_=b, ii=i: e.tensor_mul(hs_(c, ii), sg[:], psU[b_][:]),
                waits=[lastu, ev_sg[step]], inc=True)
            step += 1
        ev_h_tau = [ev_h[step - 1 - k] for k in range(L0HC)]
        for m in range(CH):
            b = dstep % 2
            if prep is not None:
                prep.drive(1)
            e_w3 = g.load(w3b[b][:], W["w3q"][m],
                          waits=[sev("w3", m), ev_dmm.get(dstep - 2)])
            last = None
            for ic in range(L0HC):
                last = mm(g, psD[b][:], w3b[b][:, ic * 128:(ic + 1) * 128], hs_(c, ic),
                          ic == 0, ic == L0HC - 1,
                          waits=(([e_w3, ev_dev.get(dstep - 2)] + ev_h_tau)
                                 if ic == 0 else ()),
                          inc=(ic == L0HC - 1))
            ev_dmm[dstep] = last
            ev_dev[dstep] = g.op(
                "vector",
                lambda e, b_=b, m_=m, t=t0: e.tensor_add(
                    xs(c, m_, t, TAU), psD[b_][:], xs(c, m_, t, TAU)),
                waits=[last], inc=True)
            ev_evac_all.append(ev_dev[dstep])
            dstep += 1
    es.close()
    return ev_evac_all


def emit_l1_attn(g, c, W):
    es = ExitStack()
    bq = g.sb("l1_bq", [128, CH], F32, es)
    bk = g.sb("l1_bk", [128, CH], F32, es)
    ev_b1 = g.load(bq[:], W["bq"])
    ev_b2 = g.load(bk[:], W["bk"])
    wqkvb = [g.sb("l1_wqkv", [128, CH * 384], BF16, es) for _ in range(2)]
    bvr = [g.sb("l1_bvr", [1, 128], BF16, es) for _ in range(2)]
    q_h = g.sb("l1_qh", [128, T], BF16, es)
    k_h = g.sb("l1_kh", [128, T], BF16, es)
    v_h = g.sb("l1_vh", [128, T], BF16, es)
    pTl = [g.sb("l1_pT", [128, 1024], BF16, es) for _ in range(2)]
    rcb = g.sb("l1_rcb", [128, 1024], F32, es)

    def load_head_w(h, waits):
        b = h % 2
        e1 = g.loadc(wqkvb[b][:], W["wqkv"][h], waits=waits)
        e2 = g.loadc(bvr[b][:], W["bvr"][h:h + 1, :], waits=waits)
        return [e1, e2]

    ev_wh = {0: load_head_w(0, [c.ev_init]), 1: load_head_w(1, [c.ev_init])}
    ev_projdone = {}
    ev_head_attn_done = {}
    for h in range(NH):
        b = h % 2
        pes = ExitStack()
        psp = [g.ps("l1_pp", [128, TAU], pes) for _ in range(2)]
        first_evs = []
        ev_pev = {}
        pstep = 0
        prev_attn = ev_head_attn_done.get(h - 1, [])
        for which, (off, bias_t) in enumerate(((0, bq), (128, bk))):
            for tau in range(NTAU):
                pb = pstep % 2
                last = None
                for chn in range(CH):
                    w0 = []
                    if chn == 0:
                        w0 = ev_wh[h] + [ev_pev.get(pstep - 2), ev_b1, ev_b2] + \
                             (prev_attn if pstep == 0 else []) + \
                             ([ev_projdone[h - 2]] if h >= 2 else [])
                    last = mm(g, psp[pb][:],
                              wqkvb[b][:, chn * 384 + off: chn * 384 + off + 128],
                              xs(c, chn, tau * TAU, TAU), chn == 0, chn == CH - 1,
                              waits=w0, inc=(chn == CH - 1))
                dst = q_h if which == 0 else k_h
                ev_pev[pstep] = g.op(
                    "vector",
                    lambda e, tt=tau, ps_=psp[pb], d=dst, bt=bias_t, hh=h: e.tensor_scalar_add(
                        d[:, tt * TAU:(tt + 1) * TAU], ps_[:], bt[:, hh:hh + 1]),
                    waits=[last], inc=True)
                first_evs.append(ev_pev[pstep])
                pstep += 1
        for grp in range(4):
            pb = pstep % 2
            last = None
            for j in range(4):
                kc = grp * 4 + j
                for chn in range(CH):
                    last = mm(g, psp[pb][:, j * 128:(j + 1) * 128],
                              xs(c, chn, kc * 128, 128),
                              wqkvb[b][:, chn * 384 + 256: chn * 384 + 384],
                              chn == 0, False,
                              waits=([ev_pev.get(pstep - 2)] if (j == 0 and chn == 0) else ()),
                              inc=False)
                last = mm(g, psp[pb][:, j * 128:(j + 1) * 128], c.ones1[:], bvr[b][:],
                          False, True, inc=(j == 3))
            ev_pev[pstep] = g.op(
                "vector",
                lambda e, gg=grp, ps_=psp[pb]: e.tensor_copy(v_h[:, gg * TAU:(gg + 1) * TAU],
                                                             ps_[:]),
                waits=[last], inc=True)
            first_evs.append(ev_pev[pstep])
            pstep += 1
        ev_projdone[h] = ev_pev[pstep - 1]
        pes.close()
        if h + 2 < NH:
            ev_wh[h + 2] = load_head_w(h + 2, [ev_projdone[h]])
        aes = ExitStack()
        psS = [g.ps("l1_s", [128, 1024], aes) for _ in range(2)]
        psR = g.ps("l1_r", [128, 1024], aes)
        psA = g.ps("l1_a", [128, 1024], aes)
        ev_half = []
        for half in range(2):
            q0 = half * 1024
            ev_S = {}
            ev_exp = {}
            ev_Rmm = {}
            ev_Amm = {}
            for kc in range(CH):
                p2 = kc % 2
                last = None
                for n in range(2):
                    w0 = []
                    if n == 0:
                        w0 = [ev_exp.get(kc - 2)]
                        if kc == 0:
                            w0 += first_evs + ev_half + prev_attn
                    last = mm(g, psS[p2][:, n * 512:(n + 1) * 512],
                              k_h[:, kc * 128:(kc + 1) * 128],
                              q_h[:, q0 + n * 512: q0 + (n + 1) * 512],
                              True, True, waits=w0, inc=(n == 1))
                ev_S[kc] = last
                ev_exp[kc] = g.op(
                    "scalar",
                    lambda e, pt_=pTl[p2], ss_=psS[p2]: e.activation(
                        pt_[:], ss_[:], AF.Exp, bias=c.cpool[:, 0:1],
                        scale=c.cpool[:, 1:2]),
                    waits=[ev_S[kc], ev_Rmm.get(kc - 2), ev_Amm.get(kc - 2)], inc=True)
                last = None
                for n in range(2):
                    last = mm(g, psR[:, n * 512:(n + 1) * 512], c.allones[:],
                              pTl[p2][:, n * 512:(n + 1) * 512], kc == 0, kc == CH - 1,
                              waits=([ev_exp[kc]] + (ev_half if kc == 0 else []))
                              if n == 0 else (),
                              inc=(n == 1))
                ev_Rmm[kc] = last
                last = None
                for n in range(2):
                    last = mm(g, psA[:, n * 512:(n + 1) * 512],
                              v_h[:, kc * 128:(kc + 1) * 128],
                              pTl[p2][:, n * 512:(n + 1) * 512], kc == 0, kc == CH - 1,
                              waits=() if n == 1 else [ev_exp[kc]],
                              inc=(n == 1))
                ev_Amm[kc] = last
            e_rc = g.op("vector", lambda e, pr_=psR: e.reciprocal(rcb[:], pr_[:]),
                        waits=[ev_Rmm[CH - 1]], inc=True)
            e_ao = g.op(
                "vector",
                lambda e, hh=h, q0_=q0, pa_=psA: e.tensor_mul(
                    c.ao[:, hh * T + q0_: hh * T + q0_ + 1024], pa_[:], rcb[:]),
                waits=[ev_Amm[CH - 1], e_rc], inc=True)
            ev_half = [e_rc, e_ao, ev_exp[CH - 1], ev_exp[CH - 2]]
        ev_head_attn_done[h] = ev_half
        aes.close()
    es.close()




def emit_l1_ffn(g, c, W):
    es = ExitStack()
    gs = g.sb("l1f_gs", [128, L1HC], F32, es)
    us = g.sb("l1f_us", [128, L1HC], F32, es)
    ds = g.sb("l1f_ds", [128, CH], F32, es)
    e_s1 = g.load(gs[:], W["gs"])
    e_s2 = g.load(us[:], W["us"])
    e_s3 = g.load(ds[:], W["ds"])
    ev_scales = [e_s1, e_s2, e_s3]
    gwb = [g.sb("l1f_gw", [128, CH * 128], BF16, es) for _ in range(2)]
    uwb = [g.sb("l1f_uw", [128, CH * 128], BF16, es) for _ in range(2)]
    dwb = [g.sb("l1f_dw", [128, L1HC * 128], BF16, es) for _ in range(2)]
    rs_bc = g.sb("l1f_rs", [128, TAU], F32, es)
    rsh_bc = g.sb("l1f_rsh", [128, TAU], F32, es)
    sgt = g.sb("l1f_sg", [128, TAU], F32, es)
    gin = g.sb("l1f_gin", [128, TAU], F32, es)
    uin = g.sb("l1f_uin", [128, TAU], F32, es)
    dtmp = g.sb("l1f_dt", [128, TAU], F32, es)
    psRS = g.ps("l1f_prs", [128, TAU], es)
    psG = [g.ps("l1f_pg", [128, TAU], es) for _ in range(2)]
    psU = [g.ps("l1f_pu", [128, TAU], es) for _ in range(2)]
    psD = [g.ps("l1f_pd", [128, TAU], es) for _ in range(2)]
    ev_evac_all = []
    ev_gmm = {}
    ev_sg = {}
    ev_h = {}
    ev_gsub = {}
    ev_usub = {}
    ev_dsub = {}
    ev_dmm = {}
    ev_dev = {}
    ev_rs_prev = None
    ev_rsh_prev = None
    step = 0
    dstep = 0
    for tau in range(NTAU):
        t0 = tau * TAU
        last = None
        for chn in range(CH):
            last = mm(g, psRS[:], c.allones[:], xs(c, chn, t0, TAU), chn == 0, chn == CH - 1,
                      waits=([c.ev_stream, c.ev_init, ev_rsh_prev] if chn == 0 else ()),
                      inc=(chn == CH - 1))
        ev_rs = g.op("vector",
                     lambda e: e.tensor_scalar(rs_bc[:], psRS[:], 128.0, None, ALU.mult),
                     waits=[last, ev_rs_prev], inc=True)
        ev_rs_prev = ev_rs
        for i in range(L1HC):
            b = step % 2
            ev_w = [
                g.loadc(gwb[b][:], W["gw"][i], waits=[ev_gmm.get(step - 2)]),
                g.loadc(uwb[b][:], W["uw"][i], waits=[ev_gmm.get(step - 2)]),
            ]
            lastg = lastu = None
            for chn in range(CH):
                lastg = mm(g, psG[b][:], gwb[b][:, chn * 128:(chn + 1) * 128],
                           xs(c, chn, t0, TAU), chn == 0, chn == CH - 1,
                           waits=(ev_w + [ev_gsub.get(step - 2)] if chn == 0 else ()),
                           inc=(chn == CH - 1))
            for chn in range(CH):
                lastu = mm(g, psU[b][:], uwb[b][:, chn * 128:(chn + 1) * 128],
                           xs(c, chn, t0, TAU), chn == 0, chn == CH - 1,
                           waits=([ev_usub.get(step - 2)] if chn == 0 else ()),
                           inc=(chn == CH - 1))
            ev_gmm[step] = lastu
            ev_gsub[step] = g.op(
                "vector",
                lambda e, b_=b: e.tensor_sub(gin[:], psG[b_][:], rs_bc[:]),
                waits=[lastg, ev_rs, ev_sg.get(step - 1)], inc=True)
            ev_sg[step] = g.op(
                "scalar",
                lambda e, ii=i: e.activation(sgt[:], gin[:], AF.Silu,
                                             scale=gs[:, ii:ii + 1]),
                waits=[ev_gsub[step], ev_scales[0], ev_h.get(step - 1)], inc=True)
            ev_usub[step] = g.op(
                "vector",
                lambda e, b_=b: e.tensor_sub(uin[:], psU[b_][:], rs_bc[:]),
                waits=[lastu, ev_rs, ev_h.get(step - 1)], inc=True)
            ev_h[step] = g.op(
                "vector",
                lambda e, ii=i: e.scalar_tensor_tensor(
                    hs_(c, ii), uin[:], us[:, ii:ii + 1], sgt[:], ALU.mult, ALU.mult),
                waits=[ev_usub[step], ev_sg[step], ev_scales[1]], inc=True)
            step += 1
        ev_h_tau = [ev_h[step - 1 - k] for k in range(L1HC)]
        last = None
        for ic in range(L1HC):
            last = mm(g, psRS[:], c.allones[:], hs_(c, ic), ic == 0, ic == L1HC - 1,
                      waits=(([ev_rs] + ev_h_tau) if ic == 0 else ()),
                      inc=(ic == L1HC - 1))
        ev_rsh = g.op("vector",
                      lambda e: e.tensor_scalar(rsh_bc[:], psRS[:], 128.0, None, ALU.mult),
                      waits=[last, ev_rsh_prev], inc=True)
        ev_rsh_prev = ev_rsh
        for m in range(CH):
            b = dstep % 2
            e_w3 = g.loadc(dwb[b][:], W["dw"][m], waits=[ev_dmm.get(dstep - 2)])
            last = None
            for ic in range(L1HC):
                last = mm(g, psD[b][:], dwb[b][:, ic * 128:(ic + 1) * 128], hs_(c, ic),
                          ic == 0, ic == L1HC - 1,
                          waits=(([e_w3, ev_dsub.get(dstep - 2)] + ev_h_tau)
                                 if ic == 0 else ()),
                          inc=(ic == L1HC - 1))
            ev_dmm[dstep] = last
            ev_dsub[dstep] = g.op(
                "vector",
                lambda e, b_=b: e.tensor_sub(dtmp[:], psD[b_][:], rsh_bc[:]),
                waits=[last, ev_rsh, ev_dev.get(dstep - 1)], inc=True)
            ev_dev[dstep] = g.op(
                "vector",
                lambda e, m_=m, t=t0: e.scalar_tensor_tensor(
                    xs(c, m_, t, TAU), dtmp[:], ds[:, m_:m_ + 1], xs(c, m_, t, TAU),
                    ALU.mult, ALU.add),
                waits=[ev_dsub[dstep], ev_scales[2]], inc=True)
            ev_evac_all.append(ev_dev[dstep])
            dstep += 1
    es.close()
    return ev_evac_all


def _declare_l0a(nc, first):
    t = {}
    t["xin"] = nc.dram_tensor("xin", [D, T], F32 if first else BF16, kind="ExternalInput").ap()
    t["wq"] = nc.dram_tensor("wq", [NH, 128, CH * 128], F32, kind="ExternalInput").ap()
    t["wkv"] = nc.dram_tensor("wkv", [128, LATC * CH * 128], F32, kind="ExternalInput").ap()
    t["wk"] = nc.dram_tensor("wk", [NH, 128, LATC * 128], F32, kind="ExternalInput").ap()
    t["wv"] = nc.dram_tensor("wv", [NH, 128, LATC * 128], F32, kind="ExternalInput").ap()
    t["wo"] = nc.dram_tensor("wo", [CH, 128, CH * 128], F32, kind="ExternalInput").ap()
    t["masks"] = nc.dram_tensor("masks", [3, 128, 128], F32, kind="ExternalInput").ap()
    t["ident"] = nc.dram_tensor("ident", [128, 128], F32, kind="ExternalInput").ap()
    return t


def _declare_l0f(nc):
    t = {}
    t["w1T"] = nc.dram_tensor("w1T", [D, L0H], F32, kind="ExternalInput").ap()
    t["w2T"] = nc.dram_tensor("w2T", [D, L0H], F32, kind="ExternalInput").ap()
    t["w3T"] = nc.dram_tensor("w3T", [L0H, D], F32, kind="ExternalInput").ap()
    t["w1q"] = nc.dram_tensor("w1q", [L0HC, 128, CH * 128], BF16, kind="Internal").ap()
    t["w2q"] = nc.dram_tensor("w2q", [L0HC, 128, CH * 128], BF16, kind="Internal").ap()
    t["w3q"] = nc.dram_tensor("w3q", [CH, 128, L0HC * 128], BF16, kind="Internal").ap()
    return t


def build_l0_ffn_launch(first=False, last=False):
    nc = bass.Bass("TRN2", target_bir_lowering=False, debug=False)
    t = _declare_l0f(nc)
    xin = nc.dram_tensor("xin", [D, T], F32 if first else BF16, kind="ExternalInput").ap()
    xo = nc.dram_tensor("xo", [D, T], F32 if last else BF16, kind="ExternalOutput").ap()
    g = G(nc)
    c = setup_common(g)
    emit_load_stream(g, c, xin, first)
    prep_evs = {}
    prep_es = ExitStack()
    pd = PrepDriver(gen_quantize_prep(g, c, [
        ("w1", t["w1T"], t["w1q"], CH, L0H),
        ("w2", t["w2T"], t["w2q"], CH, L0H),
        ("w3", t["w3T"], t["w3q"], L0HC, D),
    ], prep_evs, prep_es, {"gamma": True}))
    pd.finish()
    g.barrier()
    evs = emit_l0_ffn(g, c, t, prep_evs)
    ev_out = emit_ln(g, c, evs, final_out=(xo if last else None))
    if not last:
        ev_last = emit_store_stream(g, c, xo, waits=ev_out)
    else:
        ev_last = ev_out[-1]
    g.op("gpsimd", lambda e: e.engine_nop(), waits=[ev_last] + list(ev_out), inc=True)
    sems = {}
    sem_es = ExitStack()
    for s in ALL_SEMS:
        sems[s] = sem_es.enter_context(nc.semaphore(s))
    g.build(sems)
    g.es.close()
    sem_es.close()
    return nc


def _declare_l1a(nc):
    t = {}
    t["wqkv"] = nc.dram_tensor("wqkv", [NH, 128, CH * 384], F32, kind="ExternalInput").ap()
    t["bq"] = nc.dram_tensor("bq", [128, CH], F32, kind="ExternalInput").ap()
    t["bk"] = nc.dram_tensor("bk", [128, CH], F32, kind="ExternalInput").ap()
    t["bvr"] = nc.dram_tensor("bvr", [NH, 128], F32, kind="ExternalInput").ap()
    t["wout"] = nc.dram_tensor("wout", [CH, 128, CH * 128], F32, kind="ExternalInput").ap()
    t["bo"] = nc.dram_tensor("bo", [128, CH], F32, kind="ExternalInput").ap()
    return t


def build_l1_attn_launch(first=False, last=False):
    nc = bass.Bass("TRN2", target_bir_lowering=False, debug=False)
    t = _declare_l1a(nc)
    xin = nc.dram_tensor("xin", [D, T], F32 if first else BF16, kind="ExternalInput").ap()
    xo = nc.dram_tensor("xo", [D, T], F32 if last else BF16, kind="ExternalOutput").ap()
    g = G(nc)
    c = setup_common(g)
    bo = g.sb("l1_bo", [128, CH], F32)
    ev_bo = g.load(bo[:], t["bo"])
    emit_load_stream(g, c, xin, first)
    emit_l1_attn(g, c, t)
    g.barrier()
    ev_out = emit_outproj(g, c, t["wout"], bo, final_out=(xo if last else None))
    if not last:
        ev_last = emit_store_stream(g, c, xo, waits=ev_out)
    else:
        ev_last = ev_out[-1]
    g.op("gpsimd", lambda e: e.engine_nop(), waits=[ev_last] + list(ev_out), inc=True)
    sems = {}
    sem_es = ExitStack()
    for s in ALL_SEMS:
        sems[s] = sem_es.enter_context(nc.semaphore(s))
    g.build(sems)
    g.es.close()
    sem_es.close()
    return nc


def _declare_l1f(nc):
    t = {}
    t["gw"] = nc.dram_tensor("gw", [L1HC, 128, CH * 128], I32, kind="ExternalInput").ap()
    t["uw"] = nc.dram_tensor("uw", [L1HC, 128, CH * 128], I32, kind="ExternalInput").ap()
    t["dw"] = nc.dram_tensor("dw", [CH, 128, L1HC * 128], I32, kind="ExternalInput").ap()
    t["gs"] = nc.dram_tensor("gs", [128, L1HC], F32, kind="ExternalInput").ap()
    t["us"] = nc.dram_tensor("us", [128, L1HC], F32, kind="ExternalInput").ap()
    t["ds"] = nc.dram_tensor("ds", [128, CH], F32, kind="ExternalInput").ap()
    return t


def build_l1_ffn_launch(first=False, last=False):
    nc = bass.Bass("TRN2", target_bir_lowering=False, debug=False)
    t = _declare_l1f(nc)
    xin = nc.dram_tensor("xin", [D, T], F32 if first else BF16, kind="ExternalInput").ap()
    xo = nc.dram_tensor("xo", [D, T], F32 if last else BF16, kind="ExternalOutput").ap()
    g = G(nc)
    c = setup_common(g)
    emit_load_stream(g, c, xin, first)
    evs = emit_l1_ffn(g, c, t)
    ev_out = emit_ln(g, c, evs, final_out=(xo if last else None))
    if not last:
        ev_last = emit_store_stream(g, c, xo, waits=ev_out)
    else:
        ev_last = ev_out[-1]
    g.op("gpsimd", lambda e: e.engine_nop(), waits=[ev_last] + list(ev_out), inc=True)
    sems = {}
    sem_es = ExitStack()
    for s in ALL_SEMS:
        sems[s] = sem_es.enter_context(nc.semaphore(s))
    g.build(sems)
    g.es.close()
    sem_es.close()
    return nc


def build_fused_launch():
    nc = bass.Bass("TRN2", target_bir_lowering=False, debug=False)
    t0a = _declare_l0a(nc, True)
    t0f = _declare_l0f(nc)
    t1a = _declare_l1a(nc)
    t1f = _declare_l1f(nc)
    xo = nc.dram_tensor("xo", [D, T], F32, kind="ExternalOutput").ap()
    g = G(nc)
    c = setup_common(g, t0a["masks"], t0a["ident"])
    bo = g.sb("l1_bo", [128, CH], F32)
    g.load(bo[:], t1a["bo"])
    emit_load_stream(g, c, t0a["xin"], True)
    prep_evs = {}
    prep_es = ExitStack()
    prep_ctl = {}
    prep = PrepDriver(gen_quantize_prep(g, c, [
        ("w1", t0f["w1T"], t0f["w1q"], CH, L0H),
        ("w2", t0f["w2T"], t0f["w2q"], CH, L0H),
        ("w3", t0f["w3T"], t0f["w3q"], L0HC, D),
    ], prep_evs, prep_es, prep_ctl))
    prep.drive(1)
    emit_l0_attn(g, c, t0a, prep=prep, prep_ctl=prep_ctl)
    prep_ctl["gamma"] = True
    prep.drive(6)
    g.barrier()
    emit_outproj(g, c, t0a["wo"], None, prep=prep)
    g.barrier()
    evs = emit_l0_ffn(g, c, t0f, prep_evs, prep=prep)
    prep.finish()
    ev_out = emit_ln(g, c, evs)
    prep_es.close()
    g.barrier()
    emit_l1_attn(g, c, t1a)
    g.barrier()
    emit_outproj(g, c, t1a["wout"], bo)
    g.barrier()
    evs = emit_l1_ffn(g, c, t1f)
    ev_out = emit_ln(g, c, evs, final_out=xo)
    g.op("gpsimd", lambda e: e.engine_nop(), waits=list(ev_out), inc=True)
    sems = {}
    sem_es = ExitStack()
    for s in ALL_SEMS:
        sems[s] = sem_es.enter_context(nc.semaphore(s))
    g.build(sems)
    g.es.close()
    sem_es.close()
    return nc


def build_l0_attn_launch(first=True, last=False, debug=False):
    nc = bass.Bass("TRN2", target_bir_lowering=False, debug=False)
    t = _declare_l0a(nc, first)
    xo = nc.dram_tensor("xo", [D, T], F32 if last else BF16, kind="ExternalOutput").ap()
    if debug:
        dbg_x0 = nc.dram_tensor("dbg_x0", [D, T], BF16, kind="ExternalOutput").ap()
        dbg_ao = nc.dram_tensor("dbg_ao", [D, T], BF16, kind="ExternalOutput").ap()
        dbg_kv = nc.dram_tensor("dbg_kv", [LAT, T], BF16, kind="ExternalOutput").ap()
    g = G(nc)
    c = setup_common(g, t["masks"], t["ident"])
    emit_load_stream(g, c, t["xin"], first)
    if debug:
        for i in range(CH):
            g.store(dbg_x0[i * 128:(i + 1) * 128, :], xs(c, i, 0, T), waits=[c.ev_stream])
    emit_l0_attn(g, c, t, dbg_kv if debug else None)
    g.barrier()
    if debug:
        for i in range(CH):
            g.store(dbg_ao[i * 128:(i + 1) * 128, :], aos(c, i, 0, T))
        g.barrier()
    ev_out = emit_outproj(g, c, t["wo"], None, final_out=(xo if last else None))
    if not last:
        ev_last = emit_store_stream(g, c, xo, waits=ev_out)
    else:
        ev_last = ev_out[-1]
    g.op("gpsimd", lambda e: e.engine_nop(), waits=[ev_last] + list(ev_out), inc=True)
    sems = {}
    sem_es = ExitStack()
    for s in ALL_SEMS:
        sems[s] = sem_es.enter_context(nc.semaphore(s))
    g.build(sems)
    g.es.close()
    sem_es.close()
    return nc


def _pack_panels(w):
    K, M = w.shape
    return np.ascontiguousarray(
        w.reshape(K // 128, 128, M // 128, 128).transpose(2, 1, 0, 3).reshape(M // 128, 128, K))


def _make_masks():
    i = np.arange(128)
    NEG = -32768.0
    m_diag = np.where(i[:, None] <= i[None, :], 0.0, NEG).astype(np.float32)
    m_prev = np.where(i[:, None] > i[None, :], 0.0, NEG).astype(np.float32)
    m_prev0 = m_prev.copy()
    m_prev0[0:16, :] = 0.0
    return np.ascontiguousarray(np.stack([m_diag.T, m_prev.T, m_prev0.T]))


def _l1a_packs(inputs):
    wqkv = inputs["L1_attn_w_qkv"]
    wt = np.ascontiguousarray(wqkv.T)
    per_head = []
    for h in range(NH):
        sl = np.concatenate([wt[:, h * 128:(h + 1) * 128],
                             wt[:, 2048 + h * 128: 2048 + (h + 1) * 128],
                             wt[:, 4096 + h * 128: 4096 + (h + 1) * 128]], axis=1)
        per_head.append(sl.reshape(CH, 128, 384).transpose(1, 0, 2).reshape(128, CH * 384))
    b = inputs["L1_attn_b_qkv"]
    return {
        "wqkv": np.ascontiguousarray(np.stack(per_head)),
        "bq": np.ascontiguousarray(b[0:2048].reshape(CH, 128).T),
        "bk": np.ascontiguousarray(b[2048:4096].reshape(CH, 128).T),
        "bvr": np.ascontiguousarray(b[4096:6144].reshape(NH, 128)),
        "wout": _pack_panels(np.ascontiguousarray(inputs["L1_attn_w_out"].T)),
        "bo": np.ascontiguousarray(inputs["L1_attn_b_out"].reshape(CH, 128).T),
    }


def _l1f_packs(inputs):
    return {
        "gw": _pack_panels(np.ascontiguousarray(inputs["L1_ffn_gate_w"].T)),
        "uw": _pack_panels(np.ascontiguousarray(inputs["L1_ffn_up_w"].T)),
        "dw": _pack_panels(np.ascontiguousarray(inputs["L1_ffn_down_w"].T)),
        "gs": np.ascontiguousarray(inputs["L1_ffn_gate_scale"].reshape(L1HC, 128).T),
        "us": np.ascontiguousarray(inputs["L1_ffn_up_scale"].reshape(L1HC, 128).T),
        "ds": np.ascontiguousarray(inputs["L1_ffn_down_scale"].reshape(CH, 128).T),
    }


def _l0a_packs(inputs):
    wkv = _pack_panels(inputs["L0_attn_Wkv"])
    return {
        "wq": _pack_panels(inputs["L0_attn_Wq"]),
        "wkv": np.ascontiguousarray(wkv.transpose(1, 0, 2).reshape(128, LATC * CH * 128)),
        "wk": _pack_panels(inputs["L0_attn_Wk"]),
        "wv": _pack_panels(inputs["L0_attn_Wv"]),
        "wo": _pack_panels(inputs["L0_attn_Wo"]),
        "masks": _make_masks(),
        "ident": np.eye(128, dtype=np.float32),
    }


_FUSED = True


def _all_packs(inputs):
    p = {}
    p.update(_l0a_packs(inputs))
    p.update({
        "w1T": np.ascontiguousarray(inputs["L0_ffn_w1"].T),
        "w2T": np.ascontiguousarray(inputs["L0_ffn_w2"].T),
        "w3T": np.ascontiguousarray(inputs["L0_ffn_w3"].T),
    })
    p.update(_l1a_packs(inputs))
    p.update(_l1f_packs(inputs))
    return p


def kernel(**inputs):
    x = inputs["L0_attn_x"]
    B = x.shape[0]
    cur = [np.ascontiguousarray(x[b].T) for b in range(B)]
    if _FUSED:
        packs = _all_packs(inputs)
        nc = build_fused_launch()
        in_maps = [dict(packs, xin=cur[b]) for b in range(B)]
        res = run_bass_kernel_spmd(nc, in_maps, core_ids=list(range(B)))
        cur = [np.asarray(res.results[b]["xo"]) for b in range(B)]
    else:
        l0a = _l0a_packs(inputs)
        l0f = {
            "w1T": np.ascontiguousarray(inputs["L0_ffn_w1"].T),
            "w2T": np.ascontiguousarray(inputs["L0_ffn_w2"].T),
            "w3T": np.ascontiguousarray(inputs["L0_ffn_w3"].T),
        }
        l1a = _l1a_packs(inputs)
        l1f = _l1f_packs(inputs)
        launches = [
            ("l0a", build_l0_attn_launch, l0a),
            ("l0f", build_l0_ffn_launch, l0f),
            ("l1a", build_l1_attn_launch, l1a),
            ("l1f", build_l1_ffn_launch, l1f),
        ]
        for li, (name, builder, packs) in enumerate(launches):
            nc = builder(first=(li == 0), last=(li == len(launches) - 1))
            in_maps = [dict(packs, xin=cur[b]) for b in range(B)]
            res = run_bass_kernel_spmd(nc, in_maps, core_ids=list(range(B)))
            cur = [np.asarray(res.results[b]["xo"]) for b in range(B)]
    return np.stack([cur[b].astype(np.float32).T for b in range(B)])
